# version 05
# baseline (speedup 1.0000x reference)
import numpy as np
import concourse.bass as bass
import concourse.mybir as mybir
from concourse.bass_utils import run_bass_kernel_spmd
from contextlib import ExitStack

F32 = mybir.dt.float32
BF16 = mybir.dt.bfloat16
AF = mybir.ActivationFunctionType
ALU = mybir.AluOpType
AX = mybir.AxisListType.X

HD = 1024
NH = 32
HM = 64
CH = 128
FE = 8
BMIN, BMAX = 0.85, 0.9995
EPS = 1e-6
LN_EPS = 1e-5
LOG_ROPE = float(np.log(10000.0))
PI = float(np.pi)
TPC = 2048
_last_led = None

ENGS = ("T", "V", "S", "G", "Q")
NDMASEM = 24
FENCE_DIST = 2


class _Op:
    __slots__ = ("eng", "idx", "fn", "deps", "special", "w_inc", "w_nop",
                 "val", "fval", "dsem", "dval", "name")


class Led:
    def __init__(self):
        self.st = {e: [] for e in ENGS}
        self.dcnt = [0] * NDMASEM
        self.drr = {"Q": 0, "G": NDMASEM // 2}

    def op(self, eng, fn, deps=(), special=False, name=""):
        o = _Op()
        o.eng, o.idx, o.fn = eng, len(self.st[eng]), fn
        o.deps = [d for d in deps if d is not None]
        o.special = special
        o.w_inc = o.w_nop = False
        o.val = o.fval = o.dsem = o.dval = None
        o.name = name
        self.st[eng].append(o)
        return o

    def dma(self, eng, fn, deps=(), name=""):
        o = self.op(eng, fn, deps, name=name)
        half = NDMASEM // 2
        base = 0 if eng == "Q" else half
        o.dsem = base + (self.drr[eng] % half)
        self.drr[eng] += 1
        self.dcnt[o.dsem] += 16
        o.dval = self.dcnt[o.dsem]
        return o

    def lasts(self):
        return [self.st[e][-1] for e in ENGS if self.st[e]]

    def assign(self):
        for e in ENGS:
            for o in self.st[e]:
                for d in o.deps:
                    if d.dsem is not None:
                        continue
                    if d.eng == o.eng:
                        if d.eng == "T" and not d.special:
                            continue
                        if d.special or (o.idx - d.idx) <= FENCE_DIST:
                            d.w_inc = True
                            d.w_nop = True
                    else:
                        d.w_inc = True
                        if d.special:
                            d.w_nop = True
        for e in ENGS:
            c = 0
            for o in self.st[e]:
                if e == "S" and (o.w_inc or o.w_nop):
                    o.w_inc = False
                    o.w_nop = True
                if o.w_inc:
                    c += 1
                    o.val = c
                if o.w_nop:
                    c += 1
                    o.fval = c

    def body(self, ekey, esems, dsems):
        stream = self.st[ekey]

        def run(engine):
            wm = {}
            sem_self = esems[ekey]
            for o in stream:
                for d in o.deps:
                    if d.dsem is not None:
                        sem, tgt, key = dsems[d.dsem], d.dval, ("d", d.dsem)
                    else:
                        same = d.eng == ekey
                        if same and d.eng == "T" and not d.special:
                            continue
                        if (same and not d.special
                                and (o.idx - d.idx) > FENCE_DIST):
                            continue
                        if same or d.special or d.eng == "S":
                            tgt = d.fval if d.fval is not None else d.val
                        else:
                            tgt = d.val
                        sem, key = esems[d.eng], ("e", d.eng)
                        assert tgt is not None, (ekey, o.name, d.eng, d.name)
                    if wm.get(key, -1) < tgt:
                        engine.wait_ge(sem, tgt)
                        wm[key] = tgt
                inst = o.fn(engine)
                if o.dsem is not None:
                    inst.then_inc(dsems[o.dsem], 16)
                else:
                    if o.w_inc:
                        inst.then_inc(sem_self, 1)
                    if o.w_nop:
                        if ekey == "S":
                            nop = engine.drain(fusable=False)
                        elif ekey in ("V", "G"):
                            nop = engine.engine_nop()
                        else:
                            nop = engine.nop(nofuse=True)
                        nop.then_inc(sem_self, 1)
        return run


class Buf:
    def __init__(self):
        self.R = {}
        self.W = {}

    def r(self, op, *names):
        for n in names:
            self.R.setdefault(n, []).append(op)
        return op

    def wdeps(self, *names):
        deps = []
        for n in names:
            deps.extend(self.R.get(n, []))
            if self.W.get(n) is not None:
                deps.append(self.W[n])
        return deps

    def w(self, op, *names):
        for n in names:
            self.R[n] = []
            self.W[n] = op
        return op


class Alloc:
    def __init__(self, U, limit_words):
        self.U, self.limit = U, limit_words
        self.cur = 0
        self.peak = 0

    def take(self, parts, elems, dt=F32, name=""):
        words = elems if dt == F32 else (elems + 1) // 2
        words = (words + 15) & ~15
        off = self.cur
        self.cur += words
        self.peak = max(self.peak, self.cur)
        assert self.cur <= self.limit, \
            f"SBUF union overflow at {name}: {self.cur} > {self.limit}"
        ap = self.U[0:parts, off:off + words]
        if dt == BF16:
            ap = ap.bitcast(BF16)[:, 0:elems]
        else:
            ap = ap[:, 0:elems]
        return ap


def segs(total, step=512):
    out = []
    p = 0
    while p < total:
        w = min(step, total - p)
        out.append((p, w))
        p += w
    return out


def build(tpc=TPC, dbg=(), stop_after=None, skips=()):
    NCH = tpc // CH
    THALF = tpc // 2
    NTH = THALF // CH
    nc = bass.Bass("TRN2", target_bir_lowering=False,
                   detect_race_conditions=False)
    L = Led()
    B = Buf()
    global _last_led
    _last_led = L
    dbg = set(dbg)

    def din(name, shape):
        return nc.dram_tensor(name, list(shape), F32, kind="ExternalInput").ap()

    x_d = din("x", (tpc, HD))
    xp_d = din("xp", (tpc, HD))
    crowT_d = din("crowT_h", (128, 8))
    flag_d = din("flagc_h", (64, 1))
    poscol_d = din("poscol_h", (128, 2 * NCH))
    diota_d = din("diota", (1, 16))
    wqT_d = din("wqT", (128, 8 * HD))
    wkT_d = din("wkT", (128, 8 * HD))
    wvT_d = din("wvT", (128, 8 * HD))
    woutT_d = din("woutT", (128, 8 * HD))
    wbmT_d = din("wbmT", (128, 8 * 128))
    bbm_d = din("bbm", (1, 128))
    bout_d = din("bout_h", (1, HD))
    lng_d = din("lng", (1, HD))
    lnb_d = din("lnb", (1, HD))
    gateT_d = din("gateT", (128, 64))
    gatebc_d = din("gatebc_h", (8, 1))
    wmod_d = din("wmod", (HD, 6144))
    w0_d = din("w0r", (FE, 128, 8 * 2048))
    w1_d = din("w1r", (FE, 128, 8 * 2048))
    w2_d = din("w2r", (FE, 128, 16 * HD))
    b0t_d = din("b0t_h", (128, 128))
    b1t_d = din("b1t_h", (128, 128))
    b2_d = din("b2m", (8, HD))
    identc_d = din("identc_h", (128, 128))
    masku_d = din("masku_h", (128, 128))
    xmid_d = nc.dram_tensor("xmid", [tpc, HD], F32, kind="Internal").ap()
    out_d = nc.dram_tensor("out", [tpc, HD], F32, kind="ExternalOutput").ap()

    with ExitStack() as stack:
        ent = stack.enter_context

        h2T = ent(nc.sbuf_tensor("h2T", [128, 8 * tpc], BF16))
        costab = ent(nc.sbuf_tensor("costab", [128, 2 * NCH * 16], F32))
        sintab = ent(nc.sbuf_tensor("sintab", [128, 2 * NCH * 16], F32))
        wgate = ent(nc.sbuf_tensor("wgate", [128, NCH * 8], F32))
        wTg = ent(nc.sbuf_tensor("wTg", [8, tpc], BF16))
        identc = ent(nc.sbuf_tensor("identc", [128, 128], F32))
        masku = ent(nc.sbuf_tensor("masku", [128, 128], F32))
        onesr = ent(nc.sbuf_tensor("onesr", [1, 128], F32))
        onesrb = ent(nc.sbuf_tensor("onesrb", [1, 128], BF16))
        onescb = ent(nc.sbuf_tensor("onescb", [128, 1], BF16))
        gatebc = ent(nc.sbuf_tensor("gatebc", [8, 1], F32))
        gateTb = ent(nc.sbuf_tensor("gateTb", [128, 64], BF16))
        gbb = ent(nc.sbuf_tensor("gbb", [128, HD], BF16))
        stat = ent(nc.sbuf_tensor("stat", [128, 64], F32))
        flagc = ent(nc.sbuf_tensor("flagc", [64, 1], F32))
        crowT = ent(nc.sbuf_tensor("crowT", [128, 8], F32))
        csil = ent(nc.sbuf_tensor("csil", [128, 8], BF16))
        ftile = ent(nc.sbuf_tensor("ftile", [128, 16], F32))
        poscol = ent(nc.sbuf_tensor("poscol", [128, 2 * NCH], F32))
        zer64 = ent(nc.sbuf_tensor("zer64", [64, CH], F32))
        b2b = ent(nc.sbuf_tensor("b2b", [8, HD], BF16))
        b0t = ent(nc.sbuf_tensor("b0t", [128, 128], F32))
        b1t = ent(nc.sbuf_tensor("b1t", [128, 128], F32))
        bbmb = ent(nc.sbuf_tensor("bbmb", [128, 128], F32))
        betaTs = ent(nc.sbuf_tensor("betaTs", [64, CH], F32))
        cperow = ent(nc.sbuf_tensor("cperow", [1, HM], F32))
        kupd = ent(nc.sbuf_tensor("kupd", [64, 32], F32))
        lgT = ent(nc.sbuf_tensor("lgT", [8, CH], F32))
        lgtm = ent(nc.sbuf_tensor("lgtm", [128, 8], F32))
        g8a = ent(nc.sbuf_tensor("g8a", [128, 8], F32))
        g8b = ent(nc.sbuf_tensor("g8b", [128, 8], F32))

        pp0 = ent(nc.psum_tensor("pp0", [128, 1024], F32))
        pp1 = ent(nc.psum_tensor("pp1", [128, 1024], F32))
        pp2 = ent(nc.psum_tensor("pp2", [128, 1024], F32))
        ppd = ent(nc.psum_tensor("ppd", [128, 512], F32))
        ppt = ent(nc.psum_tensor("ppt", [128, 512], F32))

        rem_words = nc.sbuf_bytes_remaining // 4 - 8
        UN = ent(nc.sbuf_tensor("UN", [128, rem_words], F32))
        al = Alloc(UN, rem_words)

        esems = {e: ent(nc.semaphore("es_" + e)) for e in ENGS}
        dsems = [ent(nc.semaphore("ds%d" % i)) for i in range(NDMASEM)]

        mB = al.cur
        wqTb = al.take(128, 8 * HD, BF16, "wqTb")
        wkTb = al.take(128, 8 * HD, BF16, "wkTb")
        wvTb = al.take(128, 8 * HD, BF16, "wvTb")
        woutTb = al.take(128, 8 * HD, BF16, "woutTb")
        wbmb = al.take(128, 8 * 128, BF16, "wbmb")
        ca1b = al.take(128, HD, BF16, "ca1b")
        sab = al.take(128, HD, BF16, "sab")
        cb1b = al.take(128, HD, BF16, "cb1b")
        sbb = al.take(128, HD, BF16, "sbb")
        gab = al.take(128, HD, BF16, "gab")
        lngb = al.take(128, HD, BF16, "lngb")
        lnbb = al.take(128, HD, BF16, "lnbb")
        boutb = al.take(128, HD, BF16, "boutb")
        Sf = al.take(64, 32 * 33, F32, "Sf")
        Sb = al.take(64, 32 * 33, BF16, "Sb")
        xr = [al.take(128, HD, F32, "xr%d" % i) for i in range(2)]
        scr_off = al.cur
        scr = [al.take(128, HD, F32, "scr%d" % i) for i in range(5)]
        qtm = al.take(128, HD, F32, "qtm")
        ytm = qtm
        h1tm = al.take(128, HD, F32, "h1tm")
        ktm = h1tm
        _sv = al.cur
        al.cur = scr_off
        wmseg = [al.take(128, 8 * 512, BF16, "wmseg%d" % i) for i in range(2)]
        rowst = al.take(1, 512, F32, "rowst")
        rowst2 = al.take(1, 512, F32, "rowst2")
        al.cur = _sv
        h1T = al.take(128, HD, BF16, "h1T")
        fqfm = al.take(64, 16 * CH, BF16, "fqfm")
        fkfm = al.take(64, 16 * CH, BF16, "fkfm")
        vtm = al.take(128, HD, BF16, "vtm")
        vt = [al.take(128, HD, BF16, "vt%d" % i) for i in range(2)]
        kt = [al.take(128, HD, BF16, "kt%d" % i) for i in range(2)]
        Ab = [al.take(128, 4 * CH, BF16, "Ab%d" % i) for i in range(2)]
        yfm = al.take(128, 8 * CH, BF16, "yfm")
        betat = al.take(128, HM, F32, "betat")
        mixt = al.take(128, HM, F32, "mixt")
        cphm = al.take(64, CH, F32, "cphm")
        icphm = al.take(64, CH, F32, "icphm")
        icpT = al.take(128, HM, F32, "icpT")
        icpTb = al.take(128, HM, BF16, "icpTb")
        cpendb = al.take(64, 32, F32, "cpendb")
        denf = al.take(128, HM, F32, "denf")
        mden = al.take(128, HM, F32, "mden")
        szB = al.cur

        al.cur = mB
        w0r = [al.take(128, 8 * 256, BF16, "w0r%d" % i) for i in range(3)]
        w1r = [al.take(128, 8 * 256, BF16, "w1r%d" % i) for i in range(3)]
        w2b = al.take(128, 16 * HD, BF16, "w2b")
        actb = al.take(128, 16 * THALF, BF16, "actb")
        outacc = al.take(128, 8 * THALF, F32, "outacc")
        wbc = [al.take(128, THALF, F32, "wbc%d" % i) for i in range(2)]
        silscr = [al.take(128, THALF, F32, "silscr%d" % i) for i in range(2)]
        xmr = [al.take(128, HD, F32, "xmr%d" % i) for i in range(2)]
        otile = [al.take(128, HD, F32, "otile%d" % i) for i in range(2)]
        wrow = [al.take(1, THALF, BF16, "wrow%d" % i) for i in range(2)]
        szC = al.cur
        al.cur = max(szB, szC)

        def dmaQ(out, in_, deps=(), name="", slow=False):
            return L.dma("Q", lambda e, o=out, i=in_, s=slow:
                         e.dma_start(out=o, in_=i, allow_slow_non_contiguous=s),
                         deps, name=name)

        def dmaG(out, in_, deps=(), name=""):
            return L.dma("G", lambda e, o=out, i=in_: e.dma_start(out=o, in_=i),
                         deps, name=name)

        def T(fn, deps=(), name=""):
            return L.op("T", fn, deps, name=name)

        def V(fn, deps=(), special=False, name=""):
            return L.op("V", fn, deps, special=special, name=name)

        def S(fn, deps=(), name=""):
            return L.op("S", fn, deps, name=name)

        def mm(out, lhsT, rhs, start, stop, deps=(), tp=None, name="mm"):
            return T(lambda e, o=out, l=lhsT, r=rhs, a=start, b=stop, t=tp:
                     e.matmul(o, l, r, start=a, stop=b, tile_position=t),
                     deps, name=name)

        def cp_(out, in_, deps=(), name="cp"):
            return V(lambda e, o=out, i=in_: e.tensor_copy(out=o, in_=i),
                     deps, name=name)

        def tt(out, a, b, op, deps=(), name="tt"):
            return V(lambda e, o=out, x=a, y=b, p=op:
                     e.tensor_tensor(out=o, in0=x, in1=y, op=p), deps, name=name)

        def ts(out, a, s1, op0, s2=None, op1=None, deps=(), name="ts"):
            if op1 is None:
                return V(lambda e, o=out, x=a, u=s1, p=op0:
                         e.tensor_scalar(out=o, in0=x, scalar1=u, scalar2=None,
                                         op0=p), deps, name=name)
            return V(lambda e, o=out, x=a, u=s1, v=s2, p=op0, q=op1:
                     e.tensor_scalar(out=o, in0=x, scalar1=u, scalar2=v,
                                     op0=p, op1=q), deps, name=name)

        def stt(out, a, s, b, op0, op1, deps=(), name="stt"):
            return V(lambda e, o=out, x=a, u=s, y=b, p=op0, q=op1:
                     e.scalar_tensor_tensor(out=o, in0=x, scalar=u, in1=y,
                                            op0=p, op1=q), deps, name=name)

        def act(out, in_, func, bias=0.0, scale=1.0, accum=None, deps=(), name="act"):
            return S(lambda e, o=out, i=in_, f=func, b=bias, s=scale, a=accum:
                     e.activation(out=o, in_=i, func=f, bias=b, scale=s,
                                  accum_out=a), deps, name=name)

        def recip(out, in_, deps=(), name="recip"):
            return V(lambda e, o=out, i=in_: e.reciprocal(out=o, in_=i),
                     deps, special=True, name=name)

        def vmax(out, in_, deps=(), name="vmax"):
            return V(lambda e, o=out, i=in_: e.reduce_max(out=o, in_=i, axis=AX),
                     deps, special=True, name=name)

        def vsum(out, in_, deps=(), name="vsum"):
            return V(lambda e, o=out, i=in_: e.reduce_sum(out=o, in_=i, axis=AX),
                     deps, special=True, name=name)

        def scan_mul(out, d0, d1, deps=(), name="scan"):
            return V(lambda e, o=out, a=d0, b=d1:
                     e.tensor_tensor_scan(out=o, data0=a, data1=b, initial=1.0,
                                          op0=ALU.mult, op1=ALU.add),
                     deps, special=True, name=name)

        def memset(buf, val, deps=(), name="memset"):
            return V(lambda e, b=buf, v=val: e.memset(b, v), deps, name=name)

        d_identc = dmaQ(identc[:], identc_d, name="identc")

        def tr(out, in_, deps=(), name="tr"):
            p = in_.shape[0]
            return T(lambda e, o=out, i=in_, q=p:
                     e.transpose(o, i, identc[0:q, 0:q]),
                     list(deps) + [d_identc], name=name)

        dbg_final = []
        dbg_stage = [None]

        def dbg_dump(name, ap, deps, src=()):
            if name not in dbg:
                return
            if not isinstance(ap, bass.AP):
                ap = ap[:]
            if dbg_stage[0] is None:
                dbg_stage[0] = al.take(128, 2176, F32, "dbgstage")
            p, wdt = ap.shape[0], int(np.prod(ap.shape[1:]))
            dd = nc.dram_tensor("dbg_" + name, [p, wdt], F32,
                                kind="ExternalOutput").ap()
            st_ = dbg_stage[0][0:p, 0:wdt]
            cv = cp_(st_, ap, deps=list(deps) + B.wdeps("dbgstage"),
                     name="dbgc_" + name)
            for n_ in src:
                B.r(cv, n_)
            B.w(cv, "dbgstage")
            dm = dmaQ(dd, st_, deps=[cv], name="dbgd_" + name)
            B.r(dm, "dbgstage")
            dbg_final.append(dm)

        d_masku = dmaQ(masku[:], masku_d, name="masku")
        d_gatebc = dmaQ(gatebc[:], gatebc_d, name="gatebc")
        d_gateTb = dmaG(gateTb[:], gateT_d, name="gateTb")
        d_crowT = dmaQ(crowT[:], crowT_d, name="crowT")
        d_flag = dmaQ(flagc[:], flag_d, name="flag")
        d_poscol = dmaQ(poscol[:], poscol_d, name="poscol")
        d_diota = dmaQ(stat[0:1, 16:32], diota_d, name="diota")
        d_wq = dmaG(wqTb, wqT_d, name="wq")
        d_wk = dmaG(wkTb, wkT_d, name="wk")
        d_wv = dmaG(wvTb, wvT_d, name="wv")
        d_wout = dmaG(woutTb, woutT_d, name="wout")
        d_wbm = dmaG(wbmb, wbmT_d, name="wbm")
        d_b2b = dmaG(b2b[:], b2_d, name="b2b")
        d_b0t = dmaQ(b0t[:], b0t_d, name="b0t")
        d_b1t = dmaQ(b1t[:], b1t_d, name="b1t")
        m_on = memset(onesr[:], 1.0, name="onesr")
        m_onb = memset(onesrb[:], 1.0, name="onesrb")
        m_onc = memset(onescb[:], 1.0, name="onescb")
        m_z64 = memset(zer64[:], 0.0, name="z64")
        m_Sf = B.w(memset(Sf, 0.0, name="Sf0"), "Sf")

        a_f = act(stat[0:1, 0:16], stat[0:1, 16:32], AF.Exp,
                  scale=-LOG_ROPE / 16.0, deps=[d_diota], name="frow")
        m_f = mm(ppd[0:128, 0:16], onesr[0:1, 0:128], stat[0:1, 0:16],
                 True, True, deps=[a_f, m_on], name="fbcast")
        B.w(m_f, "ppd")
        e_f = cp_(ftile[:], ppd[0:128, 0:16], deps=[m_f], name="ftile")
        B.r(e_f, "ppd")
        NS = 2 * NCH
        t_ang = None
        for s_ in range(NS):
            t_ang = ts(costab[:, s_ * 16:(s_ + 1) * 16], ftile[:],
                       poscol[:, s_:s_ + 1], ALU.mult,
                       deps=[e_f, d_poscol], name="ang")
        HW = NS * 16 // 2
        halves = [costab[:, 0:HW], costab[:, HW:2 * HW]]
        gsc = [sintab[:, 0:HW], sintab[:, HW:2 * HW]]
        prev = [t_ang, t_ang]
        for k in range(10, -1, -1):
            K = float(2.0 * PI * (2 ** k))
            g0 = ts(gsc[0], halves[0], K, ALU.is_ge, deps=[prev[0]], name="lg0")
            g1 = ts(gsc[1], halves[1], K, ALU.is_ge, deps=[prev[1]], name="lg1")
            s0 = stt(halves[0], gsc[0], -K, halves[0], ALU.mult, ALU.add,
                     deps=[g0], name="ls0")
            s1 = stt(halves[1], gsc[1], -K, halves[1], ALU.mult, ALU.add,
                     deps=[g1], name="ls1")
            prev = [s0, s1]
        g0 = ts(gsc[0], halves[0], PI, ALU.is_ge, deps=[prev[0]], name="sg0")
        g1 = ts(gsc[1], halves[1], PI, ALU.is_ge, deps=[prev[1]], name="sg1")
        s0 = stt(halves[0], gsc[0], -2.0 * PI, halves[0], ALU.mult, ALU.add,
                 deps=[g0], name="ss0")
        s1 = stt(halves[1], gsc[1], -2.0 * PI, halves[1], ALU.mult, ALU.add,
                 deps=[g1], name="ss1")
        ab0 = act(gsc[0], halves[0], AF.Abs, deps=[s0], name="ab0")
        ab1 = act(gsc[1], halves[1], AF.Abs, deps=[s1], name="ab1")
        cg0 = ts(gsc[0], gsc[0], -1.0, ALU.mult, PI / 2.0, ALU.add, deps=[ab0],
                 name="cg0")
        cg1 = ts(gsc[1], gsc[1], -1.0, ALU.mult, PI / 2.0, ALU.add, deps=[ab1],
                 name="cg1")
        o_cos = act(sintab[:], sintab[:], AF.Sin, deps=[cg0, cg1], name="cosv")
        o_sin = act(costab[:], costab[:], AF.Sin, deps=[o_cos, s0, s1], name="sinv")
        COS, SIN = sintab, costab
        tab_deps = [o_cos, o_sin]
        dbg_dump("COSTAB", COS[:], tab_deps)
        dbg_dump("SINTAB", SIN[:], tab_deps)

        a_cs = act(csil[:], crowT[:], AF.Silu, deps=[d_crowT], name="csil")
        wmv = wmod_d.rearrange("(a p) n -> p a n", p=128)
        targets = [sab, ca1b, gab, sbb, cb1b, gbb]
        tnames = ["sab", "ca1b", "gab", "sbb", "cb1b", "gbb"]
        for s_ in range(12):
            vec, sh = s_ // 2, s_ % 2
            wmsl = "wmseg%d" % (s_ % 2)
            dseg = dmaG(wmseg[s_ % 2][:].rearrange("p (a n) -> p a n", a=8),
                        wmv[:, :, s_ * 512:(s_ + 1) * 512],
                        deps=B.wdeps(wmsl), name="wmseg")
            B.w(dseg, wmsl)
            last = None
            for kk in range(8):
                last = mm(ppd[0:1, 0:512], csil[:, kk:kk + 1],
                          wmseg[s_ % 2][:, kk * 512:(kk + 1) * 512],
                          kk == 0, kk == 7,
                          deps=[dseg, a_cs] + (B.wdeps("ppd") if kk == 0 else []),
                          name="modmm")
                B.r(last, wmsl)
            B.w(last, "ppd")
            ev = cp_(rowst[0:1, 0:512], ppd[0:1, 0:512],
                     deps=[last] + B.wdeps("rowst"), name="modev")
            B.r(ev, "ppd")
            B.w(ev, "rowst")
            post = ev
            if vec in (1, 4):
                post = ts(rowst[0:1, 0:512], rowst[0:1, 0:512], 1.0, ALU.add,
                          deps=[ev], name="plus1")
                B.w(post, "rowst")
            elif vec in (2, 5):
                if vec == 2:
                    dbg_dump("ga_raw%d" % sh, rowst[0:1, 0:512], [ev],
                             src=("rowst",))
                t1 = ts(rowst2[0:1, 0:512], rowst[0:1, 0:512], 0.0, ALU.is_ge,
                        deps=[ev] + B.wdeps("rowst2"), name="gsgn")
                B.r(t1, "rowst")
                B.w(t1, "rowst2")
                t2 = ts(rowst2[0:1, 0:512], rowst2[0:1, 0:512], 2.0, ALU.mult,
                        -1.0, ALU.add, deps=[t1], name="gsgn2")
                B.w(t2, "rowst2")
                if vec == 2:
                    dbg_dump("ga_sgn%d" % sh, rowst2[0:1, 0:512], [t2],
                             src=("rowst2",))
                t3a = act(rowst[0:1, 0:512], rowst[0:1, 0:512], AF.Abs,
                          deps=[ev] + B.wdeps("rowst"), name="gabs")
                B.w(t3a, "rowst")
                t3 = ts(rowst[0:1, 0:512], rowst[0:1, 0:512], 0.25, ALU.max,
                        deps=[t3a], name="gclamp")
                B.w(t3, "rowst")
                if vec == 2:
                    dbg_dump("ga_clp%d" % sh, rowst[0:1, 0:512], [t3],
                             src=("rowst",))
                t4 = tt(rowst[0:1, 0:512], rowst[0:1, 0:512], rowst2[0:1, 0:512],
                        ALU.mult, deps=[t3, t2] + B.wdeps("rowst"), name="gmul")
                B.r(t4, "rowst2")
                B.w(t4, "rowst")
                post = t4
            bcm = mm(pp0[:, 0:512], onesr[0:1, 0:128], rowst[0:1, 0:512],
                     True, True, deps=[post, m_on] + B.wdeps("pp0"), name="bcmm")
            B.r(bcm, "rowst")
            B.w(bcm, "pp0")
            bce = cp_(targets[vec][:, sh * 512:(sh + 1) * 512], pp0[:, 0:512],
                      deps=[bcm], name="bc_" + tnames[vec])
            B.r(bce, "pp0")
            B.w(bce, tnames[vec] + str(sh))

        for tgt, tn, src, wdt in ((lngb, "lngb", lng_d, HD),
                                  (lnbb, "lnbb", lnb_d, HD),
                                  (boutb, "boutb", bout_d, HD),
                                  (bbmb, "bbmb", bbm_d, 128)):
            for sh, (p0, wn) in enumerate(segs(wdt)):
                dr = dmaQ(rowst[0:1, 0:wn], src[0:1, p0:p0 + wn],
                          deps=B.wdeps("rowst"), name="row_" + tn)
                B.w(dr, "rowst")
                bcm = mm(pp0[:, 0:wn], onesr[0:1, 0:128], rowst[0:1, 0:wn],
                         True, True, deps=[dr, m_on] + B.wdeps("pp0"),
                         name="bcmm2")
                B.r(bcm, "rowst")
                B.w(bcm, "pp0")
                bce = cp_(tgt[:, p0:p0 + wn], pp0[:, 0:wn], deps=[bcm],
                          name="bce_" + tn)
                B.r(bce, "pp0")
                B.w(bce, tn + str(sh))

        ph0_last = L.lasts()
        for nm in ("scr0", "scr1", "scr2", "scr3", "scr4"):
            for o_ in ph0_last:
                B.r(o_, nm)
        bct_deps = [B.W["sab0"], B.W["sab1"], B.W["ca1b0"], B.W["ca1b1"]]
        bct2_deps = [B.W["sbb0"], B.W["sbb1"], B.W["cb1b0"], B.W["cb1b1"],
                     B.W["gab0"], B.W["gab1"], B.W["lngb0"], B.W["lngb1"],
                     B.W["lnbb0"], B.W["lnbb1"],
                     B.W["boutb0"], B.W["boutb1"]]
        bbm_deps = [B.W["bbmb0"]]
        gbb_deps = [B.W["gbb0"], B.W["gbb1"]]

        def ln_apply(src_ap, src_name, out_ap, out_name, c1_tile, s_tile,
                     cols, xdeps, extra_deps=()):
            sname = "stat%d" % cols
            a1 = act(out_ap, src_ap, AF.Copy, accum=stat[:, cols:cols + 1],
                     deps=list(xdeps) + B.wdeps(sname) + B.wdeps(out_name) +
                     list(extra_deps), name="ln_c")
            a2 = act(scr[4], src_ap, AF.Square, accum=stat[:, cols + 1:cols + 2],
                     deps=list(xdeps) + B.wdeps("scr4"), name="ln_sq")
            B.w(a2, "scr4")
            v1 = ts(stat[:, cols + 2:cols + 3], stat[:, cols:cols + 1],
                    1.0 / HD, ALU.mult, deps=[a1], name="ln_mu")
            v2 = ts(stat[:, cols + 3:cols + 4], stat[:, cols + 2:cols + 3],
                    stat[:, cols + 2:cols + 3], ALU.mult, deps=[v1], name="ln_mu2")
            v3 = ts(stat[:, cols + 4:cols + 5], stat[:, cols + 1:cols + 2],
                    1.0 / HD, ALU.mult, deps=[a2], name="ln_ex2")
            v4 = ts(stat[:, cols + 5:cols + 6], stat[:, cols + 4:cols + 5],
                    stat[:, cols + 3:cols + 4], ALU.subtract, LN_EPS, ALU.add,
                    deps=[v2, v3], name="ln_var")
            s3 = act(stat[:, cols + 6:cols + 7], stat[:, cols + 5:cols + 6],
                     AF.Ln, deps=[v4], name="ln_lv")
            s4 = act(stat[:, cols + 6:cols + 7], stat[:, cols + 6:cols + 7],
                     AF.Exp, scale=-0.5, deps=[s3], name="ln_rs")
            v5 = ts(stat[:, cols + 7:cols + 8], stat[:, cols + 2:cols + 3],
                    -1.0, ALU.mult, deps=[v1], name="ln_nm")
            h1a = ts(out_ap, src_ap, stat[:, cols + 7:cols + 8], ALU.add,
                     stat[:, cols + 6:cols + 7], ALU.mult,
                     deps=[v5, s4, a1] + list(xdeps), name="ln_n")
            h1b = tt(out_ap, out_ap, c1_tile, ALU.mult, deps=[h1a], name="ln_m")
            h1c = tt(out_ap, out_ap, s_tile, ALU.add, deps=[h1b], name="ln_a")
            B.w(h1a, sname)
            B.r(h1a, src_name)
            B.r(h1c, src_name)
            B.w(h1c, out_name)
            return h1c

        dbg_pre = [None]

        def rope_elu(psum_src, psum_name, out_f32, out_name, slot, dep_mm):
            TR = lambda ap: ap.transpose([0, 2, 1])
            cosr = COS[:, slot * 16:(slot + 1) * 16].unsqueeze(2) \
                .broadcast_to((128, 16, 32))
            sinr = SIN[:, slot * 16:(slot + 1) * 16].unsqueeze(2) \
                .broadcast_to((128, 16, 32))
            pv = psum_src.rearrange("p (h d) -> p h d", h=32)
            q1, q2 = TR(pv[:, :, 0:16]), TR(pv[:, :, 16:32])
            s0a = TR(scr[0][:, 0:512].rearrange("p (h d) -> p h d", h=32))
            s0b = TR(scr[0][:, 512:1024].rearrange("p (h d) -> p h d", h=32))
            s1a = TR(scr[1][:, 0:512].rearrange("p (h d) -> p h d", h=32))
            s1b = TR(scr[1][:, 512:1024].rearrange("p (h d) -> p h d", h=32))
            ov0 = out_f32.rearrange("p (h d) -> p h d", h=32)
            ov = None
            d0 = B.wdeps("scr0") + B.wdeps("scr1") + tab_deps
            ta = tt(s0a, q1, cosr, ALU.mult, deps=[dep_mm] + d0, name="rp_a")
            tb = tt(s1a, q2, sinr, ALU.mult, deps=[dep_mm], name="rp_b")
            tc = tt(s0b, q1, sinr, ALU.mult, deps=[dep_mm], name="rp_c")
            td = tt(s1b, q2, cosr, ALU.mult, deps=[dep_mm], name="rp_d")
            B.w(td, "scr0")
            B.w(td, "scr1")
            qa = tt(TR(ov0[:, :, 0:16]), s0a, s1a, ALU.subtract,
                    deps=[ta, tb] + B.wdeps(out_name), name="rp_e")
            qb = tt(TR(ov0[:, :, 16:32]), s0b, s1b, ALU.add, deps=[tc, td],
                    name="rp_f")
            B.r(qb, "scr0")
            B.r(qb, "scr1")
            if dbg_pre[0] is not None:
                dbg_dump(dbg_pre[0], out_f32, [qa, qb])
                dbg_pre[0] = None
            mn = ts(scr[4], out_f32, 0.0, ALU.min,
                    deps=[qa, qb] + B.wdeps("scr4"), name="el_m")
            B.w(mn, "scr4")
            ex = act(scr[4], scr[4], AF.Exp, deps=[mn], name="el_e")
            B.w(ex, "scr4")
            ph = stt(out_f32, out_f32, 0.0, scr[4], ALU.max, ALU.add,
                     deps=[ex, qb], name="el_p")
            B.r(ph, "scr4")
            B.w(ph, out_name)
            return ph

        def transp_batch(src_f32, src_name, dst, dst_name, nblk, bdeps, tag,
                         parts=128, src_w=128, dst_targets=None):
            per = 512 // CH
            evs = []
            bb = 0
            while bb * per < nblk:
                trs = []
                for j in range(per):
                    jj = bb * per + j
                    if jj >= nblk:
                        break
                    trs.append(tr(ppt[0:src_w, j * CH:(j + 1) * CH],
                                  src_f32[:, jj * src_w:(jj + 1) * src_w],
                                  deps=list(bdeps) + B.wdeps("ppt"),
                                  name="tr_" + tag))
                    B.r(trs[-1], src_name)
                if dst_targets is None:
                    ev = cp_(dst[0:src_w, bb * per * CH:
                                 (bb * per + len(trs)) * CH],
                             ppt[0:src_w, 0:len(trs) * CH],
                             deps=[trs[-1]], name="trev_" + tag)
                    evl = [ev]
                else:
                    evl = []
                    for j in range(len(trs)):
                        jj = bb * per + j
                        evl.append(cp_(dst_targets[jj],
                                       ppt[0:src_w, j * CH:(j + 1) * CH],
                                       deps=[trs[-1]], name="trev_" + tag))
                B.w(trs[0], "ppt")
                B.R["ppt"] = list(evl)
                evs.extend(evl)
                bb += 1
            B.w(evs[-1], dst_name)
            for e_ in evs:
                pass
            return evs

        def proj(psum_ap, psum_name, wT, wdep, nseg, h1dep, tag, war=()):
            lastseg = None
            for n in range(nseg):
                last = None
                for k in range(8):
                    last = mm(psum_ap[:, n * 512:(n + 1) * 512],
                              h1T[:, k * 128:(k + 1) * 128],
                              wT[:, k * (nseg * 512) + n * 512:
                                 k * (nseg * 512) + (n + 1) * 512],
                              k == 0, k == 7,
                              deps=([h1dep, wdep] +
                                    (B.wdeps(psum_name, *war)
                                     if n == 0 and k == 0 else [])),
                              name="pj_" + tag)
                    B.r(last, "h1T")
                if n == 0:
                    B.w(last, psum_name)
                lastseg = last
            B.W[psum_name] = lastseg
            return lastseg

        CPE_VIEW = cpendb[:].rearrange("p (a m) -> p a m", a=16)

        def emit_chunk(c, stage):
            par = c % 2
            xrb = xr[par]
            xname = "xr%d" % par
            src = xp_d if stage == "A" else x_d
            slot = (NCH + c) if stage == "A" else c
            d_x = dmaQ(xrb, src[c * CH:(c + 1) * CH, :],
                       deps=B.wdeps(xname), name="xin")
            B.w(d_x, xname)

            h1c = ln_apply(xrb, xname, h1tm, "h1tm", ca1b, sab, 32, [d_x],
                           extra_deps=bct_deps)
            if stage == "B" and c == 0:
                dbg_dump("h1_c0", h1tm, [h1c], src=("h1tm",))
            h1evs = transp_batch(h1tm, "h1tm", h1T, "h1T", 8,
                                 [h1c] + B.wdeps("h1T"), "h1")
            h1dep = h1evs[-1]

            lastbm = None
            for k in range(8):
                lastbm = mm(ppd[:, 0:128], h1T[:, k * 128:(k + 1) * 128],
                            wbmb[:, k * 128:(k + 1) * 128], k == 0, k == 7,
                            deps=[h1dep, d_wbm] +
                                 (B.wdeps("ppd") if k == 0 else []),
                            name="pj_bm")
                B.r(lastbm, "h1T")
            B.w(lastbm, "ppd")
            zb = tt(scr[2][:, 0:128], ppd[:, 0:128], bbmb[:, 0:128], ALU.add,
                    deps=[lastbm] + bbm_deps + B.wdeps("scr2"), name="bm_z")
            B.r(zb, "ppd")
            B.w(zb, "scr2")
            en = act(scr[3][:, 0:64], scr[2][:, 0:64], AF.Exp, scale=-1.0,
                     deps=[zb] + B.wdeps("scr3"), name="b_e")
            B.w(en, "scr3")
            d1 = ts(scr[3][:, 0:64], scr[3][:, 0:64], 1.0, ALU.add,
                    deps=[en], name="b_d")
            bv = recip(scr[3][:, 64:128], scr[3][:, 0:64], deps=[d1], name="b_r")
            bc_ = ts(betat[:], scr[3][:, 64:128], BMIN, ALU.max, BMAX, ALU.min,
                     deps=[bv] + B.wdeps("betat"), name="b_c")
            B.w(bc_, "betat")
            if stage == "B" and c == 0:
                dbg_dump("beta_c0", betat, [bc_], src=("betat",))
            mxv = None
            if stage == "B" and "mix" not in skips:
                m0v = scr[2][:, 64:128][:, 0::2]
                m1v = scr[2][:, 64:128][:, 1::2]
                mx = tt(scr[3][:, 0:32], m0v, m1v, ALU.max, deps=[zb, bc_],
                        name="mx_m")
                B.r(mx, "scr2")
                mzs = tt(scr[3][:, 64:128].rearrange("p (h m) -> p h m", h=32),
                         scr[2][:, 64:128].rearrange("p (h m) -> p h m", h=32),
                         scr[3][:, 0:32].unsqueeze(2).broadcast_to((128, 32, 2)),
                         ALU.subtract, deps=[mx], name="mx_s")
                B.r(mzs, "scr2")
                mex = act(scr[3][:, 64:128], scr[3][:, 64:128], AF.Exp,
                          deps=[mzs], name="mx_e")
                msm = tt(scr[3][:, 32:64], scr[3][:, 64:128][:, 0::2],
                         scr[3][:, 64:128][:, 1::2], ALU.add, deps=[mex],
                         name="mx_su")
                mrs = recip(scr[3][:, 0:32], scr[3][:, 32:64], deps=[msm],
                            name="mx_r")
                mxv = tt(mixt[:].rearrange("p (h m) -> p h m", h=32),
                         scr[3][:, 64:128].rearrange("p (h m) -> p h m", h=32),
                         scr[3][:, 0:32].unsqueeze(2).broadcast_to((128, 32, 2)),
                         ALU.mult, deps=[mrs, mex] + B.wdeps("mixt"), name="mx_v")
                B.w(mxv, "mixt")
                B.w(mxv, "scr3")
                if c == 0:
                    dbg_dump("mix_c0", mixt, [mxv], src=("mixt",))

            bT = tr(ppd[0:64, 384:512], betat[:, 0:64],
                    deps=[bc_] + B.wdeps("ppd"), name="bT")
            B.w(bT, "ppd")
            bTe = cp_(betaTs[:], ppd[0:64, 384:512],
                      deps=[bT] + B.wdeps("betaTs"), name="bTe")
            B.r(bTe, "ppd")
            B.w(bTe, "betaTs")
            cpv = scan_mul(cphm[:], betaTs[:], zer64[:],
                           deps=[bTe, m_z64] + B.wdeps("cphm"), name="cp")
            B.r(cpv, "betaTs")
            B.w(cpv, "cphm")
            icv = recip(icphm[:], cphm[:], deps=[cpv] + B.wdeps("icphm"),
                        name="icp")
            B.r(icv, "cphm")
            B.w(icv, "icphm")
            if stage == "B" and c == 0:
                dbg_dump("cp_c0", cphm, [cpv], src=("cphm",))
            iT = tr(ppd[:, 192:256], icphm[:],
                    deps=[icv] + B.wdeps("ppd"), name="icpT")
            B.r(iT, "icphm")
            B.w(iT, "ppd")
            iTe = cp_(icpT[:], ppd[:, 192:256], deps=[iT] + B.wdeps("icpT"),
                      name="icpTe")
            B.w(iTe, "icpT")
            iTb = cp_(icpTb[:], ppd[:, 192:256], deps=[iT] + B.wdeps("icpTb"),
                      name="icpTb")
            B.r(iTe, "ppd")
            B.r(iTb, "ppd")
            B.w(iTb, "icpTb")
            ceT = tr(ppd[0:1, 256:320], cphm[:, CH - 1:CH],
                     deps=[cpv] + B.wdeps("ppd"), name="cpeT")
            B.r(ceT, "cphm")
            B.w(ceT, "ppd")
            ceE = cp_(cperow[0:1, 0:HM], ppd[0:1, 256:320],
                      deps=[ceT] + B.wdeps("cperow"), name="cpeE")
            B.r(ceE, "ppd")
            B.w(ceE, "cperow")
            cpev = cperow[0:1, 0:HM].rearrange("p (a b m) -> p a b m", a=16, b=2)
            cr0 = cp_(rowst2[0:1, 0:32], cpev[:, :, 0, :],
                      deps=[ceE] + B.wdeps("rowst2"), name="cpe0")
            B.r(cr0, "cperow")
            B.w(cr0, "rowst2")
            cr1 = cp_(rowst2[0:1, 32:64], cpev[:, :, 1, :], deps=[ceE, cr0],
                      name="cpe1")
            B.r(cr1, "cperow")
            B.W["rowst2"] = cr1
            cbm0 = mm(ppd[0:32, 320:352], onesr[0:1, 0:32],
                      rowst2[0:1, 0:32], True, True,
                      deps=[cr0, m_on] + B.wdeps("ppd"), name="cpbm0")
            B.r(cbm0, "rowst2")
            B.w(cbm0, "ppd")
            cbe0 = cp_(cpendb[0:32, :], ppd[0:32, 320:352],
                       deps=[cbm0] + B.wdeps("cpendb"), name="cpbe0")
            B.r(cbe0, "ppd")
            cbm1 = mm(ppd[32:64, 320:352], onesr[0:1, 0:32],
                      rowst2[0:1, 32:64], True, True, tp=(0, 32),
                      deps=[cr1, cbe0], name="cpbm1")
            B.r(cbm1, "rowst2")
            B.w(cbm1, "ppd")
            cbe = cp_(cpendb[32:64, :], ppd[32:64, 320:352],
                      deps=[cbm1], name="cpbe1")
            B.r(cbe, "ppd")
            B.w(cbe, "cpendb")
            B.R["cpendb"] = [cbe0, cbe]
            cbe = cbe0 and cbe

            fq_evs = None
            if stage == "B" and "q" not in skips:
                qmm = proj(pp0, "pp0", wqTb, d_wq, 2, h1dep, "q")
                ph_q = rope_elu(pp0[:, 0:1024], "pp0", qtm, "qtm", slot, qmm)
                B.r(ph_q, "pp0")
                if c == 0:
                    dbg_dump("phiq_c0", qtm, [ph_q], src=("qtm",))
                fq_evs = transp_batch(qtm, "qtm", fqfm, "fqfm", 16,
                                      [ph_q] + B.wdeps("fqfm"), "fq",
                                      src_w=64)

            kmm = proj(pp1, "pp1", wkTb, d_wk, 2, h1dep, "k", war=("pA0", "pA1"))
            if stage == "A" and c == 0:
                dbg_pre[0] = "A_ropek_c0"
            ph_k = rope_elu(pp1[:, 0:1024], "pp1", ktm, "h1tm", slot, kmm)
            B.r(ph_k, "pp1")
            if stage == "A" and c == 0:
                dbg_dump("A_phik_c0", ktm, [ph_k], src=("h1tm",))
            fk_evs = transp_batch(ktm, "h1tm", fkfm, "fkfm", 16,
                                  [ph_k] + B.wdeps("fkfm"), "fk", src_w=64)

            ktops = []
            for m in range(2):
                kv = tt(kt[m][:].rearrange("p (h d) -> p h d", h=32),
                        ktm[:].rearrange("p (h d) -> p h d", h=32),
                        icpT[:, m::2].unsqueeze(2).broadcast_to((128, 32, 32)),
                        ALU.mult, deps=[ph_k, iTe] + B.wdeps("kt%d" % m),
                        name="ktb%d" % m)
                B.r(kv, "h1tm")
                B.r(kv, "icpT")
                B.w(kv, "kt%d" % m)
                ktops.append(kv)

            vmm = proj(pp2, "pp2", wvTb, d_wv, 2, h1dep, "v", war=("pN0", "pN1", "st_pp2"))
            vc = cp_(vtm[:], pp2[:, 0:1024], deps=[vmm] + B.wdeps("vtm"),
                     name="vtm")
            B.r(vc, "pp2")
            B.w(vc, "vtm")
            vtops = []
            if stage == "B" and "vt" not in skips:
                for m in range(2):
                    vv = tt(vt[m][:].rearrange("p (h d) -> p h d", h=32),
                            vtm[:].rearrange("p (h d) -> p h d", h=32),
                            icpT[:, m::2].unsqueeze(2).broadcast_to((128, 32, 32)),
                            ALU.mult, deps=[vc, iTe] + B.wdeps("vt%d" % m),
                            name="vtb%d" % m)
                    B.r(vv, "vtm")
                    B.r(vv, "icpT")
                    B.w(vv, "vt%d" % m)
                    vtops.append(vv)

            yops = []
            if stage == "B" and "scan" in skips:
                yops = [memset(ytm[:], 0.1, deps=B.wdeps("qtm"), name="yfake")]
                B.w(yops[0], "qtm")
            if stage == "B" and "scan" not in skips:
                nq_evs = {}
                den_last = None
                for g in range(8):
                    ga_ = g % 2
                    heads = [2 * j + ga_ for j in range(4 * (g // 2),
                                                        4 * (g // 2) + 4)]
                    pA = pp1[:, ga_ * 512:(ga_ + 1) * 512]
                    pAn = "pA%d" % ga_
                    amms = []
                    for hh in range(4):
                        h = heads[hh]
                        pr, hb = h // 2, h % 2
                        dps = [fk_evs[pr // 4], fq_evs[pr // 4]]
                        if hh == 0:
                            dps += B.wdeps(pAn)
                            if g == 0:
                                dps += B.wdeps("pp1")
                        amms.append(mm(
                            pA[:, hh * 128:(hh + 1) * 128],
                            fkfm[32 * hb:32 * hb + 32, pr * 128:(pr + 1) * 128],
                            fqfm[32 * hb:32 * hb + 32, pr * 128:(pr + 1) * 128],
                            hh == 0, hh == 3, tp=(32 * hb, 0), deps=dps,
                            name="Amm"))
                        B.r(amms[-1], "fkfm")
                        B.r(amms[-1], "fqfm")
                    B.w(amms[0], pAn)
                    B.W[pAn] = amms[-1]
                    mev = tt(Ab[ga_][:].rearrange("p (h t) -> p h t", h=4)
                             .transpose([0, 2, 1]),
                             pA[:].rearrange("p (h t) -> p h t", h=4)
                             .transpose([0, 2, 1]),
                             masku[:].unsqueeze(2).broadcast_to((128, CH, 4)),
                             ALU.mult,
                             deps=[amms[-1], d_masku] + B.wdeps("Ab%d" % ga_),
                             name="Aev")
                    B.r(mev, pAn)
                    B.w(mev, "Ab%d" % ga_)
                    qt = g // 2
                    pN = pp2[:, (qt % 2) * 512:((qt % 2) + 1) * 512]
                    pNn = "pN%d" % (qt % 2)
                    if "nd" in skips:
                        continue
                    for hh in range(4):
                        h = heads[hh]
                        pr, hb, ha = h // 2, h % 2, h // 2
                        Abl = Ab[ga_][:, hh * 128:(hh + 1) * 128]
                        fql = fqfm[32 * hb:32 * hb + 32, pr * 128:(pr + 1) * 128]
                        for m in range(2):
                            hm = 2 * h + m
                            ncol = (hm % 16) * 32
                            scol = (ha * 2 + m) * 33
                            dps = [mev, vtops[m], iTb]
                            if hh == 0 and m == 0 and g % 2 == 0:
                                dps += B.wdeps(pNn)
                                if g < 2:
                                    dps += B.wdeps("pp2")
                            n1 = mm(pN[:, ncol:ncol + 32], Abl,
                                    vt[m][:, h * 32:(h + 1) * 32],
                                    hh == 0 and m == 0 and g % 2 == 0, False,
                                    deps=dps, name="nd_av")
                            B.r(n1, "vt%d" % m)
                            B.r(n1, "Ab%d" % ga_)
                            ddps = [mev, iTb]
                            if hm == 0:
                                ddps += B.wdeps("ppd")
                            d1_ = mm(ppd[:, 128 + hm:129 + hm], Abl,
                                     icpTb[:, hm:hm + 1], hm == 0, False,
                                     deps=ddps, name="nd_ai")
                            B.r(d1_, "icpTb")
                            B.r(d1_, "Ab%d" % ga_)
                            n2 = mm(pN[:, ncol:ncol + 32], fql,
                                    Sb[32 * hb:32 * hb + 32, scol:scol + 32],
                                    False,
                                    g % 2 == 1 and hh == 3 and m == 1,
                                    deps=[B.W.get("Sb")],
                                    name="nd_qs")
                            d2_ = mm(ppd[:, 128 + hm:129 + hm], fql,
                                     Sb[32 * hb:32 * hb + 32,
                                        scol + 32:scol + 33],
                                     False, hm == 63, deps=[], name="nd_qk")
                            B.r(n2, "Sb")
                            B.r(d2_, "Sb")
                            B.r(n2, "fqfm")
                            B.r(d2_, "fqfm")
                            if hm == 0:
                                B.w(d1_, "ppd")
                            if hh == 0 and m == 0 and g % 2 == 0:
                                B.w(n1, pNn)
                            B.W[pNn] = n2
                            den_last = d2_
                    B.W["ppd"] = den_last
                    if g % 2 == 1:
                        dst = scr[2] if qt < 2 else scr[3]
                        dstn = "scr2" if qt < 2 else "scr3"
                        dps = [B.W[pNn]]
                        if qt % 2 == 0:
                            dps += B.wdeps(dstn)
                        ev = cp_(dst[:, (qt % 2) * 512:((qt % 2) + 1) * 512],
                                 pN[:], deps=dps, name="ndev")
                        B.r(ev, pNn)
                        if qt % 2 == 0:
                            B.w(ev, dstn)
                        else:
                            B.W[dstn] = ev
                        nq_evs[qt] = ev
                nq_done = [nq_evs[q] for q in range(4)] if "nd" not in skips else []
                if "nd" in skips or "y" in skips:
                    yops = [memset(ytm[:], 0.1, deps=B.wdeps("qtm") +
                                   (B.wdeps("pN0") + B.wdeps("pN1")), name="yfake")]
                    B.w(yops[0], "qtm")
                if "nd" not in skips and "y" not in skips:
                    dv = stt(denf[:], icpT[:], EPS, ppd[:, 128:192],
                             ALU.mult, ALU.add,
                             deps=[den_last, iTe] + B.wdeps("denf"), name="den")
                    B.r(dv, "ppd")
                    B.r(dv, "icpT")
                    B.w(dv, "denf")
                    drc = recip(mden[:], denf[:], deps=[dv] + B.wdeps("mden"),
                                name="denr")
                    B.r(drc, "denf")
                    md = tt(mden[:], mixt[:], mden[:], ALU.mult, deps=[drc, mxv],
                            name="mden")
                    B.r(md, "mixt")
                    B.w(md, "mden")
                    mdv = mden[:].rearrange("p (h m) -> p h m", h=32)
                    for half in range(2):
                        srcv = (scr[2] if half == 0 else scr[3])[:] \
                            .rearrange("p (h m c) -> p h m c", h=16, m=2)
                        y_sl = ytm[:, half * 512:(half + 1) * 512] \
                            .rearrange("p (h c) -> p h c", h=16)
                        tmp = scr[4][:, 0:512].rearrange("p (h c) -> p h c", h=16)
                        y1 = tt(y_sl, srcv[:, :, 0, :],
                                mdv[:, half * 16:(half + 1) * 16, 0:1]
                                .broadcast_to((128, 16, 32)), ALU.mult,
                                deps=[md, nq_done[2 * half], nq_done[2 * half + 1]] +
                                (B.wdeps("qtm") if half == 0 else []), name="y1")
                        y2 = tt(tmp, srcv[:, :, 1, :],
                                mdv[:, half * 16:(half + 1) * 16, 1:2]
                                .broadcast_to((128, 16, 32)), ALU.mult,
                                deps=[md, nq_done[2 * half], nq_done[2 * half + 1]] +
                                B.wdeps("scr4"), name="y2")
                        B.w(y2, "scr4")
                        y3 = tt(y_sl, y_sl, tmp, ALU.add, deps=[y1, y2], name="y3")
                        B.r(y3, "scr4")
                        B.r(y3, "scr2" if half == 0 else "scr3")
                        B.r(y3, "mden")
                        yops.append(y3)
                    B.w(yops[0], "qtm")
                    B.W["qtm"] = yops[-1]
                    if c == 0:
                        dbg_dump("y_c0", ytm, yops, src=("qtm",))

            last_sb = None
            for par in range(2):
                smms = []
                kmms = []
                for ha in range(16):
                    h = 2 * ha + par
                    for m in range(2):
                        col = (ha * 2 + m) * 32
                        dps = [ktops[m], vc]
                        if ha == 0 and m == 0:
                            dps += (B.wdeps("pN0") + B.wdeps("pN1") +
                                    B.wdeps("st_pp2") + B.wdeps("pp2"))
                        smms.append(mm(pp2[32 * par:32 * par + 32,
                                           col:col + 32],
                                       kt[m][:, h * 32:(h + 1) * 32],
                                       vtm[:, h * 32:(h + 1) * 32],
                                       (ha in (0, 8) and m == 0),
                                       (ha in (7, 15) and m == 1),
                                       tp=(0, 32 * par) if par else None,
                                       deps=dps, name="st_s"))
                        B.r(smms[-1], "kt%d" % m)
                        B.r(smms[-1], "vtm")
                        kdps = [ktops[m], m_onc]
                        if ha == 0 and m == 0:
                            kdps += B.wdeps("ppd")
                        kmms.append(mm(ppd[32 * par:32 * par + 32,
                                           224 + (ha * 2 + m):
                                           225 + (ha * 2 + m)],
                                       kt[m][:, h * 32:(h + 1) * 32],
                                       onescb[:], (ha == 0 and m == 0),
                                       (ha == 15 and m == 1),
                                       tp=(0, 32 * par) if par else None,
                                       deps=kdps, name="st_k"))
                        B.r(kmms[-1], "kt%d" % m)
                B.w(smms[0], "st_pp2")
                B.W["st_pp2"] = smms[-1]
                B.w(kmms[0], "ppd")
                B.W["ppd"] = kmms[-1]
                rsl = slice(32 * par, 32 * par + 32)
                Sv = Sf[rsl.start - 32 * par if False else 0:32, :] if False \
                    else None
                Sview = Sf[rsl, :].rearrange("p (a m c) -> p a m c", a=16, m=2) \
                    if par else Sf[0:32, :].rearrange("p (a m c) -> p a m c",
                                                      a=16, m=2)
                SS = Sview[:, :, :, 0:32]
                SK = Sview[:, :, :, 32:33]
                pSv = pp2[rsl, 0:1024].rearrange("p (a m c) -> p a m c",
                                                 a=16, m=2)
                sc1v = scr[1][rsl, 0:1024].rearrange("p (a m c) -> p a m c",
                                                     a=16, m=2)
                cpeview = cpendb[rsl, :].rearrange("p (a m) -> p a m", a=16)
                u1 = tt(sc1v, pSv, SS, ALU.add,
                        deps=[smms[-1], B.W.get("Sf")] + B.wdeps("scr1"),
                        name="su_add")
                B.r(u1, "st_pp2")
                B.r(u1, "Sf")
                B.w(u1, "scr1")
                u2 = tt(SS, sc1v,
                        cpeview.unsqueeze(3).broadcast_to((32, 16, 2, 32)),
                        ALU.mult, deps=[u1, cbe] + B.wdeps("Sf"),
                        name="su_mul")
                B.r(u2, "scr1")
                B.r(u2, "cpendb")
                uk1 = tt(kupd[rsl, 0:32].rearrange("p (a m) -> p a m", a=16),
                         ppd[rsl, 224:256].rearrange("p (a m) -> p a m", a=16),
                         SK.squeeze(3), ALU.add,
                         deps=[kmms[-1], B.W.get("Sf")] + B.wdeps("kupd"),
                         name="ku_add")
                B.r(uk1, "ppd")
                B.r(uk1, "Sf")
                B.w(uk1, "kupd")
                uk2 = tt(SK.squeeze(3),
                         kupd[rsl, 0:32].rearrange("p (a m) -> p a m", a=16),
                         cpeview, ALU.mult, deps=[uk1, cbe, u2], name="ku_mul")
                B.r(uk2, "kupd")
                B.r(uk2, "cpendb")
                B.w(uk2, "Sf")
                B.W["Sf"] = uk2
            sb_ = cp_(Sb, Sf, deps=[B.W["Sf"]] + B.wdeps("Sb"), name="Sbc")
            B.r(sb_, "Sf")
            B.w(sb_, "Sb")

            if stage == "A":
                if c == 0:
                    dbg_dump("A_kt0_c0", kt[0], [ktops[0]], src=("kt0",))
                    dbg_dump("A_vtm_c0", vtm, [vc], src=("vtm",))
                    dbg_dump("A_cp_c0", cphm, [cpv], src=("cphm",))
                    dbg_dump("A_S_c0", Sf, [B.W["Sf"]], src=("Sf",))
                    dbg_dump("A_cpe_c0", cpendb, [cbe], src=("cpendb",))
                return

            ytevs = transp_batch(ytm, "qtm", yfm, "yfm", 8,
                                 yops + B.wdeps("yfm"), "yT")
            oplast = None
            for n in range(2):
                for k in range(8):
                    dps = [ytevs[-1], d_wout]
                    if n == 0 and k == 0:
                        dps += B.wdeps("pp0")
                    oplast = mm(pp0[:, n * 512:(n + 1) * 512],
                                yfm[:, k * 128:(k + 1) * 128],
                                woutTb[:, k * 1024 + n * 512:
                                       k * 1024 + (n + 1) * 512],
                                k == 0, k == 7, deps=dps, name="oproj")
                    B.r(oplast, "yfm")
            B.w(oplast, "pp0")
            atno = tt(scr[1], pp0[:, 0:1024], boutb[:],
                      ALU.add, deps=[oplast] + bct2_deps + B.wdeps("scr1"),
                      name="atno")
            B.r(atno, "pp0")
            B.w(atno, "scr1")
            if c == 0:
                dbg_dump("atno_c0", scr[1], [atno], src=("scr1",))
            h1r = ln_apply(xrb, xname, scr[0], "scr0", ca1b, sab, 40, [d_x],
                           extra_deps=bct_deps + [atno])
            if c == 0:
                dbg_dump("h1z_c0", scr[0], [h1r], src=("scr0",))
            zz = tt(scr[1], scr[1], scr[0], ALU.add, deps=[h1r, atno],
                    name="z")
            B.r(zz, "scr0")
            B.w(zz, "scr1")
            if c == 0:
                dbg_dump("z_c0", scr[1], [zz], src=("scr1",))
            a1 = act(scr[4], scr[1], AF.Copy, accum=stat[:, 48:49],
                     deps=[zz] + B.wdeps("scr4") + B.wdeps("stat48"),
                     name="z_c")
            B.w(a1, "scr4")
            a2 = act(scr[4], scr[1], AF.Square, accum=stat[:, 49:50],
                     deps=[zz, a1], name="z_sq")
            B.w(a2, "scr4")
            v1 = ts(stat[:, 50:51], stat[:, 48:49], 1.0 / HD, ALU.mult,
                    deps=[a1], name="z_mu")
            v2 = ts(stat[:, 51:52], stat[:, 50:51], stat[:, 50:51], ALU.mult,
                    deps=[v1], name="z_mu2")
            v3 = ts(stat[:, 52:53], stat[:, 49:50], 1.0 / HD, ALU.mult,
                    deps=[a2], name="z_ex2")
            v4 = ts(stat[:, 53:54], stat[:, 52:53], stat[:, 51:52],
                    ALU.subtract, LN_EPS, ALU.add, deps=[v2, v3], name="z_var")
            s3 = act(stat[:, 54:55], stat[:, 53:54], AF.Ln, deps=[v4],
                     name="z_lv")
            s4 = act(stat[:, 54:55], stat[:, 54:55], AF.Exp, scale=-0.5,
                     deps=[s3], name="z_rs")
            v5 = ts(stat[:, 55:56], stat[:, 50:51], -1.0, ALU.mult, deps=[v1],
                    name="z_nm")
            zc = ts(scr[1], scr[1], stat[:, 55:56], ALU.add, stat[:, 54:55],
                    ALU.mult, deps=[v5, s4], name="z_n")
            B.w(zc, "scr1")
            B.w(zc, "stat48")
            if c == 0:
                dbg_dump("zn_c0", scr[1], [zc], src=("scr1",))
                dbg_dump("zstat_c0", stat[:, 48:56], [zc], src=("stat48",))
            af1 = tt(scr[1], scr[1], lngb[:], ALU.mult, deps=[zc] + bct2_deps,
                     name="af1")
            af2 = tt(scr[1], scr[1], lnbb[:], ALU.add, deps=[af1], name="af2")
            if c == 0:
                dbg_dump("af_c0", scr[1], [af2], src=("scr1",))
                dbg_dump("gab_t", gab[:], [af2] + bct2_deps)
                dbg_dump("lngb_t", lngb[:], [af2] + bct2_deps)
                dbg_dump("lnbb_t", lnbb[:], [af2] + bct2_deps)
            xg = tt(scr[1], scr[1], gab[:], ALU.mult, deps=[af2], name="xga")
            if c == 0:
                dbg_dump("xga_c0", scr[1], [xg], src=("scr1",))
            xm = tt(scr[2], scr[1], xrb, ALU.add,
                    deps=[xg, d_x] + B.wdeps("scr2"), name="xm")
            B.r(xm, xname)
            B.r(xm, "scr1")
            B.w(xm, "scr2")
            if c == 0:
                dbg_dump("xmid_c0", scr[2], [xm], src=("scr2",))
            d_xm = dmaQ(xmid_d[c * CH:(c + 1) * CH, :], scr[2], deps=[xm],
                        name="xmout")
            B.r(d_xm, "scr2")
            h2c = ln_apply(scr[2], "scr2", scr[0], "scr0", cb1b, sbb, 32, [xm],
                           extra_deps=bct2_deps)
            if c == 0:
                dbg_dump("h2_c0", scr[0], [h2c], src=("scr0",))
            h2tgts = [h2T[:, j * tpc + c * CH:j * tpc + (c + 1) * CH]
                      for j in range(8)]
            h2evs = transp_batch(scr[0], "scr0", h2T, "h2T_c%d" % c, 8,
                                 [h2c], "h2", dst_targets=h2tgts)
            glast = None
            for k in range(8):
                dps = [h2evs[k], d_gateTb]
                if k == 0:
                    dps += B.wdeps("ppd")
                glast = mm(ppd[0:8, 0:128], gateTb[:, k * 8:(k + 1) * 8],
                           h2T[:, k * tpc + c * CH:k * tpc + (c + 1) * CH],
                           k == 0, k == 7, deps=dps, name="gmm")
            B.w(glast, "ppd")
            gev = ts(lgT[0:8, 0:128], ppd[0:8, 0:128], gatebc[:], ALU.add,
                     deps=[glast, d_gatebc] + B.wdeps("lgT"), name="gev")
            B.r(gev, "ppd")
            B.w(gev, "lgT")
            gtr = tr(ppd[:, 128:136], lgT[0:8, 0:128],
                     deps=[gev] + B.wdeps("ppd"), name="gtr")
            B.w(gtr, "ppd")
            gte = cp_(lgtm[:], ppd[:, 128:136], deps=[gtr] + B.wdeps("lgtm"),
                      name="gte")
            B.r(gte, "ppd")
            B.w(gte, "lgtm")
            if c == 0:
                dbg_dump("logit_c0", lgtm, [gte], src=("lgtm",))
            t_m1 = vmax(stat[:, 56:57], lgtm[:],
                        deps=[gte] + B.wdeps("stat56"), name="g_m1")
            t_ge = ts(g8a[:], lgtm[:], stat[:, 56:57], ALU.is_ge,
                      deps=[t_m1] + B.wdeps("g8a"), name="g_ge")
            B.w(t_ge, "g8a")
            t_mk = stt(g8b[:], g8a[:], -1e30, lgtm[:], ALU.mult, ALU.add,
                       deps=[t_ge] + B.wdeps("g8b"), name="g_mk")
            B.w(t_mk, "g8b")
            t_m2 = vmax(stat[:, 57:58], g8b[:], deps=[t_mk], name="g_m2")
            t_sel = ts(g8a[:], lgtm[:], stat[:, 57:58], ALU.is_ge,
                       deps=[t_m2, t_ge], name="g_sel")
            B.w(t_sel, "g8a")
            t_nm = ts(stat[:, 58:59], stat[:, 56:57], -1.0, ALU.mult,
                      deps=[t_m1], name="g_nm")
            t_e = act(g8b[:], lgtm[:], AF.Exp, bias=stat[:, 58:59],
                      deps=[t_nm, t_mk, t_m2], name="g_e")
            B.w(t_e, "g8b")
            t_wr = tt(g8b[:], g8b[:], g8a[:], ALU.mult, deps=[t_e, t_sel],
                      name="g_wr")
            B.r(t_wr, "g8a")
            B.w(t_wr, "g8b")
            t_ss = vsum(stat[:, 59:60], g8b[:], deps=[t_wr], name="g_ss")
            t_rs = recip(stat[:, 60:61], stat[:, 59:60], deps=[t_ss],
                         name="g_rs")
            t_w = ts(wgate[:, c * 8:(c + 1) * 8], g8b[:], stat[:, 60:61],
                     ALU.mult, deps=[t_rs] + B.wdeps("stat56"), name="g_w")
            B.r(t_w, "g8b")
            B.w(t_w, "stat56")
            if c == 0:
                dbg_dump("wg_c0", wgate[:, 0:8], [t_w])
            wtr = tr(ppd[0:8, 136:264], wgate[:, c * 8:(c + 1) * 8],
                     deps=[t_w] + B.wdeps("ppd") + B.wdeps("ppd") +
                     B.wdeps("ppd"), name="wtr")
            B.w(wtr, "ppd")
            wte = cp_(wTg[0:8, c * CH:(c + 1) * CH], ppd[0:8, 136:264],
                      deps=[wtr], name="wte")
            B.r(wte, "ppd")
            B.r(wte, "ppd")
            B.r(wte, "ppd")
            B.w(wte, "wTg_c%d" % c)

        if stop_after == "phase0":
            L.op("Q", lambda e: e.nop(), deps=L.lasts(), name="final_wait")
            L.assign()
            with nc.Block() as block:
                block.sync(L.body("Q", esems, dsems))
                block.gpsimd(L.body("G", esems, dsems))
                block.tensor(L.body("T", esems, dsems))
                block.vector(L.body("V", esems, dsems))
                block.scalar(L.body("S", esems, dsems))
            return nc
        for c in range(NCH):
            emit_chunk(c, "A")
            if stop_after == "chunkA0" and c == 0:
                break
        fl = ts(Sf, Sf, flagc[:], ALU.mult,
                deps=[d_flag] + B.wdeps("Sf"), name="flagS")
        B.r(fl, "Sf")
        B.w(fl, "Sf")
        sb0 = cp_(Sb, Sf, deps=[fl] + B.wdeps("Sb"), name="Sb0")
        B.r(sb0, "Sf")
        B.w(sb0, "Sb")
        dbg_dump("S_in", Sf, [fl])

        if stop_after in ("stageA", "chunkA0"):
            L.op("Q", lambda e: e.nop(), deps=L.lasts(), name="final_wait")
            L.assign()
            with nc.Block() as block:
                block.sync(L.body("Q", esems, dsems))
                block.gpsimd(L.body("G", esems, dsems))
                block.tensor(L.body("T", esems, dsems))
                block.vector(L.body("V", esems, dsems))
                block.scalar(L.body("S", esems, dsems))
            return nc
        for c in range(NCH):
            emit_chunk(c, "B")
            if stop_after == "chunkB0" and c == 0:
                break

        if stop_after in ("stageB", "chunkB0"):
            L.op("Q", lambda e: e.nop(), deps=L.lasts(), name="final_wait")
            L.assign()
            with nc.Block() as block:
                block.sync(L.body("Q", esems, dsems))
                block.gpsimd(L.body("G", esems, dsems))
                block.tensor(L.body("T", esems, dsems))
                block.vector(L.body("V", esems, dsems))
                block.scalar(L.body("S", esems, dsems))
            return nc
        bars = L.lasts()
        first_deps = list(bars)
        wTg_deps = [B.W["wTg_c%d" % c] for c in range(NCH)]
        h2T_deps = [B.W["h2T_c%d" % c] for c in range(NCH)]
        SEGS = segs(THALF)
        all_douts = []

        for hf in range(2):
            oz = memset(outacc, 0.0,
                        deps=B.wdeps(*["oacc%d" % m_ for m_ in range(8)]) +
                        first_deps, name="oz")
            B.w(oz, *["oacc%d" % m_ for m_ in range(8)])
            for e in range(FE):
                ep = e % 2
                dw2 = []
                for q4 in range(4):
                    dw2.append(dmaG(w2b[:, q4 * 4096:(q4 + 1) * 4096],
                                    w2_d[e][:, q4 * 4096:(q4 + 1) * 4096],
                                    deps=B.wdeps("w2b") + first_deps,
                                    name="w2l"))
                B.w(dw2[-1], "w2b")
                dwr = dmaQ(wrow[ep][0:1, 0:THALF],
                           wTg[e:e + 1, hf * THALF:(hf + 1) * THALF],
                           deps=B.wdeps("wrow%d" % ep) + wTg_deps + first_deps,
                           name="wrl")
                B.w(dwr, "wrow%d" % ep)
                wbm_last = None
                for n, (p0_, wn) in enumerate(SEGS):
                    wbm_last = mm(pp2[:, p0_:p0_ + wn], onesrb[0:1, 0:128],
                                  wrow[ep][0:1, p0_:p0_ + wn], True, True,
                                  deps=[dwr, m_onb] +
                                  ((B.wdeps("pp2f") + first_deps)
                                   if n == 0 else []),
                                  name="wbm")
                    if n == 0:
                        B.w(wbm_last, "pp2f")
                B.W["pp2f"] = wbm_last
                B.r(wbm_last, "wrow%d" % ep)
                wbe = cp_(wbc[ep][:, 0:THALF], pp2[:, 0:THALF],
                          deps=[wbm_last] + B.wdeps("wbc%d" % ep), name="wbe")
                B.r(wbe, "pp2f")
                B.w(wbe, "wbc%d" % ep)

                def load_slot(gg, e=e):
                    sl = gg % 3
                    d0 = dmaG(w0r[sl][:].rearrange("p (a n) -> p a n", a=8),
                              w0_d[e].rearrange("p (a h) -> p a h", a=8)
                              [:, :, gg * 256:(gg + 1) * 256],
                              deps=B.wdeps("w0r%d" % sl) + first_deps,
                              name="w0l")
                    B.w(d0, "w0r%d" % sl)
                    d1_ = dmaG(w1r[sl][:].rearrange("p (a n) -> p a n", a=8),
                               w1_d[e].rearrange("p (a h) -> p a h", a=8)
                               [:, :, gg * 256:(gg + 1) * 256],
                               deps=B.wdeps("w1r%d" % sl) + first_deps,
                               name="w1l")
                    B.w(d1_, "w1r%d" % sl)
                    return d0, d1_

                slot_dma = {gg: load_slot(gg) for gg in range(3)}
                for j in range(16):
                    gg, jj = j // 2, j % 2
                    sl = gg % 3
                    jp = j % 2
                    d0, d1_ = slot_dma[gg]
                    ph0 = pp0 if jp == 0 else pp1
                    ph0n = "pf0_%d" % jp
                    h0last = None
                    for n, (p0_, wn) in enumerate(SEGS):
                        for k in range(8):
                            dps = [d0] + h2T_deps
                            if n == 0 and k == 0:
                                dps += B.wdeps(ph0n)
                            h0last = mm(ph0[:, p0_:p0_ + wn],
                                        w0r[sl][:, k * 256 + jj * 128:
                                                k * 256 + jj * 128 + 128],
                                        h2T[:, k * tpc + hf * THALF + p0_:
                                            k * tpc + hf * THALF + p0_ + wn],
                                        k == 0, k == 7, deps=dps, name="h0mm")
                            B.r(h0last, "w0r%d" % sl)
                    B.w(h0last, ph0n)
                    B.W[ph0n] = h0last
                    if jp == 0:
                        h1tgts = [(pp2, p0_, wn) for (p0_, wn) in SEGS]
                        h1n = "pp2f"
                    else:
                        tg = [ppd, ppt]
                        h1tgts = [(tg[n], 0, wn)
                                  for n, (p0_, wn) in enumerate(SEGS)]
                        h1n = "pdtf"
                    h1last = None
                    for n, (p0_, wn) in enumerate(SEGS):
                        tgt, tb, _ = h1tgts[n]
                        toff = tb if jp else p0_
                        for k in range(8):
                            dps = [d1_]
                            if n == 0 and k == 0:
                                dps += B.wdeps(h1n)
                            h1last = mm(tgt[:, toff:toff + wn],
                                        w1r[sl][:, k * 256 + jj * 128:
                                                k * 256 + jj * 128 + 128],
                                        h2T[:, k * tpc + hf * THALF + p0_:
                                            k * tpc + hf * THALF + p0_ + wn],
                                        k == 0, k == 7, deps=dps, name="h1mm")
                            B.r(h1last, "w1r%d" % sl)
                    B.w(h1last, h1n)
                    B.W[h1n] = h1last
                    sil_last = None
                    for n, (p0_, wn) in enumerate(SEGS):
                        tgt, tb, _ = h1tgts[n]
                        toff = tb if jp else p0_
                        dps = [h1last, d_b1t]
                        if n == 0:
                            dps += B.wdeps("sil%d" % jp)
                        sil_last = act(silscr[jp][:, p0_:p0_ + wn],
                                       tgt[:, toff:toff + wn], AF.Silu,
                                       bias=b1t[:, e * 16 + j:e * 16 + j + 1],
                                       deps=dps, name="sil")
                        B.r(sil_last, h1n)
                        if n == 0:
                            B.w(sil_last, "sil%d" % jp)
                    B.W["sil%d" % jp] = sil_last
                    ab_ = stt(actb[:, j * THALF:(j + 1) * THALF],
                              ph0[:, 0:THALF],
                              b0t[:, e * 16 + j:e * 16 + j + 1],
                              silscr[jp][:, 0:THALF], ALU.add, ALU.mult,
                              deps=[h0last, sil_last, d_b0t] +
                              B.wdeps("actb%d" % j), name="actb")
                    B.r(ab_, ph0n)
                    B.r(ab_, "sil%d" % jp)
                    B.w(ab_, "actb%d" % j)
                    if jj == 1 and gg + 3 <= 7:
                        slot_dma[gg + 3] = load_slot(gg + 3)

                for m_ in range(8):
                    pO = pp0 if m_ % 2 == 0 else pp1
                    pOn = "pf0_%d" % (m_ % 2)
                    olast = None
                    for j in range(16):
                        for n, (p0_, wn) in enumerate(SEGS):
                            dps = [dw2[-1], B.W["actb%d" % j]]
                            if j == 0 and n == 0:
                                dps += B.wdeps(pOn)
                            olast = mm(pO[:, p0_:p0_ + wn],
                                       w2b[:, j * 1024 + m_ * 128:
                                           j * 1024 + m_ * 128 + 128],
                                       actb[:, j * THALF + p0_:
                                            j * THALF + p0_ + wn],
                                       j == 0, j == 15, deps=dps, name="o_mm")
                            B.r(olast, "actb%d" % j)
                            B.r(olast, "w2b")
                    B.w(olast, pOn)
                    B.W[pOn] = olast
                    tmp = silscr[m_ % 2]
                    t1 = tt(tmp[:, 0:THALF], pO[:, 0:THALF],
                            wbc[ep][:, 0:THALF], ALU.mult,
                            deps=[olast, wbe] + B.wdeps("sil%d" % (m_ % 2)),
                            name="of1")
                    B.r(t1, pOn)
                    B.r(t1, "wbc%d" % ep)
                    B.w(t1, "sil%d" % (m_ % 2))
                    t2 = tt(outacc[:, m_ * THALF:(m_ + 1) * THALF],
                            outacc[:, m_ * THALF:(m_ + 1) * THALF],
                            tmp[:, 0:THALF], ALU.add,
                            deps=[t1] + B.wdeps("oacc%d" % m_), name="of2")
                    B.r(t2, "sil%d" % (m_ % 2))
                    B.w(t2, "oacc%d" % m_)

            for tc in range(NTH):
                par = tc % 2
                dxm = dmaQ(xmr[par], xmid_d[hf * THALF + tc * CH:
                                            hf * THALF + (tc + 1) * CH, :],
                           deps=B.wdeps("xmr%d" % par) + first_deps,
                           name="xmin")
                B.w(dxm, "xmr%d" % par)
                trs = []
                for m_ in range(8):
                    dps = [B.W.get("oacc%d" % m_)]
                    if m_ == 0:
                        dps += B.wdeps("pp2f")
                    trs.append(tr(pp2[:, m_ * 128:(m_ + 1) * 128],
                                  outacc[:, m_ * THALF + tc * CH:
                                         m_ * THALF + (tc + 1) * CH],
                                  deps=dps, name="fT"))
                    B.r(trs[-1], "oacc%d" % m_)
                B.w(trs[0], "pp2f")
                B.W["pp2f"] = trs[-1]
                bmms = []
                for n in range(2):
                    dps = [d_b2b] + wTg_deps
                    if n == 0:
                        dps += B.wdeps("pf0_0")
                    bmms.append(mm(pp0[:, n * 512:(n + 1) * 512],
                                   wTg[0:8, hf * THALF + tc * CH:
                                       hf * THALF + (tc + 1) * CH],
                                   b2b[0:8, n * 512:(n + 1) * 512],
                                   True, True, deps=dps, name="b2mm"))
                B.w(bmms[0], "pf0_0")
                B.W["pf0_0"] = bmms[-1]
                u1 = tt(otile[par], pp2[:, 0:1024], gbb[:], ALU.mult,
                        deps=[trs[-1]] + gbb_deps + B.wdeps("ot%d" % par),
                        name="u1")
                B.r(u1, "pp2f")
                B.w(u1, "ot%d" % par)
                u2 = tt(otile[par], otile[par], xmr[par], ALU.add,
                        deps=[u1, dxm], name="u2")
                B.r(u2, "xmr%d" % par)
                u3 = tt(xmr[par], pp0[:, 0:1024], gbb[:], ALU.mult,
                        deps=[bmms[-1], u2], name="u3")
                B.r(u3, "pf0_0")
                B.w(u3, "xmr%d" % par)
                u4 = tt(otile[par], otile[par], xmr[par], ALU.add,
                        deps=[u3], name="u4")
                B.r(u4, "xmr%d" % par)
                B.w(u4, "ot%d" % par)
                dout = dmaQ(out_d[hf * THALF + tc * CH:
                                  hf * THALF + (tc + 1) * CH, :],
                            otile[par], deps=[u4], name="dout")
                B.r(dout, "ot%d" % par)
                all_douts.append(dout)

        final_deps = all_douts + dbg_final
        L.op("Q", lambda e: e.nop(), deps=final_deps, name="final_wait")

        assert al.peak <= al.limit
        L.assign()
        with nc.Block() as block:
            block.sync(L.body("Q", esems, dsems))
            block.gpsimd(L.body("G", esems, dsems))
            block.tensor(L.body("T", esems, dsems))
            block.vector(L.body("V", esems, dsems))
            block.scalar(L.body("S", esems, dsems))
    return nc


def _blk(wT, kchunks, ncols):
    return np.ascontiguousarray(
        wT.reshape(kchunks, 128, ncols).transpose(1, 0, 2).reshape(128, -1))


def prep_inputs(inputs, tpc=TPC):
    f32 = lambda a: np.ascontiguousarray(np.asarray(a, np.float32))
    NCH = tpc // CH
    x = f32(inputs["L0_attn_x"])
    nb = x.shape[0]
    Wq, Wk, Wv = (f32(inputs["L0_attn_W" + s]) for s in "qkv")
    Wout = f32(inputs["L0_attn_Wout"])
    Wbeta, Wmix = f32(inputs["L0_attn_Wbeta"]), f32(inputs["L0_attn_Wmix"])
    shared = dict(
        wqT=_blk(Wq.T, 8, HD), wkT=_blk(Wk.T, 8, HD), wvT=_blk(Wv.T, 8, HD),
        woutT=_blk(Wout.T, 8, HD),
        wbmT=_blk(np.concatenate([Wbeta.T, Wmix.T], axis=1), 8, 128),
        bbm=np.concatenate([f32(inputs["L0_attn_bbeta"]),
                            f32(inputs["L0_attn_bmix"])])[None, :],
        bout_h=f32(inputs["L0_attn_bout"])[None, :],
        lng=f32(inputs["L0_attn_ln_g"])[None, :],
        lnb=f32(inputs["L0_attn_ln_b"])[None, :],
        gateT=_blk(f32(inputs["L0_ffn_gate_w"]).T, 8, 8),
        gatebc_h=f32(inputs["L0_ffn_gate_b"]).reshape(8, 1),
        wmod=f32(inputs["L0_wmod"]),
        w0r=np.stack([_blk(f32(inputs["L0_ffn_w0"])[e].T, 8, 2048)
                      for e in range(FE)]),
        w1r=np.stack([_blk(f32(inputs["L0_ffn_w1"])[e].T, 8, 2048)
                      for e in range(FE)]),
        w2r=np.stack([_blk(f32(inputs["L0_ffn_w2"])[e].T, 16, HD)
                      for e in range(FE)]),
        b0t_h=f32(inputs["L0_ffn_b0"]).reshape(8, 16, 128).transpose(2, 0, 1)
        .reshape(128, 128).copy(),
        b1t_h=f32(inputs["L0_ffn_b1"]).reshape(8, 16, 128).transpose(2, 0, 1)
        .reshape(128, 128).copy(),
        b2m=f32(inputs["L0_ffn_b2"]),
        diota=np.arange(16, dtype=np.float32)[None, :],
        identc_h=np.eye(128, dtype=np.float32),
        masku_h=np.triu(np.ones((128, 128), np.float32)),
    )
    crows = f32(inputs["L0_c"])
    in_maps = []
    for core in range(2 * nb):
        b, hf = core // 2, core % 2
        own = np.arange(hf * tpc, (hf + 1) * tpc, dtype=np.float32)
        prt = np.arange(0, tpc, dtype=np.float32)
        pos = np.concatenate([own.reshape(NCH, CH).T,
                              prt.reshape(NCH, CH).T], axis=1)
        im = dict(shared)
        im["x"] = np.ascontiguousarray(x[b, hf * tpc:(hf + 1) * tpc])
        im["xp"] = np.ascontiguousarray(x[b, 0:tpc])
        im["crowT_h"] = crows[b].reshape(8, 128).T.copy()
        im["flagc_h"] = np.full((64, 1), float(hf), np.float32)
        im["poscol_h"] = np.ascontiguousarray(pos)
        in_maps.append(im)
    return in_maps


def kernel(**inputs):
    in_maps = prep_inputs(inputs)
    nc = build()
    res = run_bass_kernel_spmd(nc, in_maps, core_ids=list(range(8)))
    out = np.empty((4, 2 * TPC, HD), np.float32)
    for core in range(8):
        b, hf = core // 2, core % 2
        out[b, hf * TPC:(hf + 1) * TPC] = res.results[core]["out"]
    return out


# version 06
# speedup vs baseline: 1.0149x; 1.0149x over previous
import numpy as np
import concourse.bass as bass
import concourse.mybir as mybir
from concourse.bass_utils import run_bass_kernel_spmd
from contextlib import ExitStack

F32 = mybir.dt.float32
BF16 = mybir.dt.bfloat16
AF = mybir.ActivationFunctionType
ALU = mybir.AluOpType
AX = mybir.AxisListType.X

HD = 1024
NH = 32
HM = 64
CH = 128
FE = 8
BMIN, BMAX = 0.85, 0.9995
EPS = 1e-6
LN_EPS = 1e-5
LOG_ROPE = float(np.log(10000.0))
PI = float(np.pi)
TPC = 2048
_last_led = None

ENGS = ("T", "V", "S", "G", "Q")
NDMASEM = 24
FENCE_DIST = 2


class _Op:
    __slots__ = ("eng", "idx", "fn", "deps", "special", "w_inc", "w_nop",
                 "val", "fval", "dsem", "dval", "name")


class Led:
    def __init__(self):
        self.st = {e: [] for e in ENGS}
        self.dcnt = [0] * NDMASEM
        self.drr = {"Q": 0, "G": NDMASEM // 2}

    def op(self, eng, fn, deps=(), special=False, name=""):
        o = _Op()
        o.eng, o.idx, o.fn = eng, len(self.st[eng]), fn
        o.deps = [d for d in deps if d is not None]
        o.special = special
        o.w_inc = o.w_nop = False
        o.val = o.fval = o.dsem = o.dval = None
        o.name = name
        self.st[eng].append(o)
        return o

    def dma(self, eng, fn, deps=(), name=""):
        o = self.op(eng, fn, deps, name=name)
        half = NDMASEM // 2
        base = 0 if eng == "Q" else half
        o.dsem = base + (self.drr[eng] % half)
        self.drr[eng] += 1
        self.dcnt[o.dsem] += 16
        o.dval = self.dcnt[o.dsem]
        return o

    def lasts(self):
        return [self.st[e][-1] for e in ENGS if self.st[e]]

    def assign(self):
        for e in ENGS:
            for o in self.st[e]:
                for d in o.deps:
                    if d.dsem is not None:
                        continue
                    if d.eng == o.eng:
                        if d.eng == "T" and not d.special:
                            continue
                        if d.special or (o.idx - d.idx) <= FENCE_DIST:
                            d.w_inc = True
                            d.w_nop = True
                    else:
                        d.w_inc = True
                        if d.special:
                            d.w_nop = True
        for e in ENGS:
            c = 0
            for o in self.st[e]:
                if e == "S" and (o.w_inc or o.w_nop):
                    o.w_inc = False
                    o.w_nop = True
                if o.w_inc:
                    c += 1
                    o.val = c
                if o.w_nop:
                    c += 1
                    o.fval = c

    def body(self, ekey, esems, dsems):
        stream = self.st[ekey]

        def run(engine):
            wm = {}
            sem_self = esems[ekey]
            for o in stream:
                for d in o.deps:
                    if d.dsem is not None:
                        sem, tgt, key = dsems[d.dsem], d.dval, ("d", d.dsem)
                    else:
                        same = d.eng == ekey
                        if same and d.eng == "T" and not d.special:
                            continue
                        if (same and not d.special
                                and (o.idx - d.idx) > FENCE_DIST):
                            continue
                        if same or d.special or d.eng == "S":
                            tgt = d.fval if d.fval is not None else d.val
                        else:
                            tgt = d.val
                        sem, key = esems[d.eng], ("e", d.eng)
                        assert tgt is not None, (ekey, o.name, d.eng, d.name)
                    if wm.get(key, -1) < tgt:
                        engine.wait_ge(sem, tgt)
                        wm[key] = tgt
                inst = o.fn(engine)
                if o.dsem is not None:
                    inst.then_inc(dsems[o.dsem], 16)
                else:
                    if o.w_inc:
                        inst.then_inc(sem_self, 1)
                    if o.w_nop:
                        if ekey == "S":
                            nop = engine.drain(fusable=False)
                        elif ekey in ("V", "G"):
                            nop = engine.engine_nop()
                        else:
                            nop = engine.nop(nofuse=True)
                        nop.then_inc(sem_self, 1)
        return run


class Buf:
    def __init__(self):
        self.R = {}
        self.W = {}

    def r(self, op, *names):
        for n in names:
            self.R.setdefault(n, []).append(op)
        return op

    def wdeps(self, *names):
        deps = []
        for n in names:
            deps.extend(self.R.get(n, []))
            if self.W.get(n) is not None:
                deps.append(self.W[n])
        return deps

    def w(self, op, *names):
        for n in names:
            self.R[n] = []
            self.W[n] = op
        return op


def segs(total, step=512):
    out = []
    p = 0
    while p < total:
        w = min(step, total - p)
        out.append((p, w))
        p += w
    return out


def build(tpc=TPC, dbg=(), stop_after=None, skips=()):
    NCH = tpc // CH
    THALF = tpc // 2
    NTH = THALF // CH
    nc = bass.Bass("TRN2", target_bir_lowering=False,
                   detect_race_conditions=False)
    L = Led()
    B = Buf()
    global _last_led
    _last_led = L
    dbg = set(dbg)

    def din(name, shape):
        return nc.dram_tensor(name, list(shape), F32, kind="ExternalInput").ap()

    x_d = din("x", (tpc, HD))
    xp_d = din("xp", (tpc, HD))
    crowT_d = din("crowT_h", (128, 8))
    flag_d = din("flagc_h", (64, 1))
    poscol_d = din("poscol_h", (128, 2 * NCH))
    diota_d = din("diota", (1, 16))
    wqT_d = din("wqT", (128, 8 * HD))
    wkT_d = din("wkT", (128, 8 * HD))
    wvT_d = din("wvT", (128, 8 * HD))
    woutT_d = din("woutT", (128, 8 * HD))
    wbmT_d = din("wbmT", (128, 8 * 128))
    bbm_d = din("bbm", (1, 128))
    bout_d = din("bout_h", (1, HD))
    lng_d = din("lng", (1, HD))
    lnb_d = din("lnb", (1, HD))
    gateT_d = din("gateT", (128, 64))
    gatebc_d = din("gatebc_h", (8, 1))
    wmod_d = din("wmod", (HD, 6144))
    w0_d = din("w0r", (FE, 128, 8 * 2048))
    w1_d = din("w1r", (FE, 128, 8 * 2048))
    w2_d = din("w2r", (FE, 128, 16 * HD))
    b0t_d = din("b0t_h", (128, 128))
    b1t_d = din("b1t_h", (128, 128))
    b2_d = din("b2m", (8, HD))
    identc_d = din("identc_h", (128, 128))
    masku_d = din("masku_h", (128, 128))
    xmid_d = nc.dram_tensor("xmid", [tpc, HD], F32, kind="Internal").ap()
    out_d = nc.dram_tensor("out", [tpc, HD], F32, kind="ExternalOutput").ap()

    with ExitStack() as stack:
        ent = stack.enter_context

        h2T = ent(nc.sbuf_tensor("h2T", [128, 8 * tpc], BF16))
        costab = ent(nc.sbuf_tensor("costab", [128, 2 * NCH * 16], F32))
        sintab = ent(nc.sbuf_tensor("sintab", [128, 2 * NCH * 16], F32))
        wgate = ent(nc.sbuf_tensor("wgate", [128, NCH * 8], F32))
        wTg = ent(nc.sbuf_tensor("wTg", [8, tpc], BF16))
        identc = ent(nc.sbuf_tensor("identc", [128, 128], F32))
        masku = ent(nc.sbuf_tensor("masku", [128, 128], F32))
        onesr = ent(nc.sbuf_tensor("onesr", [1, 128], F32))
        onesrb = ent(nc.sbuf_tensor("onesrb", [1, 128], BF16))
        onescb = ent(nc.sbuf_tensor("onescb", [128, 1], BF16))
        gatebc = ent(nc.sbuf_tensor("gatebc", [8, 1], F32))
        gateTb = ent(nc.sbuf_tensor("gateTb", [128, 64], BF16))
        gbb = ent(nc.sbuf_tensor("gbb", [128, HD], BF16))
        stat = ent(nc.sbuf_tensor("stat", [128, 64], F32))
        flagc = ent(nc.sbuf_tensor("flagc", [64, 1], F32))
        crowT = ent(nc.sbuf_tensor("crowT", [128, 8], F32))
        csil = ent(nc.sbuf_tensor("csil", [128, 8], BF16))
        ftile = ent(nc.sbuf_tensor("ftile", [128, 16], F32))
        poscol = ent(nc.sbuf_tensor("poscol", [128, 2 * NCH], F32))
        zer64 = ent(nc.sbuf_tensor("zer64", [64, CH], F32))
        b2b = ent(nc.sbuf_tensor("b2b", [8, HD], BF16))
        b0t = ent(nc.sbuf_tensor("b0t", [128, 128], F32))
        b1t = ent(nc.sbuf_tensor("b1t", [128, 128], F32))
        bbmb = ent(nc.sbuf_tensor("bbmb", [128, 128], F32))
        betaTs = ent(nc.sbuf_tensor("betaTs", [64, CH], F32))
        cperow = ent(nc.sbuf_tensor("cperow", [1, HM], F32))
        kupd = ent(nc.sbuf_tensor("kupd", [64, 32], F32))
        lgT = ent(nc.sbuf_tensor("lgT", [8, CH], F32))
        lgtm = ent(nc.sbuf_tensor("lgtm", [128, 8], F32))
        g8a = ent(nc.sbuf_tensor("g8a", [128, 8], F32))
        g8b = ent(nc.sbuf_tensor("g8b", [128, 8], F32))

        pp0 = ent(nc.psum_tensor("pp0", [128, 1024], F32))
        pp1 = ent(nc.psum_tensor("pp1", [128, 1024], F32))
        pp2 = ent(nc.psum_tensor("pp2", [128, 1024], F32))
        ppd = ent(nc.psum_tensor("ppd", [128, 512], F32))
        ppt = ent(nc.psum_tensor("ppt", [128, 512], F32))

        esems = {e: ent(nc.semaphore("es_" + e)) for e in ENGS}
        dsems = [ent(nc.semaphore("ds%d" % i)) for i in range(NDMASEM)]

        stB = ExitStack()
        stack.callback(stB.close)

        def tk(name, parts, elems, dt=F32, stk=None):
            return (stk or stB).enter_context(
                nc.sbuf_tensor(name, [parts, elems], dt))[:]

        wqTb = tk("wqTb", 128, 8 * HD, BF16)
        wkTb = tk("wkTb", 128, 8 * HD, BF16)
        wvTb = tk("wvTb", 128, 8 * HD, BF16)
        woutTb = tk("woutTb", 128, 8 * HD, BF16)
        wbmb = tk("wbmb", 128, 8 * 128, BF16)
        ca1b = tk("ca1b", 128, HD, BF16)
        sab = tk("sab", 128, HD, BF16)
        cb1b = tk("cb1b", 128, HD, BF16)
        sbb = tk("sbb", 128, HD, BF16)
        gab = tk("gab", 128, HD, BF16)
        lngb = tk("lngb", 128, HD, BF16)
        lnbb = tk("lnbb", 128, HD, BF16)
        boutb = tk("boutb", 128, HD, BF16)
        Sf = tk("Sf", 64, 32 * 33, F32)
        Sb = tk("Sb", 64, 32 * 33, BF16)
        xr = [tk("xr%d" % i, 128, HD) for i in range(2)]
        scr = [tk("scr%d" % i, 128, HD) for i in range(5)]
        qtm = tk("qtm", 128, HD)
        ytm = qtm
        h1tm = tk("h1tm", 128, HD)
        ktm = h1tm
        wmseg = [woutTb[:, 0:8 * 512], woutTb[:, 8 * 512:16 * 512]]
        rowst = scr[0][0:1, 0:512]
        rowst2 = scr[1][0:1, 0:512]
        h1T = tk("h1T", 128, HD, BF16)
        fqfm = tk("fqfm", 64, 16 * CH, BF16)
        fkfm = tk("fkfm", 64, 16 * CH, BF16)
        vtm = tk("vtm", 128, HD, BF16)
        vt = [tk("vt%d" % i, 128, HD, BF16) for i in range(2)]
        kt = [tk("kt%d" % i, 128, HD, BF16) for i in range(2)]
        Ab = [tk("Ab%d" % i, 128, 4 * CH, BF16) for i in range(2)]
        yfm = tk("yfm", 128, 8 * CH, BF16)
        betat = tk("betat", 128, HM)
        mixt = tk("mixt", 128, HM)
        cphm = tk("cphm", 64, CH)
        icphm = tk("icphm", 64, CH)
        icpT = tk("icpT", 128, HM)
        icpTb = tk("icpTb", 128, HM, BF16)
        cpendb = tk("cpendb", 64, 32)
        denf = tk("denf", 128, HM)
        mden = tk("mden", 128, HM)
        dbg_stage = [tk("dbgstage", 128, 2176)] if dbg else [None]

        def dmaQ(out, in_, deps=(), name="", slow=False):
            return L.dma("Q", lambda e, o=out, i=in_, s=slow:
                         e.dma_start(out=o, in_=i, allow_slow_non_contiguous=s),
                         deps, name=name)

        def dmaG(out, in_, deps=(), name=""):
            return L.dma("G", lambda e, o=out, i=in_: e.dma_start(out=o, in_=i),
                         deps, name=name)

        def T(fn, deps=(), name=""):
            return L.op("T", fn, deps, name=name)

        def V(fn, deps=(), special=False, name=""):
            return L.op("V", fn, deps, special=special, name=name)

        def S(fn, deps=(), name=""):
            return L.op("S", fn, deps, name=name)

        def mm(out, lhsT, rhs, start, stop, deps=(), tp=None, name="mm"):
            return T(lambda e, o=out, l=lhsT, r=rhs, a=start, b=stop, t=tp:
                     e.matmul(o, l, r, start=a, stop=b, tile_position=t),
                     deps, name=name)

        def cp_(out, in_, deps=(), name="cp"):
            return V(lambda e, o=out, i=in_: e.tensor_copy(out=o, in_=i),
                     deps, name=name)

        def tt(out, a, b, op, deps=(), name="tt"):
            return V(lambda e, o=out, x=a, y=b, p=op:
                     e.tensor_tensor(out=o, in0=x, in1=y, op=p), deps, name=name)

        def ts(out, a, s1, op0, s2=None, op1=None, deps=(), name="ts"):
            if op1 is None:
                return V(lambda e, o=out, x=a, u=s1, p=op0:
                         e.tensor_scalar(out=o, in0=x, scalar1=u, scalar2=None,
                                         op0=p), deps, name=name)
            return V(lambda e, o=out, x=a, u=s1, v=s2, p=op0, q=op1:
                     e.tensor_scalar(out=o, in0=x, scalar1=u, scalar2=v,
                                     op0=p, op1=q), deps, name=name)

        def stt(out, a, s, b, op0, op1, deps=(), name="stt"):
            return V(lambda e, o=out, x=a, u=s, y=b, p=op0, q=op1:
                     e.scalar_tensor_tensor(out=o, in0=x, scalar=u, in1=y,
                                            op0=p, op1=q), deps, name=name)

        def act(out, in_, func, bias=0.0, scale=1.0, accum=None, deps=(), name="act"):
            return S(lambda e, o=out, i=in_, f=func, b=bias, s=scale, a=accum:
                     e.activation(out=o, in_=i, func=f, bias=b, scale=s,
                                  accum_out=a), deps, name=name)

        def recip(out, in_, deps=(), name="recip"):
            return V(lambda e, o=out, i=in_: e.reciprocal(out=o, in_=i),
                     deps, special=True, name=name)

        def vmax(out, in_, deps=(), name="vmax"):
            return V(lambda e, o=out, i=in_: e.reduce_max(out=o, in_=i, axis=AX),
                     deps, special=True, name=name)

        def vsum(out, in_, deps=(), name="vsum"):
            return V(lambda e, o=out, i=in_: e.reduce_sum(out=o, in_=i, axis=AX),
                     deps, special=True, name=name)

        def scan_mul(out, d0, d1, deps=(), name="scan"):
            return V(lambda e, o=out, a=d0, b=d1:
                     e.tensor_tensor_scan(out=o, data0=a, data1=b, initial=1.0,
                                          op0=ALU.mult, op1=ALU.add),
                     deps, special=True, name=name)

        def memset(buf, val, deps=(), name="memset"):
            return V(lambda e, b=buf, v=val: e.memset(b, v), deps, name=name)

        d_identc = dmaQ(identc[:], identc_d, name="identc")

        def tr(out, in_, deps=(), name="tr"):
            p = in_.shape[0]
            return T(lambda e, o=out, i=in_, q=p:
                     e.transpose(o, i, identc[0:q, 0:q]),
                     list(deps) + [d_identc], name=name)

        dbg_final = []

        def dbg_dump(name, ap, deps, src=()):
            if name not in dbg:
                return
            if not isinstance(ap, bass.AP):
                ap = ap[:]
            p, wdt = ap.shape[0], int(np.prod(ap.shape[1:]))
            dd = nc.dram_tensor("dbg_" + name, [p, wdt], F32,
                                kind="ExternalOutput").ap()
            st_ = dbg_stage[0][0:p, 0:wdt]
            cv = cp_(st_, ap, deps=list(deps) + B.wdeps("dbgstage"),
                     name="dbgc_" + name)
            for n_ in src:
                B.r(cv, n_)
            B.w(cv, "dbgstage")
            dm = dmaQ(dd, st_, deps=[cv], name="dbgd_" + name)
            B.r(dm, "dbgstage")
            dbg_final.append(dm)

        d_masku = dmaQ(masku[:], masku_d, name="masku")
        d_gatebc = dmaQ(gatebc[:], gatebc_d, name="gatebc")
        d_gateTb = dmaG(gateTb[:], gateT_d, name="gateTb")
        d_crowT = dmaQ(crowT[:], crowT_d, name="crowT")
        d_flag = dmaQ(flagc[:], flag_d, name="flag")
        d_poscol = dmaQ(poscol[:], poscol_d, name="poscol")
        d_diota = dmaQ(stat[0:1, 16:32], diota_d, name="diota")
        d_wq = dmaG(wqTb, wqT_d, name="wq")
        d_wk = dmaG(wkTb, wkT_d, name="wk")
        d_wv = dmaG(wvTb, wvT_d, name="wv")
        d_wbm = dmaG(wbmb, wbmT_d, name="wbm")
        d_b2b = dmaG(b2b[:], b2_d, name="b2b")
        d_b0t = dmaQ(b0t[:], b0t_d, name="b0t")
        d_b1t = dmaQ(b1t[:], b1t_d, name="b1t")
        m_on = memset(onesr[:], 1.0, name="onesr")
        m_onb = memset(onesrb[:], 1.0, name="onesrb")
        m_onc = memset(onescb[:], 1.0, name="onescb")
        m_z64 = memset(zer64[:], 0.0, name="z64")
        m_Sf = B.w(memset(Sf, 0.0, name="Sf0"), "Sf")

        a_f = act(stat[0:1, 0:16], stat[0:1, 16:32], AF.Exp,
                  scale=-LOG_ROPE / 16.0, deps=[d_diota], name="frow")
        m_f = mm(ppd[0:128, 0:16], onesr[0:1, 0:128], stat[0:1, 0:16],
                 True, True, deps=[a_f, m_on], name="fbcast")
        B.w(m_f, "ppd")
        e_f = cp_(ftile[:], ppd[0:128, 0:16], deps=[m_f], name="ftile")
        B.r(e_f, "ppd")
        NS = 2 * NCH
        t_ang = None
        for s_ in range(NS):
            t_ang = ts(costab[:, s_ * 16:(s_ + 1) * 16], ftile[:],
                       poscol[:, s_:s_ + 1], ALU.mult,
                       deps=[e_f, d_poscol], name="ang")
        HW = NS * 16 // 2
        halves = [costab[:, 0:HW], costab[:, HW:2 * HW]]
        gsc = [sintab[:, 0:HW], sintab[:, HW:2 * HW]]
        prev = [t_ang, t_ang]
        for k in range(10, -1, -1):
            K = float(2.0 * PI * (2 ** k))
            g0 = ts(gsc[0], halves[0], K, ALU.is_ge, deps=[prev[0]], name="lg0")
            g1 = ts(gsc[1], halves[1], K, ALU.is_ge, deps=[prev[1]], name="lg1")
            s0 = stt(halves[0], gsc[0], -K, halves[0], ALU.mult, ALU.add,
                     deps=[g0], name="ls0")
            s1 = stt(halves[1], gsc[1], -K, halves[1], ALU.mult, ALU.add,
                     deps=[g1], name="ls1")
            prev = [s0, s1]
        g0 = ts(gsc[0], halves[0], PI, ALU.is_ge, deps=[prev[0]], name="sg0")
        g1 = ts(gsc[1], halves[1], PI, ALU.is_ge, deps=[prev[1]], name="sg1")
        s0 = stt(halves[0], gsc[0], -2.0 * PI, halves[0], ALU.mult, ALU.add,
                 deps=[g0], name="ss0")
        s1 = stt(halves[1], gsc[1], -2.0 * PI, halves[1], ALU.mult, ALU.add,
                 deps=[g1], name="ss1")
        ab0 = act(gsc[0], halves[0], AF.Abs, deps=[s0], name="ab0")
        ab1 = act(gsc[1], halves[1], AF.Abs, deps=[s1], name="ab1")
        cg0 = ts(gsc[0], gsc[0], -1.0, ALU.mult, PI / 2.0, ALU.add, deps=[ab0],
                 name="cg0")
        cg1 = ts(gsc[1], gsc[1], -1.0, ALU.mult, PI / 2.0, ALU.add, deps=[ab1],
                 name="cg1")
        o_cos = act(sintab[:], sintab[:], AF.Sin, deps=[cg0, cg1], name="cosv")
        o_sin = act(costab[:], costab[:], AF.Sin, deps=[o_cos, s0, s1], name="sinv")
        COS, SIN = sintab, costab
        tab_deps = [o_cos, o_sin]
        dbg_dump("COSTAB", COS[:], tab_deps)
        dbg_dump("SINTAB", SIN[:], tab_deps)

        a_cs = act(csil[:], crowT[:], AF.Silu, deps=[d_crowT], name="csil")
        wmv = wmod_d.rearrange("(a p) n -> p a n", p=128)
        targets = [sab, ca1b, gab, sbb, cb1b, gbb]
        tnames = ["sab", "ca1b", "gab", "sbb", "cb1b", "gbb"]
        for s_ in range(12):
            vec, sh = s_ // 2, s_ % 2
            wmsl = "wmseg%d" % (s_ % 2)
            dseg = dmaG(wmseg[s_ % 2][:].rearrange("p (a n) -> p a n", a=8),
                        wmv[:, :, s_ * 512:(s_ + 1) * 512],
                        deps=B.wdeps(wmsl), name="wmseg")
            B.w(dseg, wmsl)
            last = None
            for kk in range(8):
                last = mm(ppd[0:1, 0:512], csil[:, kk:kk + 1],
                          wmseg[s_ % 2][:, kk * 512:(kk + 1) * 512],
                          kk == 0, kk == 7,
                          deps=[dseg, a_cs] + (B.wdeps("ppd") if kk == 0 else []),
                          name="modmm")
                B.r(last, wmsl)
            B.w(last, "ppd")
            ev = cp_(rowst[0:1, 0:512], ppd[0:1, 0:512],
                     deps=[last] + B.wdeps("rowst"), name="modev")
            B.r(ev, "ppd")
            B.w(ev, "rowst")
            post = ev
            if vec in (1, 4):
                post = ts(rowst[0:1, 0:512], rowst[0:1, 0:512], 1.0, ALU.add,
                          deps=[ev], name="plus1")
                B.w(post, "rowst")
            elif vec in (2, 5):
                if vec == 2:
                    dbg_dump("ga_raw%d" % sh, rowst[0:1, 0:512], [ev],
                             src=("rowst",))
                t1 = ts(rowst2[0:1, 0:512], rowst[0:1, 0:512], 0.0, ALU.is_ge,
                        deps=[ev] + B.wdeps("rowst2"), name="gsgn")
                B.r(t1, "rowst")
                B.w(t1, "rowst2")
                t2 = ts(rowst2[0:1, 0:512], rowst2[0:1, 0:512], 2.0, ALU.mult,
                        -1.0, ALU.add, deps=[t1], name="gsgn2")
                B.w(t2, "rowst2")
                if vec == 2:
                    dbg_dump("ga_sgn%d" % sh, rowst2[0:1, 0:512], [t2],
                             src=("rowst2",))
                t3a = act(rowst[0:1, 0:512], rowst[0:1, 0:512], AF.Abs,
                          deps=[ev] + B.wdeps("rowst"), name="gabs")
                B.w(t3a, "rowst")
                t3 = ts(rowst[0:1, 0:512], rowst[0:1, 0:512], 0.25, ALU.max,
                        deps=[t3a], name="gclamp")
                B.w(t3, "rowst")
                if vec == 2:
                    dbg_dump("ga_clp%d" % sh, rowst[0:1, 0:512], [t3],
                             src=("rowst",))
                t4 = tt(rowst[0:1, 0:512], rowst[0:1, 0:512], rowst2[0:1, 0:512],
                        ALU.mult, deps=[t3, t2] + B.wdeps("rowst"), name="gmul")
                B.r(t4, "rowst2")
                B.w(t4, "rowst")
                post = t4
            bcm = mm(pp0[:, 0:512], onesr[0:1, 0:128], rowst[0:1, 0:512],
                     True, True, deps=[post, m_on] + B.wdeps("pp0"), name="bcmm")
            B.r(bcm, "rowst")
            B.w(bcm, "pp0")
            bce = cp_(targets[vec][:, sh * 512:(sh + 1) * 512], pp0[:, 0:512],
                      deps=[bcm], name="bc_" + tnames[vec])
            B.r(bce, "pp0")
            B.w(bce, tnames[vec] + str(sh))

        for tgt, tn, src, wdt in ((lngb, "lngb", lng_d, HD),
                                  (lnbb, "lnbb", lnb_d, HD),
                                  (boutb, "boutb", bout_d, HD),
                                  (bbmb, "bbmb", bbm_d, 128)):
            for sh, (p0, wn) in enumerate(segs(wdt)):
                dr = dmaQ(rowst[0:1, 0:wn], src[0:1, p0:p0 + wn],
                          deps=B.wdeps("rowst"), name="row_" + tn)
                B.w(dr, "rowst")
                bcm = mm(pp0[:, 0:wn], onesr[0:1, 0:128], rowst[0:1, 0:wn],
                         True, True, deps=[dr, m_on] + B.wdeps("pp0"),
                         name="bcmm2")
                B.r(bcm, "rowst")
                B.w(bcm, "pp0")
                bce = cp_(tgt[:, p0:p0 + wn], pp0[:, 0:wn], deps=[bcm],
                          name="bce_" + tn)
                B.r(bce, "pp0")
                B.w(bce, tn + str(sh))

        d_wout = dmaG(woutTb, woutT_d,
                      deps=B.wdeps("wmseg0") + B.wdeps("wmseg1"),
                      name="wout")

        ph0_last = L.lasts()
        for nm in ("scr0", "scr1", "scr2", "scr3", "scr4"):
            for o_ in ph0_last:
                B.r(o_, nm)
        bct_deps = [B.W["sab0"], B.W["sab1"], B.W["ca1b0"], B.W["ca1b1"]]
        bct2_deps = [B.W["sbb0"], B.W["sbb1"], B.W["cb1b0"], B.W["cb1b1"],
                     B.W["gab0"], B.W["gab1"], B.W["lngb0"], B.W["lngb1"],
                     B.W["lnbb0"], B.W["lnbb1"],
                     B.W["boutb0"], B.W["boutb1"]]
        bbm_deps = [B.W["bbmb0"]]
        gbb_deps = [B.W["gbb0"], B.W["gbb1"]]

        def ln_apply(src_ap, src_name, out_ap, out_name, c1_tile, s_tile,
                     cols, xdeps, extra_deps=()):
            sname = "stat%d" % cols
            a1 = act(out_ap, src_ap, AF.Copy, accum=stat[:, cols:cols + 1],
                     deps=list(xdeps) + B.wdeps(sname) + B.wdeps(out_name) +
                     list(extra_deps), name="ln_c")
            a2 = act(scr[4], src_ap, AF.Square, accum=stat[:, cols + 1:cols + 2],
                     deps=list(xdeps) + B.wdeps("scr4"), name="ln_sq")
            B.w(a2, "scr4")
            v1 = ts(stat[:, cols + 2:cols + 3], stat[:, cols:cols + 1],
                    1.0 / HD, ALU.mult, deps=[a1], name="ln_mu")
            v2 = ts(stat[:, cols + 3:cols + 4], stat[:, cols + 2:cols + 3],
                    stat[:, cols + 2:cols + 3], ALU.mult, deps=[v1], name="ln_mu2")
            v3 = ts(stat[:, cols + 4:cols + 5], stat[:, cols + 1:cols + 2],
                    1.0 / HD, ALU.mult, deps=[a2], name="ln_ex2")
            v4 = ts(stat[:, cols + 5:cols + 6], stat[:, cols + 4:cols + 5],
                    stat[:, cols + 3:cols + 4], ALU.subtract, LN_EPS, ALU.add,
                    deps=[v2, v3], name="ln_var")
            s3 = act(stat[:, cols + 6:cols + 7], stat[:, cols + 5:cols + 6],
                     AF.Ln, deps=[v4], name="ln_lv")
            s4 = act(stat[:, cols + 6:cols + 7], stat[:, cols + 6:cols + 7],
                     AF.Exp, scale=-0.5, deps=[s3], name="ln_rs")
            v5 = ts(stat[:, cols + 7:cols + 8], stat[:, cols + 2:cols + 3],
                    -1.0, ALU.mult, deps=[v1], name="ln_nm")
            h1a = ts(out_ap, src_ap, stat[:, cols + 7:cols + 8], ALU.add,
                     stat[:, cols + 6:cols + 7], ALU.mult,
                     deps=[v5, s4, a1] + list(xdeps), name="ln_n")
            h1b = tt(out_ap, out_ap, c1_tile, ALU.mult, deps=[h1a], name="ln_m")
            h1c = tt(out_ap, out_ap, s_tile, ALU.add, deps=[h1b], name="ln_a")
            B.w(h1a, sname)
            B.r(h1a, src_name)
            B.r(h1c, src_name)
            B.w(h1c, out_name)
            return h1c

        dbg_pre = [None]

        def rope_elu(psum_src, psum_name, out_f32, out_name, slot, dep_mm):
            TR = lambda ap: ap.transpose([0, 2, 1])
            cosr = COS[:, slot * 16:(slot + 1) * 16].unsqueeze(2) \
                .broadcast_to((128, 16, 32))
            sinr = SIN[:, slot * 16:(slot + 1) * 16].unsqueeze(2) \
                .broadcast_to((128, 16, 32))
            pv = psum_src.rearrange("p (h d) -> p h d", h=32)
            q1, q2 = TR(pv[:, :, 0:16]), TR(pv[:, :, 16:32])
            s0a = TR(scr[0][:, 0:512].rearrange("p (h d) -> p h d", h=32))
            s0b = TR(scr[0][:, 512:1024].rearrange("p (h d) -> p h d", h=32))
            s1a = TR(scr[1][:, 0:512].rearrange("p (h d) -> p h d", h=32))
            s1b = TR(scr[1][:, 512:1024].rearrange("p (h d) -> p h d", h=32))
            ov0 = out_f32.rearrange("p (h d) -> p h d", h=32)
            ov = None
            d0 = B.wdeps("scr0") + B.wdeps("scr1") + tab_deps
            ta = tt(s0a, q1, cosr, ALU.mult, deps=[dep_mm] + d0, name="rp_a")
            tb = tt(s1a, q2, sinr, ALU.mult, deps=[dep_mm], name="rp_b")
            tc = tt(s0b, q1, sinr, ALU.mult, deps=[dep_mm], name="rp_c")
            td = tt(s1b, q2, cosr, ALU.mult, deps=[dep_mm], name="rp_d")
            B.w(td, "scr0")
            B.w(td, "scr1")
            qa = tt(TR(ov0[:, :, 0:16]), s0a, s1a, ALU.subtract,
                    deps=[ta, tb] + B.wdeps(out_name), name="rp_e")
            qb = tt(TR(ov0[:, :, 16:32]), s0b, s1b, ALU.add, deps=[tc, td],
                    name="rp_f")
            B.r(qb, "scr0")
            B.r(qb, "scr1")
            if dbg_pre[0] is not None:
                dbg_dump(dbg_pre[0], out_f32, [qa, qb])
                dbg_pre[0] = None
            mn = ts(scr[4], out_f32, 0.0, ALU.min,
                    deps=[qa, qb] + B.wdeps("scr4"), name="el_m")
            B.w(mn, "scr4")
            ex = act(scr[4], scr[4], AF.Exp, deps=[mn], name="el_e")
            B.w(ex, "scr4")
            ph = stt(out_f32, out_f32, 0.0, scr[4], ALU.max, ALU.add,
                     deps=[ex, qb], name="el_p")
            B.r(ph, "scr4")
            B.w(ph, out_name)
            return ph

        def transp_batch(src_f32, src_name, dst, dst_name, nblk, bdeps, tag,
                         parts=128, src_w=128, dst_targets=None):
            per = 512 // CH
            evs = []
            bb = 0
            while bb * per < nblk:
                trs = []
                for j in range(per):
                    jj = bb * per + j
                    if jj >= nblk:
                        break
                    trs.append(tr(ppt[0:src_w, j * CH:(j + 1) * CH],
                                  src_f32[:, jj * src_w:(jj + 1) * src_w],
                                  deps=list(bdeps) + B.wdeps("ppt"),
                                  name="tr_" + tag))
                    B.r(trs[-1], src_name)
                if dst_targets is None:
                    ev = cp_(dst[0:src_w, bb * per * CH:
                                 (bb * per + len(trs)) * CH],
                             ppt[0:src_w, 0:len(trs) * CH],
                             deps=[trs[-1]], name="trev_" + tag)
                    evl = [ev]
                else:
                    evl = []
                    for j in range(len(trs)):
                        jj = bb * per + j
                        evl.append(cp_(dst_targets[jj],
                                       ppt[0:src_w, j * CH:(j + 1) * CH],
                                       deps=[trs[-1]], name="trev_" + tag))
                B.w(trs[0], "ppt")
                B.R["ppt"] = list(evl)
                evs.extend(evl)
                bb += 1
            B.w(evs[-1], dst_name)
            for e_ in evs:
                pass
            return evs

        def proj(psum_ap, psum_name, wT, wdep, nseg, h1dep, tag, war=()):
            lastseg = None
            for n in range(nseg):
                last = None
                for k in range(8):
                    last = mm(psum_ap[:, n * 512:(n + 1) * 512],
                              h1T[:, k * 128:(k + 1) * 128],
                              wT[:, k * (nseg * 512) + n * 512:
                                 k * (nseg * 512) + (n + 1) * 512],
                              k == 0, k == 7,
                              deps=([h1dep, wdep] +
                                    (B.wdeps(psum_name, *war)
                                     if n == 0 and k == 0 else [])),
                              name="pj_" + tag)
                    B.r(last, "h1T")
                if n == 0:
                    B.w(last, psum_name)
                lastseg = last
            B.W[psum_name] = lastseg
            return lastseg

        CPE_VIEW = cpendb[:].rearrange("p (a m) -> p a m", a=16)

        def emit_chunk(c, stage):
            par = c % 2
            xrb = xr[par]
            xname = "xr%d" % par
            src = xp_d if stage == "A" else x_d
            slot = (NCH + c) if stage == "A" else c
            d_x = dmaQ(xrb, src[c * CH:(c + 1) * CH, :],
                       deps=B.wdeps(xname), name="xin")
            B.w(d_x, xname)

            h1c = ln_apply(xrb, xname, h1tm, "h1tm", ca1b, sab, 32, [d_x],
                           extra_deps=bct_deps)
            if stage == "B" and c == 0:
                dbg_dump("h1_c0", h1tm, [h1c], src=("h1tm",))
            h1evs = transp_batch(h1tm, "h1tm", h1T, "h1T", 8,
                                 [h1c] + B.wdeps("h1T"), "h1")
            h1dep = h1evs[-1]

            lastbm = None
            for k in range(8):
                lastbm = mm(ppd[:, 0:128], h1T[:, k * 128:(k + 1) * 128],
                            wbmb[:, k * 128:(k + 1) * 128], k == 0, k == 7,
                            deps=[h1dep, d_wbm] +
                                 (B.wdeps("ppd") if k == 0 else []),
                            name="pj_bm")
                B.r(lastbm, "h1T")
            B.w(lastbm, "ppd")
            zb = tt(scr[2][:, 0:128], ppd[:, 0:128], bbmb[:, 0:128], ALU.add,
                    deps=[lastbm] + bbm_deps + B.wdeps("scr2"), name="bm_z")
            B.r(zb, "ppd")
            B.w(zb, "scr2")
            en = act(scr[3][:, 0:64], scr[2][:, 0:64], AF.Exp, scale=-1.0,
                     deps=[zb] + B.wdeps("scr3"), name="b_e")
            B.w(en, "scr3")
            d1 = ts(scr[3][:, 0:64], scr[3][:, 0:64], 1.0, ALU.add,
                    deps=[en], name="b_d")
            bv = recip(scr[3][:, 64:128], scr[3][:, 0:64], deps=[d1], name="b_r")
            bc_ = ts(betat[:], scr[3][:, 64:128], BMIN, ALU.max, BMAX, ALU.min,
                     deps=[bv] + B.wdeps("betat"), name="b_c")
            B.w(bc_, "betat")
            if stage == "B" and c == 0:
                dbg_dump("beta_c0", betat, [bc_], src=("betat",))
            mxv = None
            if stage == "B" and "mix" not in skips:
                m0v = scr[2][:, 64:128][:, 0::2]
                m1v = scr[2][:, 64:128][:, 1::2]
                mx = tt(scr[3][:, 0:32], m0v, m1v, ALU.max, deps=[zb, bc_],
                        name="mx_m")
                B.r(mx, "scr2")
                mzs = tt(scr[3][:, 64:128].rearrange("p (h m) -> p h m", h=32),
                         scr[2][:, 64:128].rearrange("p (h m) -> p h m", h=32),
                         scr[3][:, 0:32].unsqueeze(2).broadcast_to((128, 32, 2)),
                         ALU.subtract, deps=[mx], name="mx_s")
                B.r(mzs, "scr2")
                mex = act(scr[3][:, 64:128], scr[3][:, 64:128], AF.Exp,
                          deps=[mzs], name="mx_e")
                msm = tt(scr[3][:, 32:64], scr[3][:, 64:128][:, 0::2],
                         scr[3][:, 64:128][:, 1::2], ALU.add, deps=[mex],
                         name="mx_su")
                mrs = recip(scr[3][:, 0:32], scr[3][:, 32:64], deps=[msm],
                            name="mx_r")
                mxv = tt(mixt[:].rearrange("p (h m) -> p h m", h=32),
                         scr[3][:, 64:128].rearrange("p (h m) -> p h m", h=32),
                         scr[3][:, 0:32].unsqueeze(2).broadcast_to((128, 32, 2)),
                         ALU.mult, deps=[mrs, mex] + B.wdeps("mixt"), name="mx_v")
                B.w(mxv, "mixt")
                B.w(mxv, "scr3")
                if c == 0:
                    dbg_dump("mix_c0", mixt, [mxv], src=("mixt",))

            bT = tr(ppd[0:64, 384:512], betat[:, 0:64],
                    deps=[bc_] + B.wdeps("ppd"), name="bT")
            B.w(bT, "ppd")
            bTe = cp_(betaTs[:], ppd[0:64, 384:512],
                      deps=[bT] + B.wdeps("betaTs"), name="bTe")
            B.r(bTe, "ppd")
            B.w(bTe, "betaTs")
            cpv = scan_mul(cphm[:], betaTs[:], zer64[:],
                           deps=[bTe, m_z64] + B.wdeps("cphm"), name="cp")
            B.r(cpv, "betaTs")
            B.w(cpv, "cphm")
            icv = recip(icphm[:], cphm[:], deps=[cpv] + B.wdeps("icphm"),
                        name="icp")
            B.r(icv, "cphm")
            B.w(icv, "icphm")
            if stage == "B" and c == 0:
                dbg_dump("cp_c0", cphm, [cpv], src=("cphm",))
            iT = tr(ppd[:, 192:256], icphm[:],
                    deps=[icv] + B.wdeps("ppd"), name="icpT")
            B.r(iT, "icphm")
            B.w(iT, "ppd")
            iTe = cp_(icpT[:], ppd[:, 192:256], deps=[iT] + B.wdeps("icpT"),
                      name="icpTe")
            B.w(iTe, "icpT")
            iTb = cp_(icpTb[:], ppd[:, 192:256], deps=[iT] + B.wdeps("icpTb"),
                      name="icpTb")
            B.r(iTe, "ppd")
            B.r(iTb, "ppd")
            B.w(iTb, "icpTb")
            ceT = tr(ppd[0:1, 256:320], cphm[:, CH - 1:CH],
                     deps=[cpv] + B.wdeps("ppd"), name="cpeT")
            B.r(ceT, "cphm")
            B.w(ceT, "ppd")
            ceE = cp_(cperow[0:1, 0:HM], ppd[0:1, 256:320],
                      deps=[ceT] + B.wdeps("cperow"), name="cpeE")
            B.r(ceE, "ppd")
            B.w(ceE, "cperow")
            cpev = cperow[0:1, 0:HM].rearrange("p (a b m) -> p a b m", a=16, b=2)
            cr0 = cp_(rowst2[0:1, 0:32], cpev[:, :, 0, :],
                      deps=[ceE] + B.wdeps("rowst2"), name="cpe0")
            B.r(cr0, "cperow")
            B.w(cr0, "rowst2")
            cr1 = cp_(rowst2[0:1, 32:64], cpev[:, :, 1, :], deps=[ceE, cr0],
                      name="cpe1")
            B.r(cr1, "cperow")
            B.W["rowst2"] = cr1
            cbm0 = mm(ppd[0:32, 320:352], onesr[0:1, 0:32],
                      rowst2[0:1, 0:32], True, True,
                      deps=[cr0, m_on] + B.wdeps("ppd"), name="cpbm0")
            B.r(cbm0, "rowst2")
            B.w(cbm0, "ppd")
            cbe0 = cp_(cpendb[0:32, :], ppd[0:32, 320:352],
                       deps=[cbm0] + B.wdeps("cpendb"), name="cpbe0")
            B.r(cbe0, "ppd")
            cbm1 = mm(ppd[32:64, 320:352], onesr[0:1, 0:32],
                      rowst2[0:1, 32:64], True, True, tp=(0, 32),
                      deps=[cr1, cbe0], name="cpbm1")
            B.r(cbm1, "rowst2")
            B.w(cbm1, "ppd")
            cbe = cp_(cpendb[32:64, :], ppd[32:64, 320:352],
                      deps=[cbm1], name="cpbe1")
            B.r(cbe, "ppd")
            B.w(cbe, "cpendb")
            B.R["cpendb"] = [cbe0, cbe]
            cbe = cbe0 and cbe

            fq_evs = None
            if stage == "B" and "q" not in skips:
                qmm = proj(pp0, "pp0", wqTb, d_wq, 2, h1dep, "q")
                ph_q = rope_elu(pp0[:, 0:1024], "pp0", qtm, "qtm", slot, qmm)
                B.r(ph_q, "pp0")
                if c == 0:
                    dbg_dump("phiq_c0", qtm, [ph_q], src=("qtm",))
                fq_evs = transp_batch(qtm, "qtm", fqfm, "fqfm", 16,
                                      [ph_q] + B.wdeps("fqfm"), "fq",
                                      src_w=64)

            kmm = proj(pp1, "pp1", wkTb, d_wk, 2, h1dep, "k", war=("pA0", "pA1"))
            if stage == "A" and c == 0:
                dbg_pre[0] = "A_ropek_c0"
            ph_k = rope_elu(pp1[:, 0:1024], "pp1", ktm, "h1tm", slot, kmm)
            B.r(ph_k, "pp1")
            if stage == "A" and c == 0:
                dbg_dump("A_phik_c0", ktm, [ph_k], src=("h1tm",))
            fk_evs = transp_batch(ktm, "h1tm", fkfm, "fkfm", 16,
                                  [ph_k] + B.wdeps("fkfm"), "fk", src_w=64)

            ktops = []
            for m in range(2):
                kv = tt(kt[m][:].rearrange("p (h d) -> p h d", h=32),
                        ktm[:].rearrange("p (h d) -> p h d", h=32),
                        icpT[:, m::2].unsqueeze(2).broadcast_to((128, 32, 32)),
                        ALU.mult, deps=[ph_k, iTe] + B.wdeps("kt%d" % m),
                        name="ktb%d" % m)
                B.r(kv, "h1tm")
                B.r(kv, "icpT")
                B.w(kv, "kt%d" % m)
                ktops.append(kv)

            vmm = proj(pp2, "pp2", wvTb, d_wv, 2, h1dep, "v", war=("pN0", "pN1", "st_pp2"))
            vc = cp_(vtm[:], pp2[:, 0:1024], deps=[vmm] + B.wdeps("vtm"),
                     name="vtm")
            B.r(vc, "pp2")
            B.w(vc, "vtm")
            vtops = []
            if stage == "B" and "vt" not in skips:
                for m in range(2):
                    vv = tt(vt[m][:].rearrange("p (h d) -> p h d", h=32),
                            vtm[:].rearrange("p (h d) -> p h d", h=32),
                            icpT[:, m::2].unsqueeze(2).broadcast_to((128, 32, 32)),
                            ALU.mult, deps=[vc, iTe] + B.wdeps("vt%d" % m),
                            name="vtb%d" % m)
                    B.r(vv, "vtm")
                    B.r(vv, "icpT")
                    B.w(vv, "vt%d" % m)
                    vtops.append(vv)

            yops = []
            if stage == "B" and "scan" in skips:
                yops = [memset(ytm[:], 0.1, deps=B.wdeps("qtm"), name="yfake")]
                B.w(yops[0], "qtm")
            if stage == "B" and "scan" not in skips:
                nq_evs = {}
                den_last = None
                for g in range(8):
                    ga_ = g % 2
                    heads = [2 * j + ga_ for j in range(4 * (g // 2),
                                                        4 * (g // 2) + 4)]
                    pA = pp1[:, ga_ * 512:(ga_ + 1) * 512]
                    pAn = "pA%d" % ga_
                    amms = []
                    for hh in range(4):
                        h = heads[hh]
                        pr, hb = h // 2, h % 2
                        dps = [fk_evs[pr // 4], fq_evs[pr // 4]]
                        if hh == 0:
                            dps += B.wdeps(pAn)
                            if g == 0:
                                dps += B.wdeps("pp1")
                        amms.append(mm(
                            pA[:, hh * 128:(hh + 1) * 128],
                            fkfm[32 * hb:32 * hb + 32, pr * 128:(pr + 1) * 128],
                            fqfm[32 * hb:32 * hb + 32, pr * 128:(pr + 1) * 128],
                            hh == 0, hh == 3, tp=(32 * hb, 0), deps=dps,
                            name="Amm"))
                        B.r(amms[-1], "fkfm")
                        B.r(amms[-1], "fqfm")
                    B.w(amms[0], pAn)
                    B.W[pAn] = amms[-1]
                    mev = tt(Ab[ga_][:].rearrange("p (h t) -> p h t", h=4)
                             .transpose([0, 2, 1]),
                             pA[:].rearrange("p (h t) -> p h t", h=4)
                             .transpose([0, 2, 1]),
                             masku[:].unsqueeze(2).broadcast_to((128, CH, 4)),
                             ALU.mult,
                             deps=[amms[-1], d_masku] + B.wdeps("Ab%d" % ga_),
                             name="Aev")
                    B.r(mev, pAn)
                    B.w(mev, "Ab%d" % ga_)
                    qt = g // 2
                    pN = pp2[:, (qt % 2) * 512:((qt % 2) + 1) * 512]
                    pNn = "pN%d" % (qt % 2)
                    if "nd" in skips:
                        continue
                    for hh in range(4):
                        h = heads[hh]
                        pr, hb, ha = h // 2, h % 2, h // 2
                        Abl = Ab[ga_][:, hh * 128:(hh + 1) * 128]
                        fql = fqfm[32 * hb:32 * hb + 32, pr * 128:(pr + 1) * 128]
                        for m in range(2):
                            hm = 2 * h + m
                            ncol = (hm % 16) * 32
                            scol = (ha * 2 + m) * 33
                            dps = [mev, vtops[m], iTb]
                            if hh == 0 and m == 0 and g % 2 == 0:
                                dps += B.wdeps(pNn)
                                if g < 2:
                                    dps += B.wdeps("pp2")
                            n1 = mm(pN[:, ncol:ncol + 32], Abl,
                                    vt[m][:, h * 32:(h + 1) * 32],
                                    hh == 0 and m == 0 and g % 2 == 0, False,
                                    deps=dps, name="nd_av")
                            B.r(n1, "vt%d" % m)
                            B.r(n1, "Ab%d" % ga_)
                            ddps = [mev, iTb]
                            if hm == 0:
                                ddps += B.wdeps("ppd")
                            d1_ = mm(ppd[:, 128 + hm:129 + hm], Abl,
                                     icpTb[:, hm:hm + 1], hm == 0, False,
                                     deps=ddps, name="nd_ai")
                            B.r(d1_, "icpTb")
                            B.r(d1_, "Ab%d" % ga_)
                            n2 = mm(pN[:, ncol:ncol + 32], fql,
                                    Sb[32 * hb:32 * hb + 32, scol:scol + 32],
                                    False,
                                    g % 2 == 1 and hh == 3 and m == 1,
                                    deps=[B.W.get("Sb")],
                                    name="nd_qs")
                            d2_ = mm(ppd[:, 128 + hm:129 + hm], fql,
                                     Sb[32 * hb:32 * hb + 32,
                                        scol + 32:scol + 33],
                                     False, hm == 63, deps=[], name="nd_qk")
                            B.r(n2, "Sb")
                            B.r(d2_, "Sb")
                            B.r(n2, "fqfm")
                            B.r(d2_, "fqfm")
                            if hm == 0:
                                B.w(d1_, "ppd")
                            if hh == 0 and m == 0 and g % 2 == 0:
                                B.w(n1, pNn)
                            B.W[pNn] = n2
                            den_last = d2_
                    B.W["ppd"] = den_last
                    if g % 2 == 1:
                        dst = scr[2] if qt < 2 else scr[3]
                        dstn = "scr2" if qt < 2 else "scr3"
                        dps = [B.W[pNn]]
                        if qt % 2 == 0:
                            dps += B.wdeps(dstn)
                        ev = cp_(dst[:, (qt % 2) * 512:((qt % 2) + 1) * 512],
                                 pN[:], deps=dps, name="ndev")
                        B.r(ev, pNn)
                        if qt % 2 == 0:
                            B.w(ev, dstn)
                        else:
                            B.W[dstn] = ev
                        nq_evs[qt] = ev
                nq_done = [nq_evs[q] for q in range(4)] if "nd" not in skips else []
                if "nd" in skips or "y" in skips:
                    yops = [memset(ytm[:], 0.1, deps=B.wdeps("qtm") +
                                   (B.wdeps("pN0") + B.wdeps("pN1")), name="yfake")]
                    B.w(yops[0], "qtm")
                if "nd" not in skips and "y" not in skips:
                    dv = stt(denf[:], icpT[:], EPS, ppd[:, 128:192],
                             ALU.mult, ALU.add,
                             deps=[den_last, iTe] + B.wdeps("denf"), name="den")
                    B.r(dv, "ppd")
                    B.r(dv, "icpT")
                    B.w(dv, "denf")
                    drc = recip(mden[:], denf[:], deps=[dv] + B.wdeps("mden"),
                                name="denr")
                    B.r(drc, "denf")
                    md = tt(mden[:], mixt[:], mden[:], ALU.mult, deps=[drc, mxv],
                            name="mden")
                    B.r(md, "mixt")
                    B.w(md, "mden")
                    mdv = mden[:].rearrange("p (h m) -> p h m", h=32)
                    for half in range(2):
                        srcv = (scr[2] if half == 0 else scr[3])[:] \
                            .rearrange("p (h m c) -> p h m c", h=16, m=2)
                        y_sl = ytm[:, half * 512:(half + 1) * 512] \
                            .rearrange("p (h c) -> p h c", h=16)
                        tmp = scr[4][:, 0:512].rearrange("p (h c) -> p h c", h=16)
                        y1 = tt(y_sl, srcv[:, :, 0, :],
                                mdv[:, half * 16:(half + 1) * 16, 0:1]
                                .broadcast_to((128, 16, 32)), ALU.mult,
                                deps=[md, nq_done[2 * half], nq_done[2 * half + 1]] +
                                (B.wdeps("qtm") if half == 0 else []), name="y1")
                        y2 = tt(tmp, srcv[:, :, 1, :],
                                mdv[:, half * 16:(half + 1) * 16, 1:2]
                                .broadcast_to((128, 16, 32)), ALU.mult,
                                deps=[md, nq_done[2 * half], nq_done[2 * half + 1]] +
                                B.wdeps("scr4"), name="y2")
                        B.w(y2, "scr4")
                        y3 = tt(y_sl, y_sl, tmp, ALU.add, deps=[y1, y2], name="y3")
                        B.r(y3, "scr4")
                        B.r(y3, "scr2" if half == 0 else "scr3")
                        B.r(y3, "mden")
                        yops.append(y3)
                    B.w(yops[0], "qtm")
                    B.W["qtm"] = yops[-1]
                    if c == 0:
                        dbg_dump("y_c0", ytm, yops, src=("qtm",))

            last_sb = None
            for par in range(2):
                smms = []
                kmms = []
                for ha in range(16):
                    h = 2 * ha + par
                    for m in range(2):
                        col = (ha * 2 + m) * 32
                        dps = [ktops[m], vc]
                        if ha == 0 and m == 0:
                            dps += (B.wdeps("pN0") + B.wdeps("pN1") +
                                    B.wdeps("st_pp2") + B.wdeps("pp2"))
                        smms.append(mm(pp2[32 * par:32 * par + 32,
                                           col:col + 32],
                                       kt[m][:, h * 32:(h + 1) * 32],
                                       vtm[:, h * 32:(h + 1) * 32],
                                       (ha in (0, 8) and m == 0),
                                       (ha in (7, 15) and m == 1),
                                       tp=(0, 32 * par) if par else None,
                                       deps=dps, name="st_s"))
                        B.r(smms[-1], "kt%d" % m)
                        B.r(smms[-1], "vtm")
                        kdps = [ktops[m], m_onc]
                        if ha == 0 and m == 0:
                            kdps += B.wdeps("ppd")
                        kmms.append(mm(ppd[32 * par:32 * par + 32,
                                           224 + (ha * 2 + m):
                                           225 + (ha * 2 + m)],
                                       kt[m][:, h * 32:(h + 1) * 32],
                                       onescb[:], (ha == 0 and m == 0),
                                       (ha == 15 and m == 1),
                                       tp=(0, 32 * par) if par else None,
                                       deps=kdps, name="st_k"))
                        B.r(kmms[-1], "kt%d" % m)
                B.w(smms[0], "st_pp2")
                B.W["st_pp2"] = smms[-1]
                B.w(kmms[0], "ppd")
                B.W["ppd"] = kmms[-1]
                rsl = slice(32 * par, 32 * par + 32)
                Sv = Sf[rsl.start - 32 * par if False else 0:32, :] if False \
                    else None
                Sview = Sf[rsl, :].rearrange("p (a m c) -> p a m c", a=16, m=2) \
                    if par else Sf[0:32, :].rearrange("p (a m c) -> p a m c",
                                                      a=16, m=2)
                SS = Sview[:, :, :, 0:32]
                SK = Sview[:, :, :, 32:33]
                pSv = pp2[rsl, 0:1024].rearrange("p (a m c) -> p a m c",
                                                 a=16, m=2)
                sc1v = scr[1][rsl, 0:1024].rearrange("p (a m c) -> p a m c",
                                                     a=16, m=2)
                cpeview = cpendb[rsl, :].rearrange("p (a m) -> p a m", a=16)
                u1 = tt(sc1v, pSv, SS, ALU.add,
                        deps=[smms[-1], B.W.get("Sf")] + B.wdeps("scr1"),
                        name="su_add")
                B.r(u1, "st_pp2")
                B.r(u1, "Sf")
                B.w(u1, "scr1")
                u2 = tt(SS, sc1v,
                        cpeview.unsqueeze(3).broadcast_to((32, 16, 2, 32)),
                        ALU.mult, deps=[u1, cbe] + B.wdeps("Sf"),
                        name="su_mul")
                B.r(u2, "scr1")
                B.r(u2, "cpendb")
                uk1 = tt(kupd[rsl, 0:32].rearrange("p (a m) -> p a m", a=16),
                         ppd[rsl, 224:256].rearrange("p (a m) -> p a m", a=16),
                         SK.squeeze(3), ALU.add,
                         deps=[kmms[-1], B.W.get("Sf")] + B.wdeps("kupd"),
                         name="ku_add")
                B.r(uk1, "ppd")
                B.r(uk1, "Sf")
                B.w(uk1, "kupd")
                uk2 = tt(SK.squeeze(3),
                         kupd[rsl, 0:32].rearrange("p (a m) -> p a m", a=16),
                         cpeview, ALU.mult, deps=[uk1, cbe, u2], name="ku_mul")
                B.r(uk2, "kupd")
                B.r(uk2, "cpendb")
                B.w(uk2, "Sf")
                B.W["Sf"] = uk2
            sb_ = cp_(Sb, Sf, deps=[B.W["Sf"]] + B.wdeps("Sb"), name="Sbc")
            B.r(sb_, "Sf")
            B.w(sb_, "Sb")

            if stage == "A":
                if c == 0:
                    dbg_dump("A_kt0_c0", kt[0], [ktops[0]], src=("kt0",))
                    dbg_dump("A_vtm_c0", vtm, [vc], src=("vtm",))
                    dbg_dump("A_cp_c0", cphm, [cpv], src=("cphm",))
                    dbg_dump("A_S_c0", Sf, [B.W["Sf"]], src=("Sf",))
                    dbg_dump("A_cpe_c0", cpendb, [cbe], src=("cpendb",))
                return

            ytevs = transp_batch(ytm, "qtm", yfm, "yfm", 8,
                                 yops + B.wdeps("yfm"), "yT")
            oplast = None
            for n in range(2):
                for k in range(8):
                    dps = [ytevs[-1], d_wout]
                    if n == 0 and k == 0:
                        dps += B.wdeps("pp0")
                    oplast = mm(pp0[:, n * 512:(n + 1) * 512],
                                yfm[:, k * 128:(k + 1) * 128],
                                woutTb[:, k * 1024 + n * 512:
                                       k * 1024 + (n + 1) * 512],
                                k == 0, k == 7, deps=dps, name="oproj")
                    B.r(oplast, "yfm")
            B.w(oplast, "pp0")
            atno = tt(scr[1], pp0[:, 0:1024], boutb[:],
                      ALU.add, deps=[oplast] + bct2_deps + B.wdeps("scr1"),
                      name="atno")
            B.r(atno, "pp0")
            B.w(atno, "scr1")
            if c == 0:
                dbg_dump("atno_c0", scr[1], [atno], src=("scr1",))
            h1r = ln_apply(xrb, xname, scr[0], "scr0", ca1b, sab, 40, [d_x],
                           extra_deps=bct_deps + [atno])
            if c == 0:
                dbg_dump("h1z_c0", scr[0], [h1r], src=("scr0",))
            zz = tt(scr[1], scr[1], scr[0], ALU.add, deps=[h1r, atno],
                    name="z")
            B.r(zz, "scr0")
            B.w(zz, "scr1")
            if c == 0:
                dbg_dump("z_c0", scr[1], [zz], src=("scr1",))
            a1 = act(scr[4], scr[1], AF.Copy, accum=stat[:, 48:49],
                     deps=[zz] + B.wdeps("scr4") + B.wdeps("stat48"),
                     name="z_c")
            B.w(a1, "scr4")
            a2 = act(scr[4], scr[1], AF.Square, accum=stat[:, 49:50],
                     deps=[zz, a1], name="z_sq")
            B.w(a2, "scr4")
            v1 = ts(stat[:, 50:51], stat[:, 48:49], 1.0 / HD, ALU.mult,
                    deps=[a1], name="z_mu")
            v2 = ts(stat[:, 51:52], stat[:, 50:51], stat[:, 50:51], ALU.mult,
                    deps=[v1], name="z_mu2")
            v3 = ts(stat[:, 52:53], stat[:, 49:50], 1.0 / HD, ALU.mult,
                    deps=[a2], name="z_ex2")
            v4 = ts(stat[:, 53:54], stat[:, 52:53], stat[:, 51:52],
                    ALU.subtract, LN_EPS, ALU.add, deps=[v2, v3], name="z_var")
            s3 = act(stat[:, 54:55], stat[:, 53:54], AF.Ln, deps=[v4],
                     name="z_lv")
            s4 = act(stat[:, 54:55], stat[:, 54:55], AF.Exp, scale=-0.5,
                     deps=[s3], name="z_rs")
            v5 = ts(stat[:, 55:56], stat[:, 50:51], -1.0, ALU.mult, deps=[v1],
                    name="z_nm")
            zc = ts(scr[1], scr[1], stat[:, 55:56], ALU.add, stat[:, 54:55],
                    ALU.mult, deps=[v5, s4], name="z_n")
            B.w(zc, "scr1")
            B.w(zc, "stat48")
            if c == 0:
                dbg_dump("zn_c0", scr[1], [zc], src=("scr1",))
                dbg_dump("zstat_c0", stat[:, 48:56], [zc], src=("stat48",))
            af1 = tt(scr[1], scr[1], lngb[:], ALU.mult, deps=[zc] + bct2_deps,
                     name="af1")
            af2 = tt(scr[1], scr[1], lnbb[:], ALU.add, deps=[af1], name="af2")
            if c == 0:
                dbg_dump("af_c0", scr[1], [af2], src=("scr1",))
                dbg_dump("gab_t", gab[:], [af2] + bct2_deps)
                dbg_dump("lngb_t", lngb[:], [af2] + bct2_deps)
                dbg_dump("lnbb_t", lnbb[:], [af2] + bct2_deps)
            xg = tt(scr[1], scr[1], gab[:], ALU.mult, deps=[af2], name="xga")
            if c == 0:
                dbg_dump("xga_c0", scr[1], [xg], src=("scr1",))
            xm = tt(scr[2], scr[1], xrb, ALU.add,
                    deps=[xg, d_x] + B.wdeps("scr2"), name="xm")
            B.r(xm, xname)
            B.r(xm, "scr1")
            B.w(xm, "scr2")
            if c == 0:
                dbg_dump("xmid_c0", scr[2], [xm], src=("scr2",))
            d_xm = dmaQ(xmid_d[c * CH:(c + 1) * CH, :], scr[2], deps=[xm],
                        name="xmout")
            B.r(d_xm, "scr2")
            h2c = ln_apply(scr[2], "scr2", scr[0], "scr0", cb1b, sbb, 32, [xm],
                           extra_deps=bct2_deps)
            if c == 0:
                dbg_dump("h2_c0", scr[0], [h2c], src=("scr0",))
            h2tgts = [h2T[:, j * tpc + c * CH:j * tpc + (c + 1) * CH]
                      for j in range(8)]
            h2evs = transp_batch(scr[0], "scr0", h2T, "h2T_c%d" % c, 8,
                                 [h2c], "h2", dst_targets=h2tgts)
            glast = None
            for k in range(8):
                dps = [h2evs[k], d_gateTb]
                if k == 0:
                    dps += B.wdeps("ppd")
                glast = mm(ppd[0:8, 0:128], gateTb[:, k * 8:(k + 1) * 8],
                           h2T[:, k * tpc + c * CH:k * tpc + (c + 1) * CH],
                           k == 0, k == 7, deps=dps, name="gmm")
            B.w(glast, "ppd")
            gev = ts(lgT[0:8, 0:128], ppd[0:8, 0:128], gatebc[:], ALU.add,
                     deps=[glast, d_gatebc] + B.wdeps("lgT"), name="gev")
            B.r(gev, "ppd")
            B.w(gev, "lgT")
            gtr = tr(ppd[:, 128:136], lgT[0:8, 0:128],
                     deps=[gev] + B.wdeps("ppd"), name="gtr")
            B.w(gtr, "ppd")
            gte = cp_(lgtm[:], ppd[:, 128:136], deps=[gtr] + B.wdeps("lgtm"),
                      name="gte")
            B.r(gte, "ppd")
            B.w(gte, "lgtm")
            if c == 0:
                dbg_dump("logit_c0", lgtm, [gte], src=("lgtm",))
            t_m1 = vmax(stat[:, 56:57], lgtm[:],
                        deps=[gte] + B.wdeps("stat56"), name="g_m1")
            t_ge = ts(g8a[:], lgtm[:], stat[:, 56:57], ALU.is_ge,
                      deps=[t_m1] + B.wdeps("g8a"), name="g_ge")
            B.w(t_ge, "g8a")
            t_mk = stt(g8b[:], g8a[:], -1e30, lgtm[:], ALU.mult, ALU.add,
                       deps=[t_ge] + B.wdeps("g8b"), name="g_mk")
            B.w(t_mk, "g8b")
            t_m2 = vmax(stat[:, 57:58], g8b[:], deps=[t_mk], name="g_m2")
            t_sel = ts(g8a[:], lgtm[:], stat[:, 57:58], ALU.is_ge,
                       deps=[t_m2, t_ge], name="g_sel")
            B.w(t_sel, "g8a")
            t_nm = ts(stat[:, 58:59], stat[:, 56:57], -1.0, ALU.mult,
                      deps=[t_m1], name="g_nm")
            t_e = act(g8b[:], lgtm[:], AF.Exp, bias=stat[:, 58:59],
                      deps=[t_nm, t_mk, t_m2], name="g_e")
            B.w(t_e, "g8b")
            t_wr = tt(g8b[:], g8b[:], g8a[:], ALU.mult, deps=[t_e, t_sel],
                      name="g_wr")
            B.r(t_wr, "g8a")
            B.w(t_wr, "g8b")
            t_ss = vsum(stat[:, 59:60], g8b[:], deps=[t_wr], name="g_ss")
            t_rs = recip(stat[:, 60:61], stat[:, 59:60], deps=[t_ss],
                         name="g_rs")
            t_w = ts(wgate[:, c * 8:(c + 1) * 8], g8b[:], stat[:, 60:61],
                     ALU.mult, deps=[t_rs] + B.wdeps("stat56"), name="g_w")
            B.r(t_w, "g8b")
            B.w(t_w, "stat56")
            if c == 0:
                dbg_dump("wg_c0", wgate[:, 0:8], [t_w])
            wtr = tr(ppd[0:8, 136:264], wgate[:, c * 8:(c + 1) * 8],
                     deps=[t_w] + B.wdeps("ppd") + B.wdeps("ppd") +
                     B.wdeps("ppd"), name="wtr")
            B.w(wtr, "ppd")
            wte = cp_(wTg[0:8, c * CH:(c + 1) * CH], ppd[0:8, 136:264],
                      deps=[wtr], name="wte")
            B.r(wte, "ppd")
            B.r(wte, "ppd")
            B.r(wte, "ppd")
            B.w(wte, "wTg_c%d" % c)

        if stop_after == "phase0":
            L.op("Q", lambda e: e.nop(), deps=L.lasts(), name="final_wait")
            L.assign()
            with nc.Block() as block:
                block.sync(L.body("Q", esems, dsems))
                block.gpsimd(L.body("G", esems, dsems))
                block.tensor(L.body("T", esems, dsems))
                block.vector(L.body("V", esems, dsems))
                block.scalar(L.body("S", esems, dsems))
            return nc
        for c in range(NCH):
            emit_chunk(c, "A")
            if stop_after == "chunkA0" and c == 0:
                break
        fl = ts(Sf, Sf, flagc[:], ALU.mult,
                deps=[d_flag] + B.wdeps("Sf"), name="flagS")
        B.r(fl, "Sf")
        B.w(fl, "Sf")
        sb0 = cp_(Sb, Sf, deps=[fl] + B.wdeps("Sb"), name="Sb0")
        B.r(sb0, "Sf")
        B.w(sb0, "Sb")
        dbg_dump("S_in", Sf, [fl])

        if stop_after in ("stageA", "chunkA0"):
            L.op("Q", lambda e: e.nop(), deps=L.lasts(), name="final_wait")
            L.assign()
            with nc.Block() as block:
                block.sync(L.body("Q", esems, dsems))
                block.gpsimd(L.body("G", esems, dsems))
                block.tensor(L.body("T", esems, dsems))
                block.vector(L.body("V", esems, dsems))
                block.scalar(L.body("S", esems, dsems))
            return nc
        for c in range(NCH):
            emit_chunk(c, "B")
            if stop_after == "chunkB0" and c == 0:
                break

        if stop_after in ("stageB", "chunkB0"):
            L.op("Q", lambda e: e.nop(), deps=L.lasts(), name="final_wait")
            L.assign()
            with nc.Block() as block:
                block.sync(L.body("Q", esems, dsems))
                block.gpsimd(L.body("G", esems, dsems))
                block.tensor(L.body("T", esems, dsems))
                block.vector(L.body("V", esems, dsems))
                block.scalar(L.body("S", esems, dsems))
            return nc
        stB.close()
        w0r = [tk("w0r%d" % i, 128, 8 * 256, BF16, stk=stack) for i in range(3)]
        w1r = [tk("w1r%d" % i, 128, 8 * 256, BF16, stk=stack) for i in range(3)]
        w2b = tk("w2b", 128, 16 * HD, BF16, stk=stack)
        actb = tk("actb", 128, 16 * THALF, BF16, stk=stack)
        outacc = tk("outacc", 128, 8 * THALF, F32, stk=stack)
        wbc = [tk("wbc%d" % i, 128, THALF, F32, stk=stack) for i in range(2)]
        silscr = [tk("silscr%d" % i, 128, THALF, F32, stk=stack)
                  for i in range(2)]
        xmr = [tk("xmr%d" % i, 128, HD, F32, stk=stack) for i in range(2)]
        otile = [tk("otile%d" % i, 128, HD, F32, stk=stack) for i in range(2)]
        wrow = [tk("wrow%d" % i, 1, THALF, BF16, stk=stack) for i in range(2)]

        bars = L.lasts()
        first_deps = list(bars)
        wTg_deps = [B.W["wTg_c%d" % c] for c in range(NCH)]
        h2T_deps = [B.W["h2T_c%d" % c] for c in range(NCH)]
        SEGS = segs(THALF)
        all_douts = []

        for hf in range(2):
            oz = memset(outacc, 0.0,
                        deps=B.wdeps(*["oacc%d" % m_ for m_ in range(8)]) +
                        first_deps, name="oz")
            B.w(oz, *["oacc%d" % m_ for m_ in range(8)])
            for e in range(FE):
                ep = e % 2
                dw2 = []
                for q4 in range(4):
                    dw2.append(dmaG(w2b[:, q4 * 4096:(q4 + 1) * 4096],
                                    w2_d[e][:, q4 * 4096:(q4 + 1) * 4096],
                                    deps=B.wdeps("w2b") + first_deps,
                                    name="w2l"))
                B.w(dw2[-1], "w2b")
                dwr = dmaQ(wrow[ep][0:1, 0:THALF],
                           wTg[e:e + 1, hf * THALF:(hf + 1) * THALF],
                           deps=B.wdeps("wrow%d" % ep) + wTg_deps + first_deps,
                           name="wrl")
                B.w(dwr, "wrow%d" % ep)
                wbm_last = None
                for n, (p0_, wn) in enumerate(SEGS):
                    wbm_last = mm(pp2[:, p0_:p0_ + wn], onesrb[0:1, 0:128],
                                  wrow[ep][0:1, p0_:p0_ + wn], True, True,
                                  deps=[dwr, m_onb] +
                                  ((B.wdeps("pp2f") + first_deps)
                                   if n == 0 else []),
                                  name="wbm")
                    if n == 0:
                        B.w(wbm_last, "pp2f")
                B.W["pp2f"] = wbm_last
                B.r(wbm_last, "wrow%d" % ep)
                wbe = cp_(wbc[ep][:, 0:THALF], pp2[:, 0:THALF],
                          deps=[wbm_last] + B.wdeps("wbc%d" % ep), name="wbe")
                B.r(wbe, "pp2f")
                B.w(wbe, "wbc%d" % ep)

                def load_slot(gg, e=e):
                    sl = gg % 3
                    d0 = dmaG(w0r[sl][:].rearrange("p (a n) -> p a n", a=8),
                              w0_d[e].rearrange("p (a h) -> p a h", a=8)
                              [:, :, gg * 256:(gg + 1) * 256],
                              deps=B.wdeps("w0r%d" % sl) + first_deps,
                              name="w0l")
                    B.w(d0, "w0r%d" % sl)
                    d1_ = dmaG(w1r[sl][:].rearrange("p (a n) -> p a n", a=8),
                               w1_d[e].rearrange("p (a h) -> p a h", a=8)
                               [:, :, gg * 256:(gg + 1) * 256],
                               deps=B.wdeps("w1r%d" % sl) + first_deps,
                               name="w1l")
                    B.w(d1_, "w1r%d" % sl)
                    return d0, d1_

                slot_dma = {gg: load_slot(gg) for gg in range(3)}
                for j in range(16):
                    gg, jj = j // 2, j % 2
                    sl = gg % 3
                    jp = j % 2
                    d0, d1_ = slot_dma[gg]
                    ph0 = pp0 if jp == 0 else pp1
                    ph0n = "pf0_%d" % jp
                    h0last = None
                    for n, (p0_, wn) in enumerate(SEGS):
                        for k in range(8):
                            dps = [d0] + h2T_deps
                            if n == 0 and k == 0:
                                dps += B.wdeps(ph0n)
                            h0last = mm(ph0[:, p0_:p0_ + wn],
                                        w0r[sl][:, k * 256 + jj * 128:
                                                k * 256 + jj * 128 + 128],
                                        h2T[:, k * tpc + hf * THALF + p0_:
                                            k * tpc + hf * THALF + p0_ + wn],
                                        k == 0, k == 7, deps=dps, name="h0mm")
                            B.r(h0last, "w0r%d" % sl)
                    B.w(h0last, ph0n)
                    B.W[ph0n] = h0last
                    if jp == 0:
                        h1tgts = [(pp2, p0_, wn) for (p0_, wn) in SEGS]
                        h1n = "pp2f"
                    else:
                        tg = [ppd, ppt]
                        h1tgts = [(tg[n], 0, wn)
                                  for n, (p0_, wn) in enumerate(SEGS)]
                        h1n = "pdtf"
                    h1last = None
                    for n, (p0_, wn) in enumerate(SEGS):
                        tgt, tb, _ = h1tgts[n]
                        toff = tb if jp else p0_
                        for k in range(8):
                            dps = [d1_]
                            if n == 0 and k == 0:
                                dps += B.wdeps(h1n)
                            h1last = mm(tgt[:, toff:toff + wn],
                                        w1r[sl][:, k * 256 + jj * 128:
                                                k * 256 + jj * 128 + 128],
                                        h2T[:, k * tpc + hf * THALF + p0_:
                                            k * tpc + hf * THALF + p0_ + wn],
                                        k == 0, k == 7, deps=dps, name="h1mm")
                            B.r(h1last, "w1r%d" % sl)
                    B.w(h1last, h1n)
                    B.W[h1n] = h1last
                    sil_last = None
                    for n, (p0_, wn) in enumerate(SEGS):
                        tgt, tb, _ = h1tgts[n]
                        toff = tb if jp else p0_
                        dps = [h1last, d_b1t]
                        if n == 0:
                            dps += B.wdeps("sil%d" % jp)
                        sil_last = act(silscr[jp][:, p0_:p0_ + wn],
                                       tgt[:, toff:toff + wn], AF.Silu,
                                       bias=b1t[:, e * 16 + j:e * 16 + j + 1],
                                       deps=dps, name="sil")
                        B.r(sil_last, h1n)
                        if n == 0:
                            B.w(sil_last, "sil%d" % jp)
                    B.W["sil%d" % jp] = sil_last
                    ab_ = stt(actb[:, j * THALF:(j + 1) * THALF],
                              ph0[:, 0:THALF],
                              b0t[:, e * 16 + j:e * 16 + j + 1],
                              silscr[jp][:, 0:THALF], ALU.add, ALU.mult,
                              deps=[h0last, sil_last, d_b0t] +
                              B.wdeps("actb%d" % j), name="actb")
                    B.r(ab_, ph0n)
                    B.r(ab_, "sil%d" % jp)
                    B.w(ab_, "actb%d" % j)
                    if jj == 1 and gg + 3 <= 7:
                        slot_dma[gg + 3] = load_slot(gg + 3)

                for m_ in range(8):
                    pO = pp0 if m_ % 2 == 0 else pp1
                    pOn = "pf0_%d" % (m_ % 2)
                    olast = None
                    for j in range(16):
                        for n, (p0_, wn) in enumerate(SEGS):
                            dps = [dw2[-1], B.W["actb%d" % j]]
                            if j == 0 and n == 0:
                                dps += B.wdeps(pOn)
                            olast = mm(pO[:, p0_:p0_ + wn],
                                       w2b[:, j * 1024 + m_ * 128:
                                           j * 1024 + m_ * 128 + 128],
                                       actb[:, j * THALF + p0_:
                                            j * THALF + p0_ + wn],
                                       j == 0, j == 15, deps=dps, name="o_mm")
                            B.r(olast, "actb%d" % j)
                            B.r(olast, "w2b")
                    B.w(olast, pOn)
                    B.W[pOn] = olast
                    tmp = silscr[m_ % 2]
                    t1 = tt(tmp[:, 0:THALF], pO[:, 0:THALF],
                            wbc[ep][:, 0:THALF], ALU.mult,
                            deps=[olast, wbe] + B.wdeps("sil%d" % (m_ % 2)),
                            name="of1")
                    B.r(t1, pOn)
                    B.r(t1, "wbc%d" % ep)
                    B.w(t1, "sil%d" % (m_ % 2))
                    t2 = tt(outacc[:, m_ * THALF:(m_ + 1) * THALF],
                            outacc[:, m_ * THALF:(m_ + 1) * THALF],
                            tmp[:, 0:THALF], ALU.add,
                            deps=[t1] + B.wdeps("oacc%d" % m_), name="of2")
                    B.r(t2, "sil%d" % (m_ % 2))
                    B.w(t2, "oacc%d" % m_)

            for tc in range(NTH):
                par = tc % 2
                dxm = dmaQ(xmr[par], xmid_d[hf * THALF + tc * CH:
                                            hf * THALF + (tc + 1) * CH, :],
                           deps=B.wdeps("xmr%d" % par) + first_deps,
                           name="xmin")
                B.w(dxm, "xmr%d" % par)
                trs = []
                for m_ in range(8):
                    dps = [B.W.get("oacc%d" % m_)]
                    if m_ == 0:
                        dps += B.wdeps("pp2f")
                    trs.append(tr(pp2[:, m_ * 128:(m_ + 1) * 128],
                                  outacc[:, m_ * THALF + tc * CH:
                                         m_ * THALF + (tc + 1) * CH],
                                  deps=dps, name="fT"))
                    B.r(trs[-1], "oacc%d" % m_)
                B.w(trs[0], "pp2f")
                B.W["pp2f"] = trs[-1]
                bmms = []
                for n in range(2):
                    dps = [d_b2b] + wTg_deps
                    if n == 0:
                        dps += B.wdeps("pf0_0")
                    bmms.append(mm(pp0[:, n * 512:(n + 1) * 512],
                                   wTg[0:8, hf * THALF + tc * CH:
                                       hf * THALF + (tc + 1) * CH],
                                   b2b[0:8, n * 512:(n + 1) * 512],
                                   True, True, deps=dps, name="b2mm"))
                B.w(bmms[0], "pf0_0")
                B.W["pf0_0"] = bmms[-1]
                u1 = tt(otile[par], pp2[:, 0:1024], gbb[:], ALU.mult,
                        deps=[trs[-1]] + gbb_deps + B.wdeps("ot%d" % par),
                        name="u1")
                B.r(u1, "pp2f")
                B.w(u1, "ot%d" % par)
                u2 = tt(otile[par], otile[par], xmr[par], ALU.add,
                        deps=[u1, dxm], name="u2")
                B.r(u2, "xmr%d" % par)
                u3 = tt(xmr[par], pp0[:, 0:1024], gbb[:], ALU.mult,
                        deps=[bmms[-1], u2], name="u3")
                B.r(u3, "pf0_0")
                B.w(u3, "xmr%d" % par)
                u4 = tt(otile[par], otile[par], xmr[par], ALU.add,
                        deps=[u3], name="u4")
                B.r(u4, "xmr%d" % par)
                B.w(u4, "ot%d" % par)
                dout = dmaQ(out_d[hf * THALF + tc * CH:
                                  hf * THALF + (tc + 1) * CH, :],
                            otile[par], deps=[u4], name="dout")
                B.r(dout, "ot%d" % par)
                all_douts.append(dout)

        final_deps = all_douts + dbg_final
        L.op("Q", lambda e: e.nop(), deps=final_deps, name="final_wait")

        L.assign()
        with nc.Block() as block:
            block.sync(L.body("Q", esems, dsems))
            block.gpsimd(L.body("G", esems, dsems))
            block.tensor(L.body("T", esems, dsems))
            block.vector(L.body("V", esems, dsems))
            block.scalar(L.body("S", esems, dsems))
    return nc


def _blk(wT, kchunks, ncols):
    return np.ascontiguousarray(
        wT.reshape(kchunks, 128, ncols).transpose(1, 0, 2).reshape(128, -1))


def prep_inputs(inputs, tpc=TPC):
    f32 = lambda a: np.ascontiguousarray(np.asarray(a, np.float32))
    NCH = tpc // CH
    x = f32(inputs["L0_attn_x"])
    nb = x.shape[0]
    Wq, Wk, Wv = (f32(inputs["L0_attn_W" + s]) for s in "qkv")
    Wout = f32(inputs["L0_attn_Wout"])
    Wbeta, Wmix = f32(inputs["L0_attn_Wbeta"]), f32(inputs["L0_attn_Wmix"])
    shared = dict(
        wqT=_blk(Wq.T, 8, HD), wkT=_blk(Wk.T, 8, HD), wvT=_blk(Wv.T, 8, HD),
        woutT=_blk(Wout.T, 8, HD),
        wbmT=_blk(np.concatenate([Wbeta.T, Wmix.T], axis=1), 8, 128),
        bbm=np.concatenate([f32(inputs["L0_attn_bbeta"]),
                            f32(inputs["L0_attn_bmix"])])[None, :],
        bout_h=f32(inputs["L0_attn_bout"])[None, :],
        lng=f32(inputs["L0_attn_ln_g"])[None, :],
        lnb=f32(inputs["L0_attn_ln_b"])[None, :],
        gateT=_blk(f32(inputs["L0_ffn_gate_w"]).T, 8, 8),
        gatebc_h=f32(inputs["L0_ffn_gate_b"]).reshape(8, 1),
        wmod=f32(inputs["L0_wmod"]),
        w0r=np.stack([_blk(f32(inputs["L0_ffn_w0"])[e].T, 8, 2048)
                      for e in range(FE)]),
        w1r=np.stack([_blk(f32(inputs["L0_ffn_w1"])[e].T, 8, 2048)
                      for e in range(FE)]),
        w2r=np.stack([_blk(f32(inputs["L0_ffn_w2"])[e].T, 16, HD)
                      for e in range(FE)]),
        b0t_h=f32(inputs["L0_ffn_b0"]).reshape(8, 16, 128).transpose(2, 0, 1)
        .reshape(128, 128).copy(),
        b1t_h=f32(inputs["L0_ffn_b1"]).reshape(8, 16, 128).transpose(2, 0, 1)
        .reshape(128, 128).copy(),
        b2m=f32(inputs["L0_ffn_b2"]),
        diota=np.arange(16, dtype=np.float32)[None, :],
        identc_h=np.eye(128, dtype=np.float32),
        masku_h=np.triu(np.ones((128, 128), np.float32)),
    )
    crows = f32(inputs["L0_c"])
    in_maps = []
    for core in range(2 * nb):
        b, hf = core // 2, core % 2
        own = np.arange(hf * tpc, (hf + 1) * tpc, dtype=np.float32)
        prt = np.arange(0, tpc, dtype=np.float32)
        pos = np.concatenate([own.reshape(NCH, CH).T,
                              prt.reshape(NCH, CH).T], axis=1)
        im = dict(shared)
        im["x"] = np.ascontiguousarray(x[b, hf * tpc:(hf + 1) * tpc])
        im["xp"] = np.ascontiguousarray(x[b, 0:tpc])
        im["crowT_h"] = crows[b].reshape(8, 128).T.copy()
        im["flagc_h"] = np.full((64, 1), float(hf), np.float32)
        im["poscol_h"] = np.ascontiguousarray(pos)
        in_maps.append(im)
    return in_maps


def kernel(**inputs):
    in_maps = prep_inputs(inputs)
    nc = build()
    res = run_bass_kernel_spmd(nc, in_maps, core_ids=list(range(8)))
    out = np.empty((4, 2 * TPC, HD), np.float32)
    for core in range(8):
        b, hf = core // 2, core % 2
        out[b, hf * TPC:(hf + 1) * TPC] = res.results[core]["out"]
    return out


# version 07
# speedup vs baseline: 1.0607x; 1.0451x over previous
import numpy as np
import concourse.bass as bass
import concourse.mybir as mybir
from concourse.bass_utils import run_bass_kernel_spmd
from contextlib import ExitStack

F32 = mybir.dt.float32
BF16 = mybir.dt.bfloat16
AF = mybir.ActivationFunctionType
ALU = mybir.AluOpType
AX = mybir.AxisListType.X

HD = 1024
NH = 32
HM = 64
CH = 128
FE = 8
BMIN, BMAX = 0.85, 0.9995
EPS = 1e-6
LN_EPS = 1e-5
LOG_ROPE = float(np.log(10000.0))
PI = float(np.pi)
TPC = 2048
_last_led = None

ENGS = ("T", "V", "S", "G", "Q")
NDMASEM = 24
FENCE_DIST = 2


class _Op:
    __slots__ = ("eng", "idx", "fn", "deps", "special", "w_inc", "w_nop",
                 "val", "fval", "dsem", "dval", "name")


class Led:
    def __init__(self):
        self.st = {e: [] for e in ENGS}
        self.dcnt = [0] * NDMASEM
        self.drr = {"Q": 0, "G": NDMASEM // 2}

    def op(self, eng, fn, deps=(), special=False, name=""):
        o = _Op()
        o.eng, o.idx, o.fn = eng, len(self.st[eng]), fn
        o.deps = [d for d in deps if d is not None]
        o.special = special
        o.w_inc = o.w_nop = False
        o.val = o.fval = o.dsem = o.dval = None
        o.name = name
        self.st[eng].append(o)
        return o

    def dma(self, eng, fn, deps=(), name=""):
        o = self.op(eng, fn, deps, name=name)
        half = NDMASEM // 2
        base = 0 if eng == "Q" else half
        o.dsem = base + (self.drr[eng] % half)
        self.drr[eng] += 1
        self.dcnt[o.dsem] += 16
        o.dval = self.dcnt[o.dsem]
        return o

    def lasts(self):
        return [self.st[e][-1] for e in ENGS if self.st[e]]

    def assign(self):
        for e in ENGS:
            for o in self.st[e]:
                for d in o.deps:
                    if d.dsem is not None:
                        continue
                    if d.eng == o.eng:
                        if d.eng == "T" and not d.special:
                            continue
                        if d.special or (o.idx - d.idx) <= FENCE_DIST:
                            d.w_inc = True
                            d.w_nop = True
                    else:
                        d.w_inc = True
                        if d.special:
                            d.w_nop = True
        for e in ENGS:
            c = 0
            for o in self.st[e]:
                if e == "S" and (o.w_inc or o.w_nop):
                    o.w_inc = False
                    o.w_nop = True
                if o.w_inc:
                    c += 1
                    o.val = c
                if o.w_nop:
                    c += 1
                    o.fval = c

    def body(self, ekey, esems, dsems):
        stream = self.st[ekey]

        def run(engine):
            wm = {}
            sem_self = esems[ekey]
            for o in stream:
                for d in o.deps:
                    if d.dsem is not None:
                        sem, tgt, key = dsems[d.dsem], d.dval, ("d", d.dsem)
                    else:
                        same = d.eng == ekey
                        if same and d.eng == "T" and not d.special:
                            continue
                        if (same and not d.special
                                and (o.idx - d.idx) > FENCE_DIST):
                            continue
                        if same or d.special or d.eng == "S":
                            tgt = d.fval if d.fval is not None else d.val
                        else:
                            tgt = d.val
                        sem, key = esems[d.eng], ("e", d.eng)
                        assert tgt is not None, (ekey, o.name, d.eng, d.name)
                    if wm.get(key, -1) < tgt:
                        engine.wait_ge(sem, tgt)
                        wm[key] = tgt
                inst = o.fn(engine)
                if o.dsem is not None:
                    inst.then_inc(dsems[o.dsem], 16)
                else:
                    if o.w_inc:
                        inst.then_inc(sem_self, 1)
                    if o.w_nop:
                        if ekey == "S":
                            nop = engine.drain(fusable=False)
                        elif ekey in ("V", "G"):
                            nop = engine.engine_nop()
                        else:
                            nop = engine.nop(nofuse=True)
                        nop.then_inc(sem_self, 1)
        return run


class Buf:
    def __init__(self):
        self.R = {}
        self.W = {}

    def r(self, op, *names):
        for n in names:
            self.R.setdefault(n, []).append(op)
        return op

    def wdeps(self, *names):
        deps = []
        for n in names:
            deps.extend(self.R.get(n, []))
            if self.W.get(n) is not None:
                deps.append(self.W[n])
        return deps

    def w(self, op, *names):
        for n in names:
            self.R[n] = []
            self.W[n] = op
        return op


def segs(total, step=512):
    out = []
    p = 0
    while p < total:
        w = min(step, total - p)
        out.append((p, w))
        p += w
    return out


def build(tpc=TPC, dbg=(), stop_after=None, skips=()):
    NCH = tpc // CH
    THALF = tpc // 2
    NTH = THALF // CH
    nc = bass.Bass("TRN2", target_bir_lowering=False,
                   detect_race_conditions=False)
    L = Led()
    B = Buf()
    global _last_led
    _last_led = L
    dbg = set(dbg)

    def din(name, shape):
        return nc.dram_tensor(name, list(shape), F32, kind="ExternalInput").ap()

    x_d = din("x", (tpc, HD))
    xp_d = din("xp", (tpc, HD))
    crowT_d = din("crowT_h", (128, 8))
    flag_d = din("flagc_h", (64, 1))
    costab_d = din("costab_h", (128, 2 * NCH * 16))
    sintab_d = din("sintab_h", (128, 2 * NCH * 16))
    wqT_d = din("wqT", (128, 8 * HD))
    wkT_d = din("wkT", (128, 8 * HD))
    wvT_d = din("wvT", (128, 8 * HD))
    woutT_d = din("woutT", (128, 8 * HD))
    wbmT_d = din("wbmT", (128, 8 * 128))
    bbm_d = din("bbm", (1, 128))
    bout_d = din("bout_h", (1, HD))
    lng_d = din("lng", (1, HD))
    lnb_d = din("lnb", (1, HD))
    gateT_d = din("gateT", (128, 64))
    gatebc_d = din("gatebc_h", (8, 1))
    wmod_d = din("wmod", (HD, 6144))
    w0_d = din("w0r", (FE, 128, 8 * 2048))
    w1_d = din("w1r", (FE, 128, 8 * 2048))
    w2_d = din("w2r", (FE, 128, 16 * HD))
    b0t_d = din("b0t_h", (128, 128))
    b1t_d = din("b1t_h", (128, 128))
    b2_d = din("b2m", (8, HD))
    identc_d = din("identc_h", (128, 128))
    masku_d = din("masku_h", (128, 128))
    xmid_d = nc.dram_tensor("xmid", [tpc, HD], F32, kind="Internal").ap()
    out_d = nc.dram_tensor("out", [tpc, HD], F32, kind="ExternalOutput").ap()

    with ExitStack() as stack:
        ent = stack.enter_context

        h2T = ent(nc.sbuf_tensor("h2T", [128, 8 * tpc], BF16))
        costab = ent(nc.sbuf_tensor("costab", [128, 2 * NCH * 16], F32))
        sintab = ent(nc.sbuf_tensor("sintab", [128, 2 * NCH * 16], F32))
        wgate = ent(nc.sbuf_tensor("wgate", [128, NCH * 8], F32))
        wTg = ent(nc.sbuf_tensor("wTg", [8, tpc], BF16))
        identc = ent(nc.sbuf_tensor("identc", [128, 128], F32))
        masku = ent(nc.sbuf_tensor("masku", [128, 128], F32))
        onesr = ent(nc.sbuf_tensor("onesr", [1, 128], F32))
        onesrb = ent(nc.sbuf_tensor("onesrb", [1, 128], BF16))
        onescb = ent(nc.sbuf_tensor("onescb", [128, 1], BF16))
        gatebc = ent(nc.sbuf_tensor("gatebc", [8, 1], F32))
        gateTb = ent(nc.sbuf_tensor("gateTb", [128, 64], BF16))
        gbb = ent(nc.sbuf_tensor("gbb", [128, HD], BF16))
        stat = ent(nc.sbuf_tensor("stat", [128, 64], F32))
        flagc = ent(nc.sbuf_tensor("flagc", [64, 1], F32))
        crowT = ent(nc.sbuf_tensor("crowT", [128, 8], F32))
        csil = ent(nc.sbuf_tensor("csil", [128, 8], BF16))
        zer64 = ent(nc.sbuf_tensor("zer64", [64, CH], F32))
        b2b = ent(nc.sbuf_tensor("b2b", [8, HD], BF16))
        b0t = ent(nc.sbuf_tensor("b0t", [128, 128], F32))
        b1t = ent(nc.sbuf_tensor("b1t", [128, 128], F32))
        bbmb = ent(nc.sbuf_tensor("bbmb", [128, 128], F32))
        betaTs = ent(nc.sbuf_tensor("betaTs", [64, CH], F32))
        cperow = ent(nc.sbuf_tensor("cperow", [1, HM], F32))
        kupd = ent(nc.sbuf_tensor("kupd", [64, 32], F32))
        lgT = ent(nc.sbuf_tensor("lgT", [8, CH], F32))
        lgtm = ent(nc.sbuf_tensor("lgtm", [128, 8], F32))
        g8a = ent(nc.sbuf_tensor("g8a", [128, 8], F32))
        g8b = ent(nc.sbuf_tensor("g8b", [128, 8], F32))

        pp0 = ent(nc.psum_tensor("pp0", [128, 1024], F32))
        pp1 = ent(nc.psum_tensor("pp1", [128, 1024], F32))
        pp2 = ent(nc.psum_tensor("pp2", [128, 1024], F32))
        ppd = ent(nc.psum_tensor("ppd", [128, 512], F32))
        ppt = ent(nc.psum_tensor("ppt", [128, 512], F32))

        esems = {e: ent(nc.semaphore("es_" + e)) for e in ENGS}
        dsems = [ent(nc.semaphore("ds%d" % i)) for i in range(NDMASEM)]

        stB = ExitStack()
        stack.callback(stB.close)

        def tk(name, parts, elems, dt=F32, stk=None):
            return (stk or stB).enter_context(
                nc.sbuf_tensor(name, [parts, elems], dt))[:]

        wqTb = tk("wqTb", 128, 8 * HD, BF16)
        wkTb = tk("wkTb", 128, 8 * HD, BF16)
        wvTb = tk("wvTb", 128, 8 * HD, BF16)
        woutTb = tk("woutTb", 128, 8 * HD, BF16)
        wbmb = tk("wbmb", 128, 8 * 128, BF16)
        ca1b = tk("ca1b", 128, HD, BF16)
        sab = tk("sab", 128, HD, BF16)
        cb1b = tk("cb1b", 128, HD, BF16)
        sbb = tk("sbb", 128, HD, BF16)
        gab = tk("gab", 128, HD, BF16)
        lngb = tk("lngb", 128, HD, BF16)
        lnbb = tk("lnbb", 128, HD, BF16)
        boutb = tk("boutb", 128, HD, BF16)
        Sf = tk("Sf", 64, 32 * 33, F32)
        Sb = tk("Sb", 64, 32 * 33, BF16)
        xr = [tk("xr%d" % i, 128, HD) for i in range(2)]
        scr = [tk("scr%d" % i, 128, HD) for i in range(5)]
        qtm = tk("qtm", 128, HD)
        ytm = qtm
        h1tm = tk("h1tm", 128, HD)
        ktm = h1tm
        wmseg = [woutTb[:, 0:8 * 512], woutTb[:, 8 * 512:16 * 512]]
        rowst = scr[0][0:1, 0:512]
        rowst2 = scr[1][0:1, 0:512]
        h1T = tk("h1T", 128, HD, BF16)
        fqfm = tk("fqfm", 64, 16 * CH, BF16)
        fkfm = tk("fkfm", 64, 16 * CH, BF16)
        vtm = tk("vtm", 128, HD, BF16)
        vt = [tk("vt%d" % i, 128, HD, BF16) for i in range(2)]
        kt = [tk("kt%d" % i, 128, HD, BF16) for i in range(2)]
        Ab = [tk("Ab%d" % i, 128, 4 * CH, BF16) for i in range(2)]
        yfm = tk("yfm", 128, 8 * CH, BF16)
        betat = tk("betat", 128, HM)
        mixt = tk("mixt", 128, HM)
        cphm = tk("cphm", 64, CH)
        icphm = tk("icphm", 64, CH)
        icpT = tk("icpT", 128, HM)
        icpTb = tk("icpTb", 128, HM, BF16)
        cpendb = tk("cpendb", 64, 32)
        denf = tk("denf", 128, HM)
        mden = tk("mden", 128, HM)
        dbg_stage = [tk("dbgstage", 128, 2176)] if dbg else [None]

        def dmaQ(out, in_, deps=(), name="", slow=False):
            return L.dma("Q", lambda e, o=out, i=in_, s=slow:
                         e.dma_start(out=o, in_=i, allow_slow_non_contiguous=s),
                         deps, name=name)

        def dmaG(out, in_, deps=(), name=""):
            return L.dma("G", lambda e, o=out, i=in_: e.dma_start(out=o, in_=i),
                         deps, name=name)

        def T(fn, deps=(), name=""):
            return L.op("T", fn, deps, name=name)

        def V(fn, deps=(), special=False, name=""):
            return L.op("V", fn, deps, special=special, name=name)

        def S(fn, deps=(), name=""):
            return L.op("S", fn, deps, name=name)

        def mm(out, lhsT, rhs, start, stop, deps=(), tp=None, name="mm"):
            return T(lambda e, o=out, l=lhsT, r=rhs, a=start, b=stop, t=tp:
                     e.matmul(o, l, r, start=a, stop=b, tile_position=t),
                     deps, name=name)

        def cp_(out, in_, deps=(), name="cp"):
            return V(lambda e, o=out, i=in_: e.tensor_copy(out=o, in_=i),
                     deps, name=name)

        def tt(out, a, b, op, deps=(), name="tt"):
            return V(lambda e, o=out, x=a, y=b, p=op:
                     e.tensor_tensor(out=o, in0=x, in1=y, op=p), deps, name=name)

        def ts(out, a, s1, op0, s2=None, op1=None, deps=(), name="ts"):
            if op1 is None:
                return V(lambda e, o=out, x=a, u=s1, p=op0:
                         e.tensor_scalar(out=o, in0=x, scalar1=u, scalar2=None,
                                         op0=p), deps, name=name)
            return V(lambda e, o=out, x=a, u=s1, v=s2, p=op0, q=op1:
                     e.tensor_scalar(out=o, in0=x, scalar1=u, scalar2=v,
                                     op0=p, op1=q), deps, name=name)

        def stt(out, a, s, b, op0, op1, deps=(), name="stt"):
            return V(lambda e, o=out, x=a, u=s, y=b, p=op0, q=op1:
                     e.scalar_tensor_tensor(out=o, in0=x, scalar=u, in1=y,
                                            op0=p, op1=q), deps, name=name)

        def act(out, in_, func, bias=0.0, scale=1.0, accum=None, deps=(), name="act"):
            return S(lambda e, o=out, i=in_, f=func, b=bias, s=scale, a=accum:
                     e.activation(out=o, in_=i, func=f, bias=b, scale=s,
                                  accum_out=a), deps, name=name)

        def recip(out, in_, deps=(), name="recip"):
            return V(lambda e, o=out, i=in_: e.reciprocal(out=o, in_=i),
                     deps, special=True, name=name)

        def vmax(out, in_, deps=(), name="vmax"):
            return V(lambda e, o=out, i=in_: e.reduce_max(out=o, in_=i, axis=AX),
                     deps, special=True, name=name)

        def vsum(out, in_, deps=(), name="vsum"):
            return V(lambda e, o=out, i=in_: e.reduce_sum(out=o, in_=i, axis=AX),
                     deps, special=True, name=name)

        def scan_mul(out, d0, d1, deps=(), name="scan"):
            return V(lambda e, o=out, a=d0, b=d1:
                     e.tensor_tensor_scan(out=o, data0=a, data1=b, initial=1.0,
                                          op0=ALU.mult, op1=ALU.add),
                     deps, special=True, name=name)

        def memset(buf, val, deps=(), name="memset"):
            return V(lambda e, b=buf, v=val: e.memset(b, v), deps, name=name)

        d_identc = dmaQ(identc[:], identc_d, name="identc")

        def tr(out, in_, deps=(), name="tr"):
            p = in_.shape[0]
            return T(lambda e, o=out, i=in_, q=p:
                     e.transpose(o, i, identc[0:q, 0:q]),
                     list(deps) + [d_identc], name=name)

        dbg_final = []

        def dbg_dump(name, ap, deps, src=()):
            if name not in dbg:
                return
            if not isinstance(ap, bass.AP):
                ap = ap[:]
            p, wdt = ap.shape[0], int(np.prod(ap.shape[1:]))
            dd = nc.dram_tensor("dbg_" + name, [p, wdt], F32,
                                kind="ExternalOutput").ap()
            st_ = dbg_stage[0][0:p, 0:wdt]
            cv = cp_(st_, ap, deps=list(deps) + B.wdeps("dbgstage"),
                     name="dbgc_" + name)
            for n_ in src:
                B.r(cv, n_)
            B.w(cv, "dbgstage")
            dm = dmaQ(dd, st_, deps=[cv], name="dbgd_" + name)
            B.r(dm, "dbgstage")
            dbg_final.append(dm)

        d_masku = dmaQ(masku[:], masku_d, name="masku")
        d_gatebc = dmaQ(gatebc[:], gatebc_d, name="gatebc")
        d_gateTb = dmaG(gateTb[:], gateT_d, name="gateTb")
        d_crowT = dmaQ(crowT[:], crowT_d, name="crowT")
        d_flag = dmaQ(flagc[:], flag_d, name="flag")
        d_cost = dmaQ(costab[:], costab_d, name="costab")
        d_sint = dmaQ(sintab[:], sintab_d, name="sintab")
        d_wq = dmaG(wqTb, wqT_d, name="wq")
        d_wk = dmaG(wkTb, wkT_d, name="wk")
        d_wv = dmaG(wvTb, wvT_d, name="wv")
        d_wbm = dmaG(wbmb, wbmT_d, name="wbm")
        d_b2b = dmaG(b2b[:], b2_d, name="b2b")
        d_b0t = dmaQ(b0t[:], b0t_d, name="b0t")
        d_b1t = dmaQ(b1t[:], b1t_d, name="b1t")
        m_on = memset(onesr[:], 1.0, name="onesr")
        m_onb = memset(onesrb[:], 1.0, name="onesrb")
        m_onc = memset(onescb[:], 1.0, name="onescb")
        m_z64 = memset(zer64[:], 0.0, name="z64")
        m_Sf = B.w(memset(Sf, 0.0, name="Sf0"), "Sf")

        COS, SIN = costab, sintab
        tab_deps = [d_cost, d_sint]
        dbg_dump("COSTAB", COS[:], tab_deps)
        dbg_dump("SINTAB", SIN[:], tab_deps)

        a_cs = act(csil[:], crowT[:], AF.Silu, deps=[d_crowT], name="csil")
        wmv = wmod_d.rearrange("(a p) n -> p a n", p=128)
        targets = [sab, ca1b, gab, sbb, cb1b, gbb]
        tnames = ["sab", "ca1b", "gab", "sbb", "cb1b", "gbb"]
        for s_ in range(12):
            vec, sh = s_ // 2, s_ % 2
            wmsl = "wmseg%d" % (s_ % 2)
            dseg = dmaG(wmseg[s_ % 2][:].rearrange("p (a n) -> p a n", a=8),
                        wmv[:, :, s_ * 512:(s_ + 1) * 512],
                        deps=B.wdeps(wmsl), name="wmseg")
            B.w(dseg, wmsl)
            last = None
            for kk in range(8):
                last = mm(ppd[0:1, 0:512], csil[:, kk:kk + 1],
                          wmseg[s_ % 2][:, kk * 512:(kk + 1) * 512],
                          kk == 0, kk == 7,
                          deps=[dseg, a_cs] + (B.wdeps("ppd") if kk == 0 else []),
                          name="modmm")
                B.r(last, wmsl)
            B.w(last, "ppd")
            ev = cp_(rowst[0:1, 0:512], ppd[0:1, 0:512],
                     deps=[last] + B.wdeps("rowst"), name="modev")
            B.r(ev, "ppd")
            B.w(ev, "rowst")
            post = ev
            if vec in (1, 4):
                post = ts(rowst[0:1, 0:512], rowst[0:1, 0:512], 1.0, ALU.add,
                          deps=[ev], name="plus1")
                B.w(post, "rowst")
            elif vec in (2, 5):
                if vec == 2:
                    dbg_dump("ga_raw%d" % sh, rowst[0:1, 0:512], [ev],
                             src=("rowst",))
                t1 = ts(rowst2[0:1, 0:512], rowst[0:1, 0:512], 0.0, ALU.is_ge,
                        deps=[ev] + B.wdeps("rowst2"), name="gsgn")
                B.r(t1, "rowst")
                B.w(t1, "rowst2")
                t2 = ts(rowst2[0:1, 0:512], rowst2[0:1, 0:512], 2.0, ALU.mult,
                        -1.0, ALU.add, deps=[t1], name="gsgn2")
                B.w(t2, "rowst2")
                if vec == 2:
                    dbg_dump("ga_sgn%d" % sh, rowst2[0:1, 0:512], [t2],
                             src=("rowst2",))
                t3a = act(rowst[0:1, 0:512], rowst[0:1, 0:512], AF.Abs,
                          deps=[ev] + B.wdeps("rowst"), name="gabs")
                B.w(t3a, "rowst")
                t3 = ts(rowst[0:1, 0:512], rowst[0:1, 0:512], 0.25, ALU.max,
                        deps=[t3a], name="gclamp")
                B.w(t3, "rowst")
                if vec == 2:
                    dbg_dump("ga_clp%d" % sh, rowst[0:1, 0:512], [t3],
                             src=("rowst",))
                t4 = tt(rowst[0:1, 0:512], rowst[0:1, 0:512], rowst2[0:1, 0:512],
                        ALU.mult, deps=[t3, t2] + B.wdeps("rowst"), name="gmul")
                B.r(t4, "rowst2")
                B.w(t4, "rowst")
                post = t4
            bcm = mm(pp0[:, 0:512], onesr[0:1, 0:128], rowst[0:1, 0:512],
                     True, True, deps=[post, m_on] + B.wdeps("pp0"), name="bcmm")
            B.r(bcm, "rowst")
            B.w(bcm, "pp0")
            bce = cp_(targets[vec][:, sh * 512:(sh + 1) * 512], pp0[:, 0:512],
                      deps=[bcm], name="bc_" + tnames[vec])
            B.r(bce, "pp0")
            B.w(bce, tnames[vec] + str(sh))

        for tgt, tn, src, wdt in ((lngb, "lngb", lng_d, HD),
                                  (lnbb, "lnbb", lnb_d, HD),
                                  (boutb, "boutb", bout_d, HD),
                                  (bbmb, "bbmb", bbm_d, 128)):
            for sh, (p0, wn) in enumerate(segs(wdt)):
                dr = dmaQ(rowst[0:1, 0:wn], src[0:1, p0:p0 + wn],
                          deps=B.wdeps("rowst"), name="row_" + tn)
                B.w(dr, "rowst")
                bcm = mm(pp0[:, 0:wn], onesr[0:1, 0:128], rowst[0:1, 0:wn],
                         True, True, deps=[dr, m_on] + B.wdeps("pp0"),
                         name="bcmm2")
                B.r(bcm, "rowst")
                B.w(bcm, "pp0")
                bce = cp_(tgt[:, p0:p0 + wn], pp0[:, 0:wn], deps=[bcm],
                          name="bce_" + tn)
                B.r(bce, "pp0")
                B.w(bce, tn + str(sh))

        d_wout = dmaG(woutTb, woutT_d,
                      deps=B.wdeps("wmseg0") + B.wdeps("wmseg1"),
                      name="wout")

        ph0_last = L.lasts()
        for nm in ("scr0", "scr1", "scr2", "scr3", "scr4"):
            for o_ in ph0_last:
                B.r(o_, nm)
        bct_deps = [B.W["sab0"], B.W["sab1"], B.W["ca1b0"], B.W["ca1b1"]]
        bct2_deps = [B.W["sbb0"], B.W["sbb1"], B.W["cb1b0"], B.W["cb1b1"],
                     B.W["gab0"], B.W["gab1"], B.W["lngb0"], B.W["lngb1"],
                     B.W["lnbb0"], B.W["lnbb1"],
                     B.W["boutb0"], B.W["boutb1"]]
        bbm_deps = [B.W["bbmb0"]]
        gbb_deps = [B.W["gbb0"], B.W["gbb1"]]

        def ln_apply(src_ap, src_name, out_ap, out_name, c1_tile, s_tile,
                     cols, xdeps, extra_deps=()):
            sname = "stat%d" % cols
            a1 = act(out_ap, src_ap, AF.Copy, accum=stat[:, cols:cols + 1],
                     deps=list(xdeps) + B.wdeps(sname) + B.wdeps(out_name) +
                     list(extra_deps), name="ln_c")
            a2 = act(scr[4], src_ap, AF.Square, accum=stat[:, cols + 1:cols + 2],
                     deps=list(xdeps) + B.wdeps("scr4"), name="ln_sq")
            B.w(a2, "scr4")
            v1 = ts(stat[:, cols + 2:cols + 3], stat[:, cols:cols + 1],
                    1.0 / HD, ALU.mult, deps=[a1], name="ln_mu")
            v2 = ts(stat[:, cols + 3:cols + 4], stat[:, cols + 2:cols + 3],
                    stat[:, cols + 2:cols + 3], ALU.mult, deps=[v1], name="ln_mu2")
            v3 = ts(stat[:, cols + 4:cols + 5], stat[:, cols + 1:cols + 2],
                    1.0 / HD, ALU.mult, deps=[a2], name="ln_ex2")
            v4 = ts(stat[:, cols + 5:cols + 6], stat[:, cols + 4:cols + 5],
                    stat[:, cols + 3:cols + 4], ALU.subtract, LN_EPS, ALU.add,
                    deps=[v2, v3], name="ln_var")
            s3 = act(stat[:, cols + 6:cols + 7], stat[:, cols + 5:cols + 6],
                     AF.Ln, deps=[v4], name="ln_lv")
            s4 = act(stat[:, cols + 6:cols + 7], stat[:, cols + 6:cols + 7],
                     AF.Exp, scale=-0.5, deps=[s3], name="ln_rs")
            v5 = ts(stat[:, cols + 7:cols + 8], stat[:, cols + 2:cols + 3],
                    -1.0, ALU.mult, deps=[v1], name="ln_nm")
            h1a = ts(out_ap, src_ap, stat[:, cols + 7:cols + 8], ALU.add,
                     stat[:, cols + 6:cols + 7], ALU.mult,
                     deps=[v5, s4, a1] + list(xdeps), name="ln_n")
            h1b = tt(out_ap, out_ap, c1_tile, ALU.mult, deps=[h1a], name="ln_m")
            h1c = tt(out_ap, out_ap, s_tile, ALU.add, deps=[h1b], name="ln_a")
            B.w(h1a, sname)
            B.r(h1a, src_name)
            B.r(h1c, src_name)
            B.w(h1c, out_name)
            return h1c

        dbg_pre = [None]

        def rope_elu(psum_src, psum_name, out_f32, out_name, slot, dep_mm):
            TR = lambda ap: ap.transpose([0, 2, 1])
            cosr = COS[:, slot * 16:(slot + 1) * 16].unsqueeze(2) \
                .broadcast_to((128, 16, 32))
            sinr = SIN[:, slot * 16:(slot + 1) * 16].unsqueeze(2) \
                .broadcast_to((128, 16, 32))
            pv = psum_src.rearrange("p (h d) -> p h d", h=32)
            q1, q2 = TR(pv[:, :, 0:16]), TR(pv[:, :, 16:32])
            s0a = TR(scr[0][:, 0:512].rearrange("p (h d) -> p h d", h=32))
            s0b = TR(scr[0][:, 512:1024].rearrange("p (h d) -> p h d", h=32))
            s1a = TR(scr[1][:, 0:512].rearrange("p (h d) -> p h d", h=32))
            s1b = TR(scr[1][:, 512:1024].rearrange("p (h d) -> p h d", h=32))
            ov0 = out_f32.rearrange("p (h d) -> p h d", h=32)
            ov = None
            d0 = B.wdeps("scr0") + B.wdeps("scr1") + tab_deps
            ta = tt(s0a, q1, cosr, ALU.mult, deps=[dep_mm] + d0, name="rp_a")
            tb = tt(s1a, q2, sinr, ALU.mult, deps=[dep_mm], name="rp_b")
            tc = tt(s0b, q1, sinr, ALU.mult, deps=[dep_mm], name="rp_c")
            td = tt(s1b, q2, cosr, ALU.mult, deps=[dep_mm], name="rp_d")
            B.w(td, "scr0")
            B.w(td, "scr1")
            qa = tt(TR(ov0[:, :, 0:16]), s0a, s1a, ALU.subtract,
                    deps=[ta, tb] + B.wdeps(out_name), name="rp_e")
            qb = tt(TR(ov0[:, :, 16:32]), s0b, s1b, ALU.add, deps=[tc, td],
                    name="rp_f")
            B.r(qb, "scr0")
            B.r(qb, "scr1")
            if dbg_pre[0] is not None:
                dbg_dump(dbg_pre[0], out_f32, [qa, qb])
                dbg_pre[0] = None
            mn = ts(scr[4], out_f32, 0.0, ALU.min,
                    deps=[qa, qb] + B.wdeps("scr4"), name="el_m")
            B.w(mn, "scr4")
            ex = act(scr[4], scr[4], AF.Exp, deps=[mn], name="el_e")
            B.w(ex, "scr4")
            ph = stt(out_f32, out_f32, 0.0, scr[4], ALU.max, ALU.add,
                     deps=[ex, qb], name="el_p")
            B.r(ph, "scr4")
            B.w(ph, out_name)
            return ph

        def transp_batch(src_f32, src_name, dst, dst_name, nblk, bdeps, tag,
                         parts=128, src_w=128, dst_targets=None):
            per = 512 // CH
            evs = []
            bb = 0
            while bb * per < nblk:
                trs = []
                for j in range(per):
                    jj = bb * per + j
                    if jj >= nblk:
                        break
                    trs.append(tr(ppt[0:src_w, j * CH:(j + 1) * CH],
                                  src_f32[:, jj * src_w:(jj + 1) * src_w],
                                  deps=list(bdeps) + B.wdeps("ppt"),
                                  name="tr_" + tag))
                    B.r(trs[-1], src_name)
                if dst_targets is None:
                    ev = cp_(dst[0:src_w, bb * per * CH:
                                 (bb * per + len(trs)) * CH],
                             ppt[0:src_w, 0:len(trs) * CH],
                             deps=[trs[-1]], name="trev_" + tag)
                    evl = [ev]
                else:
                    evl = []
                    for j in range(len(trs)):
                        jj = bb * per + j
                        evl.append(cp_(dst_targets[jj],
                                       ppt[0:src_w, j * CH:(j + 1) * CH],
                                       deps=[trs[-1]], name="trev_" + tag))
                B.w(trs[0], "ppt")
                B.R["ppt"] = list(evl)
                evs.extend(evl)
                bb += 1
            B.w(evs[-1], dst_name)
            for e_ in evs:
                pass
            return evs

        def proj(psum_ap, psum_name, wT, wdep, nseg, h1dep, tag, war=()):
            lastseg = None
            for n in range(nseg):
                last = None
                for k in range(8):
                    last = mm(psum_ap[:, n * 512:(n + 1) * 512],
                              h1T[:, k * 128:(k + 1) * 128],
                              wT[:, k * (nseg * 512) + n * 512:
                                 k * (nseg * 512) + (n + 1) * 512],
                              k == 0, k == 7,
                              deps=([h1dep, wdep] +
                                    (B.wdeps(psum_name, *war)
                                     if n == 0 and k == 0 else [])),
                              name="pj_" + tag)
                    B.r(last, "h1T")
                if n == 0:
                    B.w(last, psum_name)
                lastseg = last
            B.W[psum_name] = lastseg
            return lastseg

        CPE_VIEW = cpendb[:].rearrange("p (a m) -> p a m", a=16)

        def emit_chunk(c, stage):
            par = c % 2
            xrb = xr[par]
            xname = "xr%d" % par
            src = xp_d if stage == "A" else x_d
            slot = (NCH + c) if stage == "A" else c
            d_x = dmaQ(xrb, src[c * CH:(c + 1) * CH, :],
                       deps=B.wdeps(xname), name="xin")
            B.w(d_x, xname)

            h1c = ln_apply(xrb, xname, h1tm, "h1tm", ca1b, sab, 32, [d_x],
                           extra_deps=bct_deps)
            if stage == "B" and c == 0:
                dbg_dump("h1_c0", h1tm, [h1c], src=("h1tm",))
            h1evs = transp_batch(h1tm, "h1tm", h1T, "h1T", 8,
                                 [h1c] + B.wdeps("h1T"), "h1")
            h1dep = h1evs[-1]

            lastbm = None
            for k in range(8):
                lastbm = mm(ppd[:, 0:128], h1T[:, k * 128:(k + 1) * 128],
                            wbmb[:, k * 128:(k + 1) * 128], k == 0, k == 7,
                            deps=[h1dep, d_wbm] +
                                 (B.wdeps("ppd") if k == 0 else []),
                            name="pj_bm")
                B.r(lastbm, "h1T")
            B.w(lastbm, "ppd")
            zb = tt(scr[2][:, 0:128], ppd[:, 0:128], bbmb[:, 0:128], ALU.add,
                    deps=[lastbm] + bbm_deps + B.wdeps("scr2"), name="bm_z")
            B.r(zb, "ppd")
            B.w(zb, "scr2")
            en = act(scr[3][:, 0:64], scr[2][:, 0:64], AF.Exp, scale=-1.0,
                     deps=[zb] + B.wdeps("scr3"), name="b_e")
            B.w(en, "scr3")
            d1 = ts(scr[3][:, 0:64], scr[3][:, 0:64], 1.0, ALU.add,
                    deps=[en], name="b_d")
            bv = recip(scr[3][:, 64:128], scr[3][:, 0:64], deps=[d1], name="b_r")
            bc_ = ts(betat[:], scr[3][:, 64:128], BMIN, ALU.max, BMAX, ALU.min,
                     deps=[bv] + B.wdeps("betat"), name="b_c")
            B.w(bc_, "betat")
            if stage == "B" and c == 0:
                dbg_dump("beta_c0", betat, [bc_], src=("betat",))
            mxv = None
            if stage == "B" and "mix" not in skips:
                m0v = scr[2][:, 64:128][:, 0::2]
                m1v = scr[2][:, 64:128][:, 1::2]
                mx = tt(scr[3][:, 0:32], m0v, m1v, ALU.max, deps=[zb, bc_],
                        name="mx_m")
                B.r(mx, "scr2")
                mzs = tt(scr[3][:, 64:128].rearrange("p (h m) -> p h m", h=32),
                         scr[2][:, 64:128].rearrange("p (h m) -> p h m", h=32),
                         scr[3][:, 0:32].unsqueeze(2).broadcast_to((128, 32, 2)),
                         ALU.subtract, deps=[mx], name="mx_s")
                B.r(mzs, "scr2")
                mex = act(scr[3][:, 64:128], scr[3][:, 64:128], AF.Exp,
                          deps=[mzs], name="mx_e")
                msm = tt(scr[3][:, 32:64], scr[3][:, 64:128][:, 0::2],
                         scr[3][:, 64:128][:, 1::2], ALU.add, deps=[mex],
                         name="mx_su")
                mrs = recip(scr[3][:, 0:32], scr[3][:, 32:64], deps=[msm],
                            name="mx_r")
                mxv = tt(mixt[:].rearrange("p (h m) -> p h m", h=32),
                         scr[3][:, 64:128].rearrange("p (h m) -> p h m", h=32),
                         scr[3][:, 0:32].unsqueeze(2).broadcast_to((128, 32, 2)),
                         ALU.mult, deps=[mrs, mex] + B.wdeps("mixt"), name="mx_v")
                B.w(mxv, "mixt")
                B.w(mxv, "scr3")
                if c == 0:
                    dbg_dump("mix_c0", mixt, [mxv], src=("mixt",))

            bT = tr(ppd[0:64, 384:512], betat[:, 0:64],
                    deps=[bc_] + B.wdeps("ppd"), name="bT")
            B.w(bT, "ppd")
            bTe = cp_(betaTs[:], ppd[0:64, 384:512],
                      deps=[bT] + B.wdeps("betaTs"), name="bTe")
            B.r(bTe, "ppd")
            B.w(bTe, "betaTs")
            cpv = scan_mul(cphm[:], betaTs[:], zer64[:],
                           deps=[bTe, m_z64] + B.wdeps("cphm"), name="cp")
            B.r(cpv, "betaTs")
            B.w(cpv, "cphm")
            icv = recip(icphm[:], cphm[:], deps=[cpv] + B.wdeps("icphm"),
                        name="icp")
            B.r(icv, "cphm")
            B.w(icv, "icphm")
            if stage == "B" and c == 0:
                dbg_dump("cp_c0", cphm, [cpv], src=("cphm",))
            iT = tr(ppd[:, 192:256], icphm[:],
                    deps=[icv] + B.wdeps("ppd"), name="icpT")
            B.r(iT, "icphm")
            B.w(iT, "ppd")
            iTe = cp_(icpT[:], ppd[:, 192:256], deps=[iT] + B.wdeps("icpT"),
                      name="icpTe")
            B.w(iTe, "icpT")
            iTb = cp_(icpTb[:], ppd[:, 192:256], deps=[iT] + B.wdeps("icpTb"),
                      name="icpTb")
            B.r(iTe, "ppd")
            B.r(iTb, "ppd")
            B.w(iTb, "icpTb")
            ceT = tr(ppd[0:1, 256:320], cphm[:, CH - 1:CH],
                     deps=[cpv] + B.wdeps("ppd"), name="cpeT")
            B.r(ceT, "cphm")
            B.w(ceT, "ppd")
            ceE = cp_(cperow[0:1, 0:HM], ppd[0:1, 256:320],
                      deps=[ceT] + B.wdeps("cperow"), name="cpeE")
            B.r(ceE, "ppd")
            B.w(ceE, "cperow")
            cpev = cperow[0:1, 0:HM].rearrange("p (a b m) -> p a b m", a=16, b=2)
            cr0 = cp_(rowst2[0:1, 0:32], cpev[:, :, 0, :],
                      deps=[ceE] + B.wdeps("rowst2"), name="cpe0")
            B.r(cr0, "cperow")
            B.w(cr0, "rowst2")
            cr1 = cp_(rowst2[0:1, 32:64], cpev[:, :, 1, :], deps=[ceE, cr0],
                      name="cpe1")
            B.r(cr1, "cperow")
            B.W["rowst2"] = cr1
            cbm0 = mm(ppd[0:32, 320:352], onesr[0:1, 0:32],
                      rowst2[0:1, 0:32], True, True,
                      deps=[cr0, m_on] + B.wdeps("ppd"), name="cpbm0")
            B.r(cbm0, "rowst2")
            B.w(cbm0, "ppd")
            cbe0 = cp_(cpendb[0:32, :], ppd[0:32, 320:352],
                       deps=[cbm0] + B.wdeps("cpendb"), name="cpbe0")
            B.r(cbe0, "ppd")
            cbm1 = mm(ppd[32:64, 320:352], onesr[0:1, 0:32],
                      rowst2[0:1, 32:64], True, True, tp=(0, 32),
                      deps=[cr1, cbe0], name="cpbm1")
            B.r(cbm1, "rowst2")
            B.w(cbm1, "ppd")
            cbe = cp_(cpendb[32:64, :], ppd[32:64, 320:352],
                      deps=[cbm1], name="cpbe1")
            B.r(cbe, "ppd")
            B.w(cbe, "cpendb")
            B.R["cpendb"] = [cbe0, cbe]
            cbe = cbe0 and cbe

            fq_evs = None
            if stage == "B" and "q" not in skips:
                qmm = proj(pp0, "pp0", wqTb, d_wq, 2, h1dep, "q")
                ph_q = rope_elu(pp0[:, 0:1024], "pp0", qtm, "qtm", slot, qmm)
                B.r(ph_q, "pp0")
                if c == 0:
                    dbg_dump("phiq_c0", qtm, [ph_q], src=("qtm",))
                fq_evs = transp_batch(qtm, "qtm", fqfm, "fqfm", 16,
                                      [ph_q] + B.wdeps("fqfm"), "fq",
                                      src_w=64)

            kmm = proj(pp1, "pp1", wkTb, d_wk, 2, h1dep, "k", war=("pA0", "pA1"))
            if stage == "A" and c == 0:
                dbg_pre[0] = "A_ropek_c0"
            ph_k = rope_elu(pp1[:, 0:1024], "pp1", ktm, "h1tm", slot, kmm)
            B.r(ph_k, "pp1")
            if stage == "A" and c == 0:
                dbg_dump("A_phik_c0", ktm, [ph_k], src=("h1tm",))
            fk_evs = transp_batch(ktm, "h1tm", fkfm, "fkfm", 16,
                                  [ph_k] + B.wdeps("fkfm"), "fk", src_w=64)

            ktops = []
            for m in range(2):
                kv = tt(kt[m][:].rearrange("p (h d) -> p h d", h=32),
                        ktm[:].rearrange("p (h d) -> p h d", h=32),
                        icpT[:, m::2].unsqueeze(2).broadcast_to((128, 32, 32)),
                        ALU.mult, deps=[ph_k, iTe] + B.wdeps("kt%d" % m),
                        name="ktb%d" % m)
                B.r(kv, "h1tm")
                B.r(kv, "icpT")
                B.w(kv, "kt%d" % m)
                ktops.append(kv)

            vmm = proj(pp2, "pp2", wvTb, d_wv, 2, h1dep, "v", war=("pN0", "pN1", "st_pp2"))
            vc = cp_(vtm[:], pp2[:, 0:1024], deps=[vmm] + B.wdeps("vtm"),
                     name="vtm")
            B.r(vc, "pp2")
            B.w(vc, "vtm")
            vtops = []
            if stage == "B" and "vt" not in skips:
                for m in range(2):
                    vv = tt(vt[m][:].rearrange("p (h d) -> p h d", h=32),
                            vtm[:].rearrange("p (h d) -> p h d", h=32),
                            icpT[:, m::2].unsqueeze(2).broadcast_to((128, 32, 32)),
                            ALU.mult, deps=[vc, iTe] + B.wdeps("vt%d" % m),
                            name="vtb%d" % m)
                    B.r(vv, "vtm")
                    B.r(vv, "icpT")
                    B.w(vv, "vt%d" % m)
                    vtops.append(vv)

            yops = []
            if stage == "B" and "scan" in skips:
                yops = [memset(ytm[:], 0.1, deps=B.wdeps("qtm"), name="yfake")]
                B.w(yops[0], "qtm")
            if stage == "B" and "scan" not in skips:
                nq_evs = {}
                den_last = None
                for g in range(8):
                    ga_ = g % 2
                    heads = [2 * j + ga_ for j in range(4 * (g // 2),
                                                        4 * (g // 2) + 4)]
                    pA = pp1[:, ga_ * 512:(ga_ + 1) * 512]
                    pAn = "pA%d" % ga_
                    amms = []
                    for hh in range(4):
                        h = heads[hh]
                        pr, hb = h // 2, h % 2
                        dps = [fk_evs[pr // 4], fq_evs[pr // 4]]
                        if hh == 0:
                            dps += B.wdeps(pAn)
                            if g == 0:
                                dps += B.wdeps("pp1")
                        amms.append(mm(
                            pA[:, hh * 128:(hh + 1) * 128],
                            fkfm[32 * hb:32 * hb + 32, pr * 128:(pr + 1) * 128],
                            fqfm[32 * hb:32 * hb + 32, pr * 128:(pr + 1) * 128],
                            hh == 0, hh == 3, tp=(32 * hb, 0), deps=dps,
                            name="Amm"))
                        B.r(amms[-1], "fkfm")
                        B.r(amms[-1], "fqfm")
                    B.w(amms[0], pAn)
                    B.W[pAn] = amms[-1]
                    mev = tt(Ab[ga_][:].rearrange("p (h t) -> p h t", h=4)
                             .transpose([0, 2, 1]),
                             pA[:].rearrange("p (h t) -> p h t", h=4)
                             .transpose([0, 2, 1]),
                             masku[:].unsqueeze(2).broadcast_to((128, CH, 4)),
                             ALU.mult,
                             deps=[amms[-1], d_masku] + B.wdeps("Ab%d" % ga_),
                             name="Aev")
                    B.r(mev, pAn)
                    B.w(mev, "Ab%d" % ga_)
                    qt = g // 2
                    pN = pp2[:, (qt % 2) * 512:((qt % 2) + 1) * 512]
                    pNn = "pN%d" % (qt % 2)
                    if "nd" in skips:
                        continue
                    for hh in range(4):
                        h = heads[hh]
                        pr, hb, ha = h // 2, h % 2, h // 2
                        Abl = Ab[ga_][:, hh * 128:(hh + 1) * 128]
                        fql = fqfm[32 * hb:32 * hb + 32, pr * 128:(pr + 1) * 128]
                        for m in range(2):
                            hm = 2 * h + m
                            ncol = (hm % 16) * 32
                            scol = (ha * 2 + m) * 33
                            dps = [mev, vtops[m], iTb]
                            if hh == 0 and m == 0 and g % 2 == 0:
                                dps += B.wdeps(pNn)
                                if g < 2:
                                    dps += B.wdeps("pp2")
                            n1 = mm(pN[:, ncol:ncol + 32], Abl,
                                    vt[m][:, h * 32:(h + 1) * 32],
                                    hh == 0 and m == 0 and g % 2 == 0, False,
                                    deps=dps, name="nd_av")
                            B.r(n1, "vt%d" % m)
                            B.r(n1, "Ab%d" % ga_)
                            ddps = [mev, iTb]
                            if hm == 0:
                                ddps += B.wdeps("ppd")
                            d1_ = mm(ppd[:, 128 + hm:129 + hm], Abl,
                                     icpTb[:, hm:hm + 1], hm == 0, False,
                                     deps=ddps, name="nd_ai")
                            B.r(d1_, "icpTb")
                            B.r(d1_, "Ab%d" % ga_)
                            n2 = mm(pN[:, ncol:ncol + 32], fql,
                                    Sb[32 * hb:32 * hb + 32, scol:scol + 32],
                                    False,
                                    g % 2 == 1 and hh == 3 and m == 1,
                                    deps=[B.W.get("Sb")],
                                    name="nd_qs")
                            d2_ = mm(ppd[:, 128 + hm:129 + hm], fql,
                                     Sb[32 * hb:32 * hb + 32,
                                        scol + 32:scol + 33],
                                     False, hm == 63, deps=[], name="nd_qk")
                            B.r(n2, "Sb")
                            B.r(d2_, "Sb")
                            B.r(n2, "fqfm")
                            B.r(d2_, "fqfm")
                            if hm == 0:
                                B.w(d1_, "ppd")
                            if hh == 0 and m == 0 and g % 2 == 0:
                                B.w(n1, pNn)
                            B.W[pNn] = n2
                            den_last = d2_
                    B.W["ppd"] = den_last
                    if g % 2 == 1:
                        dst = scr[2] if qt < 2 else scr[3]
                        dstn = "scr2" if qt < 2 else "scr3"
                        dps = [B.W[pNn]]
                        if qt % 2 == 0:
                            dps += B.wdeps(dstn)
                        ev = cp_(dst[:, (qt % 2) * 512:((qt % 2) + 1) * 512],
                                 pN[:], deps=dps, name="ndev")
                        B.r(ev, pNn)
                        if qt % 2 == 0:
                            B.w(ev, dstn)
                        else:
                            B.W[dstn] = ev
                        nq_evs[qt] = ev
                nq_done = [nq_evs[q] for q in range(4)] if "nd" not in skips else []
                if "nd" in skips or "y" in skips:
                    yops = [memset(ytm[:], 0.1, deps=B.wdeps("qtm") +
                                   (B.wdeps("pN0") + B.wdeps("pN1")), name="yfake")]
                    B.w(yops[0], "qtm")
                if "nd" not in skips and "y" not in skips:
                    dv = stt(denf[:], icpT[:], EPS, ppd[:, 128:192],
                             ALU.mult, ALU.add,
                             deps=[den_last, iTe] + B.wdeps("denf"), name="den")
                    B.r(dv, "ppd")
                    B.r(dv, "icpT")
                    B.w(dv, "denf")
                    drc = recip(mden[:], denf[:], deps=[dv] + B.wdeps("mden"),
                                name="denr")
                    B.r(drc, "denf")
                    md = tt(mden[:], mixt[:], mden[:], ALU.mult, deps=[drc, mxv],
                            name="mden")
                    B.r(md, "mixt")
                    B.w(md, "mden")
                    mdv = mden[:].rearrange("p (h m) -> p h m", h=32)
                    for half in range(2):
                        srcv = (scr[2] if half == 0 else scr[3])[:] \
                            .rearrange("p (h m c) -> p h m c", h=16, m=2)
                        y_sl = ytm[:, half * 512:(half + 1) * 512] \
                            .rearrange("p (h c) -> p h c", h=16)
                        tmp = scr[4][:, 0:512].rearrange("p (h c) -> p h c", h=16)
                        y1 = tt(y_sl, srcv[:, :, 0, :],
                                mdv[:, half * 16:(half + 1) * 16, 0:1]
                                .broadcast_to((128, 16, 32)), ALU.mult,
                                deps=[md, nq_done[2 * half], nq_done[2 * half + 1]] +
                                (B.wdeps("qtm") if half == 0 else []), name="y1")
                        y2 = tt(tmp, srcv[:, :, 1, :],
                                mdv[:, half * 16:(half + 1) * 16, 1:2]
                                .broadcast_to((128, 16, 32)), ALU.mult,
                                deps=[md, nq_done[2 * half], nq_done[2 * half + 1]] +
                                B.wdeps("scr4"), name="y2")
                        B.w(y2, "scr4")
                        y3 = tt(y_sl, y_sl, tmp, ALU.add, deps=[y1, y2], name="y3")
                        B.r(y3, "scr4")
                        B.r(y3, "scr2" if half == 0 else "scr3")
                        B.r(y3, "mden")
                        yops.append(y3)
                    B.w(yops[0], "qtm")
                    B.W["qtm"] = yops[-1]
                    if c == 0:
                        dbg_dump("y_c0", ytm, yops, src=("qtm",))

            last_sb = None
            for par in range(2):
                smms = []
                kmms = []
                for ha in range(16):
                    h = 2 * ha + par
                    for m in range(2):
                        col = (ha * 2 + m) * 32
                        dps = [ktops[m], vc]
                        if ha == 0 and m == 0:
                            dps += (B.wdeps("pN0") + B.wdeps("pN1") +
                                    B.wdeps("st_pp2") + B.wdeps("pp2"))
                        smms.append(mm(pp2[32 * par:32 * par + 32,
                                           col:col + 32],
                                       kt[m][:, h * 32:(h + 1) * 32],
                                       vtm[:, h * 32:(h + 1) * 32],
                                       (ha in (0, 8) and m == 0),
                                       (ha in (7, 15) and m == 1),
                                       tp=(0, 32 * par) if par else None,
                                       deps=dps, name="st_s"))
                        B.r(smms[-1], "kt%d" % m)
                        B.r(smms[-1], "vtm")
                        kdps = [ktops[m], m_onc]
                        if ha == 0 and m == 0:
                            kdps += B.wdeps("ppd")
                        kmms.append(mm(ppd[32 * par:32 * par + 32,
                                           224 + (ha * 2 + m):
                                           225 + (ha * 2 + m)],
                                       kt[m][:, h * 32:(h + 1) * 32],
                                       onescb[:], (ha == 0 and m == 0),
                                       (ha == 15 and m == 1),
                                       tp=(0, 32 * par) if par else None,
                                       deps=kdps, name="st_k"))
                        B.r(kmms[-1], "kt%d" % m)
                B.w(smms[0], "st_pp2")
                B.W["st_pp2"] = smms[-1]
                B.w(kmms[0], "ppd")
                B.W["ppd"] = kmms[-1]
                rsl = slice(32 * par, 32 * par + 32)
                Sv = Sf[rsl.start - 32 * par if False else 0:32, :] if False \
                    else None
                Sview = Sf[rsl, :].rearrange("p (a m c) -> p a m c", a=16, m=2) \
                    if par else Sf[0:32, :].rearrange("p (a m c) -> p a m c",
                                                      a=16, m=2)
                SS = Sview[:, :, :, 0:32]
                SK = Sview[:, :, :, 32:33]
                pSv = pp2[rsl, 0:1024].rearrange("p (a m c) -> p a m c",
                                                 a=16, m=2)
                sc1v = scr[1][rsl, 0:1024].rearrange("p (a m c) -> p a m c",
                                                     a=16, m=2)
                cpeview = cpendb[rsl, :].rearrange("p (a m) -> p a m", a=16)
                u1 = tt(sc1v, pSv, SS, ALU.add,
                        deps=[smms[-1], B.W.get("Sf")] + B.wdeps("scr1"),
                        name="su_add")
                B.r(u1, "st_pp2")
                B.r(u1, "Sf")
                B.w(u1, "scr1")
                u2 = tt(SS, sc1v,
                        cpeview.unsqueeze(3).broadcast_to((32, 16, 2, 32)),
                        ALU.mult, deps=[u1, cbe] + B.wdeps("Sf"),
                        name="su_mul")
                B.r(u2, "scr1")
                B.r(u2, "cpendb")
                uk1 = tt(kupd[rsl, 0:32].rearrange("p (a m) -> p a m", a=16),
                         ppd[rsl, 224:256].rearrange("p (a m) -> p a m", a=16),
                         SK.squeeze(3), ALU.add,
                         deps=[kmms[-1], B.W.get("Sf")] + B.wdeps("kupd"),
                         name="ku_add")
                B.r(uk1, "ppd")
                B.r(uk1, "Sf")
                B.w(uk1, "kupd")
                uk2 = tt(SK.squeeze(3),
                         kupd[rsl, 0:32].rearrange("p (a m) -> p a m", a=16),
                         cpeview, ALU.mult, deps=[uk1, cbe, u2], name="ku_mul")
                B.r(uk2, "kupd")
                B.r(uk2, "cpendb")
                B.w(uk2, "Sf")
                B.W["Sf"] = uk2
            sb_ = cp_(Sb, Sf, deps=[B.W["Sf"]] + B.wdeps("Sb"), name="Sbc")
            B.r(sb_, "Sf")
            B.w(sb_, "Sb")

            if stage == "A":
                if c == 0:
                    dbg_dump("A_kt0_c0", kt[0], [ktops[0]], src=("kt0",))
                    dbg_dump("A_vtm_c0", vtm, [vc], src=("vtm",))
                    dbg_dump("A_cp_c0", cphm, [cpv], src=("cphm",))
                    dbg_dump("A_S_c0", Sf, [B.W["Sf"]], src=("Sf",))
                    dbg_dump("A_cpe_c0", cpendb, [cbe], src=("cpendb",))
                return

            ytevs = transp_batch(ytm, "qtm", yfm, "yfm", 8,
                                 yops + B.wdeps("yfm"), "yT")
            oplast = None
            for n in range(2):
                for k in range(8):
                    dps = [ytevs[-1], d_wout]
                    if n == 0 and k == 0:
                        dps += B.wdeps("pp0")
                    oplast = mm(pp0[:, n * 512:(n + 1) * 512],
                                yfm[:, k * 128:(k + 1) * 128],
                                woutTb[:, k * 1024 + n * 512:
                                       k * 1024 + (n + 1) * 512],
                                k == 0, k == 7, deps=dps, name="oproj")
                    B.r(oplast, "yfm")
            B.w(oplast, "pp0")
            atno = tt(scr[1], pp0[:, 0:1024], boutb[:],
                      ALU.add, deps=[oplast] + bct2_deps + B.wdeps("scr1"),
                      name="atno")
            B.r(atno, "pp0")
            B.w(atno, "scr1")
            if c == 0:
                dbg_dump("atno_c0", scr[1], [atno], src=("scr1",))
            h1r = ln_apply(xrb, xname, scr[0], "scr0", ca1b, sab, 40, [d_x],
                           extra_deps=bct_deps + [atno])
            if c == 0:
                dbg_dump("h1z_c0", scr[0], [h1r], src=("scr0",))
            zz = tt(scr[1], scr[1], scr[0], ALU.add, deps=[h1r, atno],
                    name="z")
            B.r(zz, "scr0")
            B.w(zz, "scr1")
            if c == 0:
                dbg_dump("z_c0", scr[1], [zz], src=("scr1",))
            a1 = act(scr[4], scr[1], AF.Copy, accum=stat[:, 48:49],
                     deps=[zz] + B.wdeps("scr4") + B.wdeps("stat48"),
                     name="z_c")
            B.w(a1, "scr4")
            a2 = act(scr[4], scr[1], AF.Square, accum=stat[:, 49:50],
                     deps=[zz, a1], name="z_sq")
            B.w(a2, "scr4")
            v1 = ts(stat[:, 50:51], stat[:, 48:49], 1.0 / HD, ALU.mult,
                    deps=[a1], name="z_mu")
            v2 = ts(stat[:, 51:52], stat[:, 50:51], stat[:, 50:51], ALU.mult,
                    deps=[v1], name="z_mu2")
            v3 = ts(stat[:, 52:53], stat[:, 49:50], 1.0 / HD, ALU.mult,
                    deps=[a2], name="z_ex2")
            v4 = ts(stat[:, 53:54], stat[:, 52:53], stat[:, 51:52],
                    ALU.subtract, LN_EPS, ALU.add, deps=[v2, v3], name="z_var")
            s3 = act(stat[:, 54:55], stat[:, 53:54], AF.Ln, deps=[v4],
                     name="z_lv")
            s4 = act(stat[:, 54:55], stat[:, 54:55], AF.Exp, scale=-0.5,
                     deps=[s3], name="z_rs")
            v5 = ts(stat[:, 55:56], stat[:, 50:51], -1.0, ALU.mult, deps=[v1],
                    name="z_nm")
            zc = ts(scr[1], scr[1], stat[:, 55:56], ALU.add, stat[:, 54:55],
                    ALU.mult, deps=[v5, s4], name="z_n")
            B.w(zc, "scr1")
            B.w(zc, "stat48")
            if c == 0:
                dbg_dump("zn_c0", scr[1], [zc], src=("scr1",))
                dbg_dump("zstat_c0", stat[:, 48:56], [zc], src=("stat48",))
            af1 = tt(scr[1], scr[1], lngb[:], ALU.mult, deps=[zc] + bct2_deps,
                     name="af1")
            af2 = tt(scr[1], scr[1], lnbb[:], ALU.add, deps=[af1], name="af2")
            if c == 0:
                dbg_dump("af_c0", scr[1], [af2], src=("scr1",))
                dbg_dump("gab_t", gab[:], [af2] + bct2_deps)
                dbg_dump("lngb_t", lngb[:], [af2] + bct2_deps)
                dbg_dump("lnbb_t", lnbb[:], [af2] + bct2_deps)
            xg = tt(scr[1], scr[1], gab[:], ALU.mult, deps=[af2], name="xga")
            if c == 0:
                dbg_dump("xga_c0", scr[1], [xg], src=("scr1",))
            xm = tt(scr[2], scr[1], xrb, ALU.add,
                    deps=[xg, d_x] + B.wdeps("scr2"), name="xm")
            B.r(xm, xname)
            B.r(xm, "scr1")
            B.w(xm, "scr2")
            if c == 0:
                dbg_dump("xmid_c0", scr[2], [xm], src=("scr2",))
            d_xm = dmaQ(xmid_d[c * CH:(c + 1) * CH, :], scr[2], deps=[xm],
                        name="xmout")
            B.r(d_xm, "scr2")
            h2c = ln_apply(scr[2], "scr2", scr[0], "scr0", cb1b, sbb, 32, [xm],
                           extra_deps=bct2_deps)
            if c == 0:
                dbg_dump("h2_c0", scr[0], [h2c], src=("scr0",))
            h2tgts = [h2T[:, j * tpc + c * CH:j * tpc + (c + 1) * CH]
                      for j in range(8)]
            h2evs = transp_batch(scr[0], "scr0", h2T, "h2T_c%d" % c, 8,
                                 [h2c], "h2", dst_targets=h2tgts)
            glast = None
            for k in range(8):
                dps = [h2evs[k], d_gateTb]
                if k == 0:
                    dps += B.wdeps("ppd")
                glast = mm(ppd[0:8, 0:128], gateTb[:, k * 8:(k + 1) * 8],
                           h2T[:, k * tpc + c * CH:k * tpc + (c + 1) * CH],
                           k == 0, k == 7, deps=dps, name="gmm")
            B.w(glast, "ppd")
            gev = ts(lgT[0:8, 0:128], ppd[0:8, 0:128], gatebc[:], ALU.add,
                     deps=[glast, d_gatebc] + B.wdeps("lgT"), name="gev")
            B.r(gev, "ppd")
            B.w(gev, "lgT")
            gtr = tr(ppd[:, 128:136], lgT[0:8, 0:128],
                     deps=[gev] + B.wdeps("ppd"), name="gtr")
            B.w(gtr, "ppd")
            gte = cp_(lgtm[:], ppd[:, 128:136], deps=[gtr] + B.wdeps("lgtm"),
                      name="gte")
            B.r(gte, "ppd")
            B.w(gte, "lgtm")
            if c == 0:
                dbg_dump("logit_c0", lgtm, [gte], src=("lgtm",))
            t_m1 = vmax(stat[:, 56:57], lgtm[:],
                        deps=[gte] + B.wdeps("stat56"), name="g_m1")
            t_ge = ts(g8a[:], lgtm[:], stat[:, 56:57], ALU.is_ge,
                      deps=[t_m1] + B.wdeps("g8a"), name="g_ge")
            B.w(t_ge, "g8a")
            t_mk = stt(g8b[:], g8a[:], -1e30, lgtm[:], ALU.mult, ALU.add,
                       deps=[t_ge] + B.wdeps("g8b"), name="g_mk")
            B.w(t_mk, "g8b")
            t_m2 = vmax(stat[:, 57:58], g8b[:], deps=[t_mk], name="g_m2")
            t_sel = ts(g8a[:], lgtm[:], stat[:, 57:58], ALU.is_ge,
                       deps=[t_m2, t_ge], name="g_sel")
            B.w(t_sel, "g8a")
            t_nm = ts(stat[:, 58:59], stat[:, 56:57], -1.0, ALU.mult,
                      deps=[t_m1], name="g_nm")
            t_e = act(g8b[:], lgtm[:], AF.Exp, bias=stat[:, 58:59],
                      deps=[t_nm, t_mk, t_m2], name="g_e")
            B.w(t_e, "g8b")
            t_wr = tt(g8b[:], g8b[:], g8a[:], ALU.mult, deps=[t_e, t_sel],
                      name="g_wr")
            B.r(t_wr, "g8a")
            B.w(t_wr, "g8b")
            t_ss = vsum(stat[:, 59:60], g8b[:], deps=[t_wr], name="g_ss")
            t_rs = recip(stat[:, 60:61], stat[:, 59:60], deps=[t_ss],
                         name="g_rs")
            t_w = ts(wgate[:, c * 8:(c + 1) * 8], g8b[:], stat[:, 60:61],
                     ALU.mult, deps=[t_rs] + B.wdeps("stat56"), name="g_w")
            B.r(t_w, "g8b")
            B.w(t_w, "stat56")
            if c == 0:
                dbg_dump("wg_c0", wgate[:, 0:8], [t_w])
            wtr = tr(ppd[0:8, 136:264], wgate[:, c * 8:(c + 1) * 8],
                     deps=[t_w] + B.wdeps("ppd") + B.wdeps("ppd") +
                     B.wdeps("ppd"), name="wtr")
            B.w(wtr, "ppd")
            wte = cp_(wTg[0:8, c * CH:(c + 1) * CH], ppd[0:8, 136:264],
                      deps=[wtr], name="wte")
            B.r(wte, "ppd")
            B.r(wte, "ppd")
            B.r(wte, "ppd")
            B.w(wte, "wTg_c%d" % c)

        if stop_after == "phase0":
            L.op("Q", lambda e: e.nop(), deps=L.lasts(), name="final_wait")
            L.assign()
            with nc.Block() as block:
                block.sync(L.body("Q", esems, dsems))
                block.gpsimd(L.body("G", esems, dsems))
                block.tensor(L.body("T", esems, dsems))
                block.vector(L.body("V", esems, dsems))
                block.scalar(L.body("S", esems, dsems))
            return nc
        for c in range(NCH):
            emit_chunk(c, "A")
            if stop_after == "chunkA0" and c == 0:
                break
        fl = ts(Sf, Sf, flagc[:], ALU.mult,
                deps=[d_flag] + B.wdeps("Sf"), name="flagS")
        B.r(fl, "Sf")
        B.w(fl, "Sf")
        sb0 = cp_(Sb, Sf, deps=[fl] + B.wdeps("Sb"), name="Sb0")
        B.r(sb0, "Sf")
        B.w(sb0, "Sb")
        dbg_dump("S_in", Sf, [fl])

        if stop_after in ("stageA", "chunkA0"):
            L.op("Q", lambda e: e.nop(), deps=L.lasts(), name="final_wait")
            L.assign()
            with nc.Block() as block:
                block.sync(L.body("Q", esems, dsems))
                block.gpsimd(L.body("G", esems, dsems))
                block.tensor(L.body("T", esems, dsems))
                block.vector(L.body("V", esems, dsems))
                block.scalar(L.body("S", esems, dsems))
            return nc
        for c in range(NCH):
            emit_chunk(c, "B")
            if stop_after == "chunkB0" and c == 0:
                break

        if stop_after in ("stageB", "chunkB0"):
            L.op("Q", lambda e: e.nop(), deps=L.lasts(), name="final_wait")
            L.assign()
            with nc.Block() as block:
                block.sync(L.body("Q", esems, dsems))
                block.gpsimd(L.body("G", esems, dsems))
                block.tensor(L.body("T", esems, dsems))
                block.vector(L.body("V", esems, dsems))
                block.scalar(L.body("S", esems, dsems))
            return nc
        stB.close()
        w0r = [tk("w0r%d" % i, 128, 8 * 256, BF16, stk=stack) for i in range(3)]
        w1r = [tk("w1r%d" % i, 128, 8 * 256, BF16, stk=stack) for i in range(3)]
        w2b = tk("w2b", 128, 16 * HD, BF16, stk=stack)
        actb = tk("actb", 128, 16 * THALF, BF16, stk=stack)
        outacc = tk("outacc", 128, 8 * THALF, F32, stk=stack)
        wbc = [tk("wbc%d" % i, 128, THALF, F32, stk=stack) for i in range(2)]
        silscr = [tk("silscr%d" % i, 128, THALF, F32, stk=stack)
                  for i in range(2)]
        xmr = [tk("xmr%d" % i, 128, HD, F32, stk=stack) for i in range(2)]
        otile = [tk("otile%d" % i, 128, HD, F32, stk=stack) for i in range(2)]
        wrow = [tk("wrow%d" % i, 1, THALF, BF16, stk=stack) for i in range(2)]

        bars = L.lasts()
        first_deps = list(bars)
        wTg_deps = [B.W["wTg_c%d" % c] for c in range(NCH)]
        h2T_deps = [B.W["h2T_c%d" % c] for c in range(NCH)]
        SEGS = segs(THALF)
        all_douts = []

        for hf in range(2):
            oz = memset(outacc, 0.0,
                        deps=B.wdeps(*["oacc%d" % m_ for m_ in range(8)]) +
                        first_deps, name="oz")
            B.w(oz, *["oacc%d" % m_ for m_ in range(8)])
            for e in range(FE):
                ep = e % 2
                dw2 = []
                for q4 in range(4):
                    dw2.append(dmaG(w2b[:, q4 * 4096:(q4 + 1) * 4096],
                                    w2_d[e][:, q4 * 4096:(q4 + 1) * 4096],
                                    deps=B.wdeps("w2b") + first_deps,
                                    name="w2l"))
                B.w(dw2[-1], "w2b")
                dwr = dmaQ(wrow[ep][0:1, 0:THALF],
                           wTg[e:e + 1, hf * THALF:(hf + 1) * THALF],
                           deps=B.wdeps("wrow%d" % ep) + wTg_deps + first_deps,
                           name="wrl")
                B.w(dwr, "wrow%d" % ep)
                wbm_last = None
                for n, (p0_, wn) in enumerate(SEGS):
                    wbm_last = mm(pp2[:, p0_:p0_ + wn], onesrb[0:1, 0:128],
                                  wrow[ep][0:1, p0_:p0_ + wn], True, True,
                                  deps=[dwr, m_onb] +
                                  ((B.wdeps("pp2f") + first_deps)
                                   if n == 0 else []),
                                  name="wbm")
                    if n == 0:
                        B.w(wbm_last, "pp2f")
                B.W["pp2f"] = wbm_last
                B.r(wbm_last, "wrow%d" % ep)
                wbe = cp_(wbc[ep][:, 0:THALF], pp2[:, 0:THALF],
                          deps=[wbm_last] + B.wdeps("wbc%d" % ep), name="wbe")
                B.r(wbe, "pp2f")
                B.w(wbe, "wbc%d" % ep)

                def load_slot(gg, e=e):
                    sl = gg % 3
                    d0 = dmaG(w0r[sl][:].rearrange("p (a n) -> p a n", a=8),
                              w0_d[e].rearrange("p (a h) -> p a h", a=8)
                              [:, :, gg * 256:(gg + 1) * 256],
                              deps=B.wdeps("w0r%d" % sl) + first_deps,
                              name="w0l")
                    B.w(d0, "w0r%d" % sl)
                    d1_ = dmaG(w1r[sl][:].rearrange("p (a n) -> p a n", a=8),
                               w1_d[e].rearrange("p (a h) -> p a h", a=8)
                               [:, :, gg * 256:(gg + 1) * 256],
                               deps=B.wdeps("w1r%d" % sl) + first_deps,
                               name="w1l")
                    B.w(d1_, "w1r%d" % sl)
                    return d0, d1_

                slot_dma = {gg: load_slot(gg) for gg in range(3)}
                for j in range(16):
                    gg, jj = j // 2, j % 2
                    sl = gg % 3
                    jp = j % 2
                    d0, d1_ = slot_dma[gg]
                    ph0 = pp0 if jp == 0 else pp1
                    ph0n = "pf0_%d" % jp
                    h0last = None
                    for n, (p0_, wn) in enumerate(SEGS):
                        for k in range(8):
                            dps = [d0] + h2T_deps
                            if n == 0 and k == 0:
                                dps += B.wdeps(ph0n)
                            h0last = mm(ph0[:, p0_:p0_ + wn],
                                        w0r[sl][:, k * 256 + jj * 128:
                                                k * 256 + jj * 128 + 128],
                                        h2T[:, k * tpc + hf * THALF + p0_:
                                            k * tpc + hf * THALF + p0_ + wn],
                                        k == 0, k == 7, deps=dps, name="h0mm")
                            B.r(h0last, "w0r%d" % sl)
                    B.w(h0last, ph0n)
                    B.W[ph0n] = h0last
                    if jp == 0:
                        h1tgts = [(pp2, p0_, wn) for (p0_, wn) in SEGS]
                        h1n = "pp2f"
                    else:
                        tg = [ppd, ppt]
                        h1tgts = [(tg[n], 0, wn)
                                  for n, (p0_, wn) in enumerate(SEGS)]
                        h1n = "pdtf"
                    h1last = None
                    for n, (p0_, wn) in enumerate(SEGS):
                        tgt, tb, _ = h1tgts[n]
                        toff = tb if jp else p0_
                        for k in range(8):
                            dps = [d1_]
                            if n == 0 and k == 0:
                                dps += B.wdeps(h1n)
                            h1last = mm(tgt[:, toff:toff + wn],
                                        w1r[sl][:, k * 256 + jj * 128:
                                                k * 256 + jj * 128 + 128],
                                        h2T[:, k * tpc + hf * THALF + p0_:
                                            k * tpc + hf * THALF + p0_ + wn],
                                        k == 0, k == 7, deps=dps, name="h1mm")
                            B.r(h1last, "w1r%d" % sl)
                    B.w(h1last, h1n)
                    B.W[h1n] = h1last
                    sil_last = None
                    for n, (p0_, wn) in enumerate(SEGS):
                        tgt, tb, _ = h1tgts[n]
                        toff = tb if jp else p0_
                        dps = [h1last, d_b1t]
                        if n == 0:
                            dps += B.wdeps("sil%d" % jp)
                        sil_last = act(silscr[jp][:, p0_:p0_ + wn],
                                       tgt[:, toff:toff + wn], AF.Silu,
                                       bias=b1t[:, e * 16 + j:e * 16 + j + 1],
                                       deps=dps, name="sil")
                        B.r(sil_last, h1n)
                        if n == 0:
                            B.w(sil_last, "sil%d" % jp)
                    B.W["sil%d" % jp] = sil_last
                    ab_ = stt(actb[:, j * THALF:(j + 1) * THALF],
                              ph0[:, 0:THALF],
                              b0t[:, e * 16 + j:e * 16 + j + 1],
                              silscr[jp][:, 0:THALF], ALU.add, ALU.mult,
                              deps=[h0last, sil_last, d_b0t] +
                              B.wdeps("actb%d" % j), name="actb")
                    B.r(ab_, ph0n)
                    B.r(ab_, "sil%d" % jp)
                    B.w(ab_, "actb%d" % j)
                    if jj == 1 and gg + 3 <= 7:
                        slot_dma[gg + 3] = load_slot(gg + 3)

                for m_ in range(8):
                    pO = pp0 if m_ % 2 == 0 else pp1
                    pOn = "pf0_%d" % (m_ % 2)
                    olast = None
                    for j in range(16):
                        for n, (p0_, wn) in enumerate(SEGS):
                            dps = [dw2[-1], B.W["actb%d" % j]]
                            if j == 0 and n == 0:
                                dps += B.wdeps(pOn)
                            olast = mm(pO[:, p0_:p0_ + wn],
                                       w2b[:, j * 1024 + m_ * 128:
                                           j * 1024 + m_ * 128 + 128],
                                       actb[:, j * THALF + p0_:
                                            j * THALF + p0_ + wn],
                                       j == 0, j == 15, deps=dps, name="o_mm")
                            B.r(olast, "actb%d" % j)
                            B.r(olast, "w2b")
                    B.w(olast, pOn)
                    B.W[pOn] = olast
                    tmp = silscr[m_ % 2]
                    t1 = tt(tmp[:, 0:THALF], pO[:, 0:THALF],
                            wbc[ep][:, 0:THALF], ALU.mult,
                            deps=[olast, wbe] + B.wdeps("sil%d" % (m_ % 2)),
                            name="of1")
                    B.r(t1, pOn)
                    B.r(t1, "wbc%d" % ep)
                    B.w(t1, "sil%d" % (m_ % 2))
                    t2 = tt(outacc[:, m_ * THALF:(m_ + 1) * THALF],
                            outacc[:, m_ * THALF:(m_ + 1) * THALF],
                            tmp[:, 0:THALF], ALU.add,
                            deps=[t1] + B.wdeps("oacc%d" % m_), name="of2")
                    B.r(t2, "sil%d" % (m_ % 2))
                    B.w(t2, "oacc%d" % m_)

            for tc in range(NTH):
                par = tc % 2
                dxm = dmaQ(xmr[par], xmid_d[hf * THALF + tc * CH:
                                            hf * THALF + (tc + 1) * CH, :],
                           deps=B.wdeps("xmr%d" % par) + first_deps,
                           name="xmin")
                B.w(dxm, "xmr%d" % par)
                trs = []
                for m_ in range(8):
                    dps = [B.W.get("oacc%d" % m_)]
                    if m_ == 0:
                        dps += B.wdeps("pp2f")
                    trs.append(tr(pp2[:, m_ * 128:(m_ + 1) * 128],
                                  outacc[:, m_ * THALF + tc * CH:
                                         m_ * THALF + (tc + 1) * CH],
                                  deps=dps, name="fT"))
                    B.r(trs[-1], "oacc%d" % m_)
                B.w(trs[0], "pp2f")
                B.W["pp2f"] = trs[-1]
                bmms = []
                for n in range(2):
                    dps = [d_b2b] + wTg_deps
                    if n == 0:
                        dps += B.wdeps("pf0_0")
                    bmms.append(mm(pp0[:, n * 512:(n + 1) * 512],
                                   wTg[0:8, hf * THALF + tc * CH:
                                       hf * THALF + (tc + 1) * CH],
                                   b2b[0:8, n * 512:(n + 1) * 512],
                                   True, True, deps=dps, name="b2mm"))
                B.w(bmms[0], "pf0_0")
                B.W["pf0_0"] = bmms[-1]
                u1 = tt(otile[par], pp2[:, 0:1024], gbb[:], ALU.mult,
                        deps=[trs[-1]] + gbb_deps + B.wdeps("ot%d" % par),
                        name="u1")
                B.r(u1, "pp2f")
                B.w(u1, "ot%d" % par)
                u2 = tt(otile[par], otile[par], xmr[par], ALU.add,
                        deps=[u1, dxm], name="u2")
                B.r(u2, "xmr%d" % par)
                u3 = tt(xmr[par], pp0[:, 0:1024], gbb[:], ALU.mult,
                        deps=[bmms[-1], u2], name="u3")
                B.r(u3, "pf0_0")
                B.w(u3, "xmr%d" % par)
                u4 = tt(otile[par], otile[par], xmr[par], ALU.add,
                        deps=[u3], name="u4")
                B.r(u4, "xmr%d" % par)
                B.w(u4, "ot%d" % par)
                dout = dmaQ(out_d[hf * THALF + tc * CH:
                                  hf * THALF + (tc + 1) * CH, :],
                            otile[par], deps=[u4], name="dout")
                B.r(dout, "ot%d" % par)
                all_douts.append(dout)

        final_deps = all_douts + dbg_final
        L.op("Q", lambda e: e.nop(), deps=final_deps, name="final_wait")

        L.assign()
        with nc.Block() as block:
            block.sync(L.body("Q", esems, dsems))
            block.gpsimd(L.body("G", esems, dsems))
            block.tensor(L.body("T", esems, dsems))
            block.vector(L.body("V", esems, dsems))
            block.scalar(L.body("S", esems, dsems))
    return nc


def _blk(wT, kchunks, ncols):
    return np.ascontiguousarray(
        wT.reshape(kchunks, 128, ncols).transpose(1, 0, 2).reshape(128, -1))


def prep_inputs(inputs, tpc=TPC):
    f32 = lambda a: np.ascontiguousarray(np.asarray(a, np.float32))
    NCH = tpc // CH
    x = f32(inputs["L0_attn_x"])
    nb = x.shape[0]
    Wq, Wk, Wv = (f32(inputs["L0_attn_W" + s]) for s in "qkv")
    Wout = f32(inputs["L0_attn_Wout"])
    Wbeta, Wmix = f32(inputs["L0_attn_Wbeta"]), f32(inputs["L0_attn_Wmix"])
    shared = dict(
        wqT=_blk(Wq.T, 8, HD), wkT=_blk(Wk.T, 8, HD), wvT=_blk(Wv.T, 8, HD),
        woutT=_blk(Wout.T, 8, HD),
        wbmT=_blk(np.concatenate([Wbeta.T, Wmix.T], axis=1), 8, 128),
        bbm=np.concatenate([f32(inputs["L0_attn_bbeta"]),
                            f32(inputs["L0_attn_bmix"])])[None, :],
        bout_h=f32(inputs["L0_attn_bout"])[None, :],
        lng=f32(inputs["L0_attn_ln_g"])[None, :],
        lnb=f32(inputs["L0_attn_ln_b"])[None, :],
        gateT=_blk(f32(inputs["L0_ffn_gate_w"]).T, 8, 8),
        gatebc_h=f32(inputs["L0_ffn_gate_b"]).reshape(8, 1),
        wmod=f32(inputs["L0_wmod"]),
        w0r=np.stack([_blk(f32(inputs["L0_ffn_w0"])[e].T, 8, 2048)
                      for e in range(FE)]),
        w1r=np.stack([_blk(f32(inputs["L0_ffn_w1"])[e].T, 8, 2048)
                      for e in range(FE)]),
        w2r=np.stack([_blk(f32(inputs["L0_ffn_w2"])[e].T, 16, HD)
                      for e in range(FE)]),
        b0t_h=f32(inputs["L0_ffn_b0"]).reshape(8, 16, 128).transpose(2, 0, 1)
        .reshape(128, 128).copy(),
        b1t_h=f32(inputs["L0_ffn_b1"]).reshape(8, 16, 128).transpose(2, 0, 1)
        .reshape(128, 128).copy(),
        b2m=f32(inputs["L0_ffn_b2"]),
        identc_h=np.eye(128, dtype=np.float32),
        masku_h=np.triu(np.ones((128, 128), np.float32)),
    )
    inv32 = (np.float32(1.0) /
             np.float32(10000.0) ** (np.arange(16, dtype=np.float32)
                                     / np.float32(16)))

    def tabs(pos):
        ang = pos.astype(np.float32)[:, None] * inv32[None, :]
        return np.cos(ang), np.sin(ang)

    crows = f32(inputs["L0_c"])
    in_maps = []
    for core in range(2 * nb):
        b, hf = core // 2, core % 2
        own = np.arange(hf * tpc, (hf + 1) * tpc, dtype=np.float32)
        prt = np.arange(0, tpc, dtype=np.float32)
        ct, st = np.empty((2, 128, 2 * NCH * 16), np.float32)
        for s, pos in enumerate(list(own.reshape(NCH, CH)) +
                                list(prt.reshape(NCH, CH))):
            ct[:, s * 16:(s + 1) * 16], st[:, s * 16:(s + 1) * 16] = tabs(pos)
        im = dict(shared)
        im["x"] = np.ascontiguousarray(x[b, hf * tpc:(hf + 1) * tpc])
        im["xp"] = np.ascontiguousarray(x[b, 0:tpc])
        im["crowT_h"] = crows[b].reshape(8, 128).T.copy()
        im["flagc_h"] = np.full((64, 1), float(hf), np.float32)
        im["costab_h"] = np.ascontiguousarray(ct)
        im["sintab_h"] = np.ascontiguousarray(st)
        in_maps.append(im)
    return in_maps


def kernel(**inputs):
    in_maps = prep_inputs(inputs)
    nc = build()
    res = run_bass_kernel_spmd(nc, in_maps, core_ids=list(range(8)))
    out = np.empty((4, 2 * TPC, HD), np.float32)
    for core in range(8):
        b, hf = core // 2, core % 2
        out[b, hf * TPC:(hf + 1) * TPC] = res.results[core]["out"]
    return out
